# version 18
# baseline (speedup 1.0000x reference)
import math
import numpy as np
import ml_dtypes
from contextlib import ExitStack

import concourse.bass as bass
import concourse.mybir as mybir
from concourse.bass_utils import run_bass_kernel_spmd

F32 = mybir.dt.float32
BF16 = mybir.dt.bfloat16
AF = mybir.ActivationFunctionType
ALU = mybir.AluOpType
AX = mybir.AxisListType
BF = ml_dtypes.bfloat16

N_CORES = 8
D = 2048
T = 2048
NCH = 16
NH = 16
NE = 8
F1 = 8192
EPS = 1e-5
SM_SCALE = 1.0 / math.sqrt(128.0)
NEG = -30000.0

ENGINES = ("sync", "tensor", "vector", "scalar", "gpsimd")
DEBUG = {}
SUBLAYERS = (0, 1, 2, 3, 4, 5)


def _panels(W):
    K, M = W.shape
    return np.ascontiguousarray(
        W.reshape(K // 128, 128, M // 128, 128).transpose(2, 1, 0, 3).reshape(M // 128, 128, K))


def prep_inputs(inp):
    h = {}
    x = np.asarray(inp["L0_attn_x"], np.float32)
    h["x"] = [np.ascontiguousarray(x[c].T).reshape(NCH, 128, T) for c in range(N_CORES)]
    h["l0_wqkv"] = _panels(np.asarray(inp["L0_attn_w_qkv"], np.float32))
    h["l0_wo"] = _panels(np.asarray(inp["L0_attn_w_o"], np.float32))
    h["wr"] = np.ascontiguousarray(
        np.asarray(inp["L0_ffn_Wr"], np.float32).reshape(NCH, 128, NE).transpose(1, 0, 2)).reshape(128, NCH * NE)
    for nm, key in (("wg", "L0_ffn_Wg"), ("wu", "L0_ffn_Wu"), ("wd", "L0_ffn_Wd")):
        h[nm] = np.ascontiguousarray(np.stack(
            [_panels(np.asarray(inp[key], np.float32)[e]) for e in range(NE)]))
    h["wd"] = np.ascontiguousarray(
        h["wd"].reshape(NE, 16, 128, 2, 1024).transpose(0, 3, 1, 2, 4))
    h["l1_wqkv"] = _panels(np.ascontiguousarray(np.asarray(inp["L1_attn_w_qkv"], np.float32).T))
    h["l1_wout"] = _panels(np.ascontiguousarray(np.asarray(inp["L1_attn_w_out"], np.float32).T))
    h["l1_wg"] = _panels(np.ascontiguousarray(np.asarray(inp["L1_ffn_w_gate"], np.float32).T))
    h["l1_wu"] = _panels(np.ascontiguousarray(np.asarray(inp["L1_ffn_w_up"], np.float32).T))
    l1wd = _panels(np.ascontiguousarray(np.asarray(inp["L1_ffn_w_down"], np.float32).T))
    h["l1_wd"] = np.ascontiguousarray(l1wd.reshape(16, 128, 8, 1024).transpose(2, 0, 1, 3))
    h["bqk"] = np.ascontiguousarray(np.asarray(inp["L1_attn_b_qkv"], np.float32).reshape(48, 128).T)
    h["bout"] = np.ascontiguousarray(np.asarray(inp["L1_attn_b_out"], np.float32).reshape(16, 128).T)
    h["ident"] = np.eye(128, dtype=np.float32).astype(BF)
    h["tri"] = np.tril(np.ones((128, 128), np.float32)).astype(BF)
    return h



class Em:
    def __init__(self):
        self.ops = {e: [] for e in ENGINES}
        self.cnt = {}
        self.hw = {}
        self.sems = {}
        self.dv_toks = []

    def _wait(self, eng, tok):
        if tok is None:
            return
        sem, val = tok
        if val <= 0 or val <= self.hw.get((eng, sem), 0):
            return
        self.hw[(eng, sem)] = val

        def cl(E, s=sem, v=val):
            E.wait_ge(self.sems[s], v)
        self.ops[eng].append(cl)

    def emit(self, eng, fn, waits=(), inc=None, n=1):
        for t in waits:
            self._wait(eng, t)
        if inc is None:
            self.ops[eng].append(fn)
            return None
        self.cnt[inc] = self.cnt.get(inc, 0) + n
        tgt = self.cnt[inc]

        def cl(E, fn=fn, s=inc, k=n):
            fn(E).then_inc(self.sems[s], k)
        self.ops[eng].append(cl)
        return (inc, tgt)

    def dma(self, out, in_, sem, waits=(), queue="sync"):
        return self.emit(queue, lambda E, o=out, i=in_: E.dma_start(out=o, in_=i),
                         waits=waits, inc=sem, n=16)


class Buf:
    __slots__ = ("ap", "ready", "users")

    def __init__(self, ap):
        self.ap = ap
        self.ready = None
        self.users = []

    def barrier(self):
        toks = {}
        for t in self.users + ([self.ready] if self.ready else []):
            if t and t[1] > toks.get(t[0], -1):
                toks[t[0]] = t[1]
        return [(s, v) for s, v in toks.items()]

    def produced(self, tok):
        self.ready = tok
        self.users = []

    def used(self, tok):
        if tok is not None:
            self.users.append(tok)



def build_program(sublayers, first, last, tap=None):
    nc = bass.Bass("TRN2", target_bir_lowering=False)
    em = Em()
    need = set(sublayers)

    x_in = nc.dram_tensor("x", [NCH, 128, T], F32 if first else BF16, kind="ExternalInput").ap()
    y_out = nc.dram_tensor("y", [NCH, 128, T], F32 if last else BF16, kind="ExternalOutput").ap()

    W = {}
    if need & {0, 2}:
        W["wr"] = nc.dram_tensor("wr", [128, NCH * NE], F32, kind="ExternalInput").ap()
        for nm in ("wg", "wu"):
            W[nm] = nc.dram_tensor(nm, [NE, 16, 128, D], F32, kind="ExternalInput").ap()
        W["wd"] = nc.dram_tensor("wd", [NE, 2, 16, 128, 1024], F32, kind="ExternalInput").ap()
    if 1 in need:
        W["l0_wqkv"] = nc.dram_tensor("l0_wqkv", [48, 128, D], F32, kind="ExternalInput").ap()
        W["l0_wo"] = nc.dram_tensor("l0_wo", [16, 128, D], F32, kind="ExternalInput").ap()
    if 4 in need:
        W["l1_wqkv"] = nc.dram_tensor("l1_wqkv", [48, 128, D], F32, kind="ExternalInput").ap()
        W["l1_wout"] = nc.dram_tensor("l1_wout", [16, 128, D], F32, kind="ExternalInput").ap()
        W["bqk"] = nc.dram_tensor("bqk", [128, 48], F32, kind="ExternalInput").ap()
        W["bout"] = nc.dram_tensor("bout", [128, 16], F32, kind="ExternalInput").ap()
        W["tri"] = nc.dram_tensor("tri", [128, 128], BF16, kind="ExternalInput").ap()
    if need & {3, 5}:
        W["l1_wg"] = nc.dram_tensor("l1_wg", [64, 128, D], F32, kind="ExternalInput").ap()
        W["l1_wu"] = nc.dram_tensor("l1_wu", [64, 128, D], F32, kind="ExternalInput").ap()
        W["l1_wd"] = nc.dram_tensor("l1_wd", [8, 16, 128, 1024], F32, kind="ExternalInput").ap()
    W["ident"] = nc.dram_tensor("ident", [128, 128], BF16, kind="ExternalInput").ap()
    if {0, 2} <= need:
        for nm in ("wg", "wu"):
            W[nm + "_c"] = nc.dram_tensor(nm + "_c", [NE, 16, 128, D], BF16, kind="Internal").ap()
        W["wd_c"] = nc.dram_tensor("wd_c", [NE, 2, 16, 128, 1024], BF16, kind="Internal").ap()
    if {3, 5} <= need:
        W["l1_wg_c"] = nc.dram_tensor("l1_wg_c", [64, 128, D], BF16, kind="Internal").ap()
        W["l1_wu_c"] = nc.dram_tensor("l1_wu_c", [64, 128, D], BF16, kind="Internal").ap()
        W["l1_wd_c"] = nc.dram_tensor("l1_wd_c", [8, 16, 128, 1024], BF16, kind="Internal").ap()
    if need & {1, 4}:
        W["vals"] = nc.dram_tensor("vals", [NCH, 128, T], BF16, kind="Internal").ap()

    ctx = ExitStack()
    with ctx:
        def S(name, shape, dt):
            return Buf(ctx.enter_context(nc.sbuf_tensor(name, shape, dt)).ap())

        resid = S("resid", [128, NCH * T], BF16)
        xnorm = S("xnorm", [128, NCH * T], BF16)
        bigT = ctx.enter_context(nc.sbuf_tensor("big", [128, 16384], BF16)).ap()
        wland = [S(f"wland{i}", [128, 2048], F32) for i in range(2)]
        wbf = [S(f"wbf{i}", [128, 2048], BF16) for i in range(2)]
        scrT = ctx.enter_context(nc.sbuf_tensor("scr", [128, 2048], BF16)).ap()
        auxT = ctx.enter_context(nc.sbuf_tensor("aux", [128, 4096], BF16)).ap()
        rbc = Buf(auxT[:, 0:T])
        wbc = Buf(auxT[:, T:2 * T]) if need & {0, 2} else None
        aux_f32 = auxT.bitcast(F32)
        rowA = S("rowA", [1, 512], F32)
        rowB = S("rowB", [1, 512], F32)
        rowT = S("rowT", [1, 512], F32)
        rowC = S("rowC", [1, 512], BF16)
        rowD = S("rowD", [1, 512], BF16)
        wrow = S("wrow", [1, 128], BF16)
        ones1 = S("ones1", [1, 128], BF16)
        ones128 = S("ones128", [128, 128], BF16)
        stat = S("stat", [128, 12], F32)
        nrm = S("nrm", [128, 12], F32)
        bcol = S("bcol", [128, 2], F32)
        rtr = S("rtr", [128, 128], F32) if need & {0, 2} else None
        rtw = S("rtw", [128, 16], F32) if need & {0, 2} else None
        wtm = S("wtm", [128, 128], BF16) if need & {0, 2} else None
        wrB = S("wrB", [128, NCH * NE], BF16) if need & {0, 2} else None
        identS = S("identS", [128, 128], BF16)
        triS = S("triS", [128, 128], BF16) if 4 in need else None
        bqkS = S("bqkS", [128, 48], F32) if 4 in need else None
        boutS = S("boutS", [128, 16], F32) if 4 in need else None
        pTs = ([S(f"pTs{i}", [128, 128], BF16) for i in range(2)]
               + [Buf(scrT[:, 1024:1152]), Buf(scrT[:, 1152:1280])]) if 4 in need else None

        pb = [Buf(ctx.enter_context(nc.psum_tensor(f"pb{i}", [128, 512], F32)).ap())
              for i in range(6)]
        pTa = Buf(ctx.enter_context(nc.psum_tensor("pTa", [128, 1024], BF16)).ap())
        pTb = Buf(ctx.enter_context(nc.psum_tensor("pTb", [128, 1024], BF16)).ap())
        pTaF = Buf(pTa.ap.bitcast(F32))
        pTbF = Buf(pTb.ap.bitcast(F32))

        for s in ("dL", "dV", "dVL", "dO", "dC", "pe", "act", "dve", "pool"):
            em.sems[s] = ctx.enter_context(nc.semaphore(s))

        def rch(buf, c):
            return buf.ap[:, c * T:(c + 1) * T]

        def rsg(buf, c, tg, w=512):
            return buf.ap[:, c * T + tg * w: c * T + (tg + 1) * w]

        ARENA = []
        _anotes = []
        RESID_CTOKS = [[] for _ in range(NCH)]
        _rchunk = [[] for _ in range(NCH)]

        def anote(tok):
            if tok is not None:
                _anotes.append(tok)
            return tok

        def arena_flush():
            col = {}
            for s_, v_ in _anotes:
                col[s_] = max(col.get(s_, 0), v_)
            ARENA[:] = list(col.items())
            _anotes.clear()

        tok0 = em.emit("vector", lambda E: E.memset(ones1.ap[:], 1.0), inc="dve")
        ones1.produced(tok0)
        tok0 = em.emit("vector", lambda E: E.memset(ones128.ap[:], 1.0), inc="dve")
        ones128.produced(tok0)
        identS.produced(em.dma(identS.ap[:], W["ident"], "dL"))
        if 4 in need:
            triS.produced(em.dma(triS.ap[:], W["tri"], "dL"))
            bqkS.produced(em.dma(bqkS.ap[:], W["bqk"], "dL"))
            boutS.produced(em.dma(boutS.ap[:], W["bout"], "dL"))
        if need & {0, 2}:
            wrS = S("wrS", [128, NCH * NE], F32)
            wrS.produced(em.dma(wrS.ap[:], W["wr"], "dL"))
            t = em.emit("gpsimd", lambda E: E.tensor_copy(out=wrB.ap[:], in_=wrS.ap[:]),
                        waits=[wrS.ready], inc="pool")
            wrB.produced(t)

        in_toks = []
        for c in range(NCH):
            if first:
                lb = wland[c % 2]
                tl = em.dma(lb.ap[:, 0:T], x_in[c], "dL", waits=lb.barrier())
                lb.produced(tl)
                tc_ = em.emit("gpsimd",
                              lambda E, c=c, lb=lb: E.tensor_copy(out=rch(resid, c), in_=lb.ap[:, 0:T]),
                              waits=[tl], inc="pool")
                lb.used(tc_)
                in_toks.append(tc_)
            else:
                in_toks.append(em.dma(rch(resid, c), x_in[c], "dL"))
        resid.produced(in_toks[-1])
        for t in in_toks[:-1]:
            resid.used(t)
        resid_ready = resid.barrier()
        resid.users = []
        resid.ready = None
        RESID_TOKS = list(resid_ready)
        for c in range(NCH):
            RESID_CTOKS[c] = [in_toks[c]]

        def resid_wait():
            return list(RESID_TOKS)

        _pl = {"i": 0}

        def load_panel(hbm_ap, cols, scale=None, cache_to=None, cache_from=None):
            bb = wbf[_pl["i"] % 2]
            if cache_from is not None:
                _pl["i"] += 1
                tl = em.dma(bb.ap[:, 0:cols], cache_from, "dL",
                            waits=bb.barrier() + [("dC", em.cnt.get("dC", 0))])
                bb.produced(tl)
                return bb
            lb = wland[_pl["i"] % 2]
            _pl["i"] += 1
            tl = em.dma(lb.ap[:, 0:cols], hbm_ap, "dL", waits=lb.barrier())
            lb.produced(tl)
            if scale is None:
                if _pl["i"] % 2 == 0:
                    fn = lambda E, lb=lb, bb=bb, n=cols: E.tensor_copy(out=bb.ap[:, 0:n], in_=lb.ap[:, 0:n])
                    tc_ = em.emit("gpsimd", fn, waits=[tl] + bb.barrier(), inc="pool")
                else:
                    fn = lambda E, lb=lb, bb=bb, n=cols: E.activation(
                        out=bb.ap[:, 0:n], in_=lb.ap[:, 0:n], func=AF.Identity, bias=0.0, scale=1.0)
                    tc_ = em.emit("scalar", fn, waits=[tl] + bb.barrier(), inc="act")
            else:
                fn = lambda E, lb=lb, bb=bb, n=cols, s=scale: E.activation(
                    out=bb.ap[:, 0:n], in_=lb.ap[:, 0:n], func=AF.Identity, bias=0.0, scale=s)
                tc_ = em.emit("scalar", fn, waits=[tl] + bb.barrier(), inc="act")
            lb.used(tc_)
            bb.produced(tc_)
            if cache_to is not None:
                tw = em.dma(cache_to, bb.ap[:, 0:cols], "dC", waits=[tc_])
                bb.used(tw)
            return bb

        def emit_ln():
            xq = [Buf(scrT[:, 0:512]), Buf(scrT[:, 512:1024])]
            for kg in range(4):
                b_sum, b_sq = pb[0], pb[1]
                for c in range(NCH):
                    t = em.emit("tensor",
                                lambda E, c=c, kg=kg, b=b_sum: E.matmul(
                                    b.ap[:], ones128.ap[:], rsg(resid, c, kg),
                                    start=(c == 0), stop=(c == NCH - 1)),
                                waits=(RESID_CTOKS[c] + [ones128.ready] + (b_sum.barrier() if c == 0 else [])),
                                inc="pe" if c == NCH - 1 else None)
                b_sum.produced(t)
                ta = em.emit("scalar", lambda E, b=b_sum: E.copy(out=rowA.ap[0:1, :], in_=b.ap[0:1, :]),
                             waits=[b_sum.ready] + rowA.barrier(), inc="act")
                b_sum.used(ta)
                rowA.produced(ta)
                for c in range(NCH):
                    xb = xq[c % 2]
                    ts_ = em.emit("scalar",
                                  lambda E, c=c, kg=kg, xb=xb: E.activation(
                                      out=xb.ap[:], in_=rsg(resid, c, kg), func=AF.Square),
                                  waits=RESID_CTOKS[c] + xb.barrier() + ARENA, inc="act")
                    xb.produced(ts_)
                    t = em.emit("tensor",
                                lambda E, xb=xb, b=b_sq, c=c: E.matmul(
                                    b.ap[:], ones128.ap[:], xb.ap[:],
                                    start=(c == 0), stop=(c == NCH - 1)),
                                waits=[ts_] + (b_sq.barrier() if c == 0 else []),
                                inc="pe")
                    xb.used(t)
                    anote(t)
                b_sq.produced(t)
                tb = em.emit("scalar", lambda E, b=b_sq: E.copy(out=rowB.ap[0:1, :], in_=b.ap[0:1, :]),
                             waits=[b_sq.ready] + rowB.barrier(), inc="act")
                b_sq.used(tb)
                rowB.produced(tb)
                t1 = em.emit("vector", lambda E: E.tensor_scalar_mul(rowA.ap[0:1, :], rowA.ap[0:1, :], 1.0 / D),
                             waits=[rowA.ready], inc="dve")
                rowA.produced(t1)
                t2 = em.emit("vector", lambda E: E.scalar_tensor_tensor(
                    out=rowT.ap[0:1, :], in0=rowA.ap[0:1, :], scalar=-1.0, in1=rowA.ap[0:1, :],
                    op0=ALU.mult, op1=ALU.mult), waits=[t1] + rowT.barrier(), inc="dve")
                rowT.produced(t2)
                rowA.used(t2)
                t3 = em.emit("vector", lambda E: E.scalar_tensor_tensor(
                    out=rowB.ap[0:1, :], in0=rowB.ap[0:1, :], scalar=1.0 / D, in1=rowT.ap[0:1, :],
                    op0=ALU.mult, op1=ALU.add), waits=[rowB.ready, t2], inc="dve")
                rowB.produced(t3)
                rowT.used(t3)
                t3b = em.emit("vector", lambda E: E.tensor_scalar_add(rowB.ap[0:1, :], rowB.ap[0:1, :], EPS),
                              waits=[t3], inc="dve")
                t4 = em.emit("scalar", lambda E: E.activation(
                    out=rowB.ap[0:1, :], in_=rowB.ap[0:1, :], func=AF.Sqrt, bias=0.0, scale=1.0),
                    waits=[t3b], inc="act")
                rowB.produced(t4)
                t5 = em.emit("vector", lambda E: E.reciprocal(rowT.ap[0:1, :], rowB.ap[0:1, :]),
                             waits=[t4] + rowT.barrier(), inc="dve")
                rowT.produced(t5)
                rowB.used(t5)
                t5b = em.emit("vector", lambda E: E.tensor_copy(out=rowC.ap[0:1, :], in_=rowT.ap[0:1, :]),
                              waits=[t5] + rowC.barrier(), inc="dve")
                rowC.produced(t5b)
                rowT.used(t5b)
                t6 = em.emit("vector", lambda E: E.tensor_mul(rowD.ap[0:1, :], rowA.ap[0:1, :], rowT.ap[0:1, :]),
                             waits=[t5, rowA.ready] + rowD.barrier(), inc="dve")
                rowD.produced(t6)
                rowA.used(t6)
                rowT.used(t6)
                tr = em.emit("tensor", lambda E, b=b_sum: E.matmul(
                    b.ap[:], ones1.ap[:], rowC.ap[0:1, :], start=True, stop=True),
                    waits=[t5b, ones1.ready] + b_sum.barrier(), inc="pe")
                b_sum.produced(tr)
                rowC.used(tr)
                te = em.emit("scalar", lambda E, b=b_sum, kg=kg: E.copy(
                    out=rbc.ap[:, kg * 512:(kg + 1) * 512], in_=b.ap[:]),
                    waits=([tr] + rbc.barrier() + ARENA) if kg == 0 else [tr], inc="act")
                b_sum.used(te)
                if kg == 3:
                    rbc.produced(te)
                else:
                    rbc.used(te)
                bmr = pb[2 + kg]
                tm = em.emit("tensor", lambda E, b=bmr: E.matmul(
                    b.ap[:], ones1.ap[:], rowD.ap[0:1, :], start=True, stop=True),
                    waits=[t6, ones1.ready] + bmr.barrier(), inc="pe")
                bmr.produced(tm)
                rowD.used(tm)
            last = None
            for c in range(NCH):
                tmu = em.emit("vector", lambda E, c=c: E.tensor_mul(rch(xnorm, c), rch(resid, c), rbc.ap[:]),
                              waits=RESID_CTOKS[c] + [rbc.ready] + (xnorm.barrier() if c == 0 else []),
                              inc="dve")
                rbc.used(tmu)
                anote(tmu)
                for kg in range(4):
                    last = em.emit("vector", lambda E, c=c, kg=kg: E.tensor_sub(
                        rsg(xnorm, c, kg), rsg(xnorm, c, kg), pb[2 + kg].ap[:]),
                        waits=[tmu, pb[2 + kg].ready], inc="dve")
            for kg in range(4):
                pb[2 + kg].used(last)
            xnorm.produced(last)
            return last

        tbs = [Buf(scrT[:, 1024:1536]), Buf(scrT[:, 1536:2048])]

        def emit_glu(g_aps, u_aps, d_aps, d_kc, gbuf_base, clip7, use_wbc, d_scale,
                     pre_d_hook=None):
            nm = len(g_aps)
            for mi in range(nm):
                gp = load_panel(**g_aps[mi])
                up = load_panel(**u_aps[mi])
                for tgh in range(2):
                    bg = [pb[0], pb[1]]
                    bu = [pb[2], pb[3]]
                    for kc in range(NCH):
                        for j in range(2):
                            tg = tgh * 2 + j
                            t = em.emit("tensor", lambda E, gp=gp, kc=kc, tg=tg, b=bg[j]: E.matmul(
                                b.ap[:], gp.ap[:, kc * 128:(kc + 1) * 128], rsg(xnorm, kc, tg),
                                start=(kc == 0), stop=(kc == NCH - 1)),
                                waits=([gp.ready, xnorm.ready] + (bg[j].barrier() if kc == 0 else [])),
                                inc="pe" if kc == NCH - 1 else None)
                            if kc == NCH - 1:
                                bg[j].produced(t)
                                gp.used(t)
                                xnorm.used(t)
                    for kc in range(NCH):
                        for j in range(2):
                            tg = tgh * 2 + j
                            t = em.emit("tensor", lambda E, up=up, kc=kc, tg=tg, b=bu[j]: E.matmul(
                                b.ap[:], up.ap[:, kc * 128:(kc + 1) * 128], rsg(xnorm, kc, tg),
                                start=(kc == 0), stop=(kc == NCH - 1)),
                                waits=([up.ready, xnorm.ready] + (bu[j].barrier() if kc == 0 else [])),
                                inc="pe" if kc == NCH - 1 else None)
                            if kc == NCH - 1:
                                bu[j].produced(t)
                                up.used(t)
                                xnorm.used(t)
                    for j in range(2):
                        tg = tgh * 2 + j
                        tpos = (gbuf_base + mi) * T + tg * 512
                        tb = tbs[j]
                        ts_ = em.emit("scalar", lambda E, b=bg[j], tb=tb: E.activation(
                            out=tb.ap[:], in_=b.ap[:], func=AF.Silu),
                            waits=[bg[j].ready] + tb.barrier() + ARENA, inc="act")
                        tb.produced(ts_)
                        bg[j].used(ts_)
                        if use_wbc is not None:
                            tu = em.emit("vector", lambda E, b=bu[j], o=tpos, tg=tg: E.tensor_mul(
                                bigT[:, o:o + 512], b.ap[:],
                                use_wbc.ap[:, tg * 512:(tg + 1) * 512]),
                                waits=[bu[j].ready, use_wbc.ready, emit_glu.prev_d] + ARENA, inc="dve")
                            use_wbc.used(tu)
                            anote(tu)
                            bu[j].used(tu)
                            td = em.emit("vector", lambda E, tb=tb, o=tpos: E.scalar_tensor_tensor(
                                out=bigT[:, o:o + 512], in0=tb.ap[:], scalar=7.0,
                                in1=bigT[:, o:o + 512], op0=ALU.min, op1=ALU.mult),
                                waits=[ts_, tu], inc="dve")
                        else:
                            td = em.emit("vector", lambda E, tb=tb, b=bu[j], o=tpos: E.tensor_mul(
                                bigT[:, o:o + 512], tb.ap[:], b.ap[:]),
                                waits=[ts_, bu[j].ready, emit_glu.prev_d] + ARENA, inc="dve")
                            bu[j].used(td)
                        tb.used(td)
                        emit_glu.gtoks.append(td)
            gtok = emit_glu.gtoks[-1]
            gdeps = list(emit_glu.gtoks)
            emit_glu.gtoks = []
            if pre_d_hook is not None:
                pre_d_hook()
            for dm2 in range(8):
                dp = load_panel(**d_aps[dm2])
                for sub in range(2):
                    dm = dm2 * 2 + sub
                    for tgh in range(2):
                        bd = [pb[2], pb[3]] if tgh == 0 else [pb[4], pb[5]]
                        for kc in range(d_kc):
                            for j in range(2):
                                tg = tgh * 2 + j
                                t = em.emit("tensor", lambda E, dp=dp, kc=kc, tg=tg, b=bd[j],
                                            gb=gbuf_base, sub=sub: E.matmul(
                                    b.ap[:], dp.ap[:, sub * 1024 + kc * 128: sub * 1024 + (kc + 1) * 128],
                                    bigT[:, (gb + kc) * T + tg * 512: (gb + kc) * T + tg * 512 + 512],
                                    start=(kc == 0), stop=(kc == d_kc - 1)),
                                    waits=([dp.ready] + gdeps + (bd[j].barrier() if kc == 0 else [])),
                                    inc="pe" if kc == d_kc - 1 else None)
                                if kc == d_kc - 1:
                                    bd[j].produced(t)
                                    dp.used(t)
                                    emit_glu.prev_d = t
                                    anote(t)
                        for j in range(2):
                            tg = tgh * 2 + j
                            te = em.emit("vector", lambda E, dm=dm, tg=tg, b=bd[j]: E.tensor_add(
                                rsg(resid, dm, tg), rsg(resid, dm, tg), b.ap[:]),
                                waits=[bd[j].ready], inc="dve")
                            bd[j].used(te)
                            emit_glu.rtoks.append(te)
                            _rchunk[dm].append(te)
        emit_glu.gtoks = []
        emit_glu.rtoks = []
        emit_glu.prev_d = None

        def emit_router():
            emit_router.mtoks = []
            ev_toks = []
            for tc in range(NCH):
                b = pb[0] if tc % 2 == 0 else pb[1]
                for kc in range(NCH):
                    t = em.emit("tensor", lambda E, tc=tc, kc=kc, b=b: E.matmul(
                        b.ap[:, 0:8], xnorm.ap[:, kc * T + tc * 128: kc * T + tc * 128 + 128],
                        wrB.ap[:, kc * 8:(kc + 1) * 8],
                        start=(kc == 0), stop=(kc == NCH - 1)),
                        waits=([xnorm.ready, wrB.ready] + (b.barrier() if kc == 0 else [])),
                        inc="pe" if kc == NCH - 1 else None)
                b.produced(t)
                xnorm.used(t)
                te = em.emit("scalar", lambda E, tc=tc, b=b: E.copy(
                    out=rtr.ap[:, tc * 8:(tc + 1) * 8], in_=b.ap[:, 0:8]),
                    waits=[b.ready], inc="act")
                b.used(te)
                rtr.used(te)
                ev_toks.append(te)
            for tc in range(NCH):
                lg = rtr.ap[:, tc * 8:(tc + 1) * 8]
                t = em.emit("vector", lambda E, lg=lg: E.reduce_max(stat.ap[:, 0:1], lg, axis=AX.X),
                            waits=[ev_toks[tc]] + stat.barrier(), inc="dve")
                stat.produced(t)
                t1 = em.emit("vector", lambda E, lg=lg: E.tensor_scalar(
                    out=rtw.ap[:, 0:8], in0=lg, scalar1=stat.ap[:, 0:1], scalar2=1.0,
                    op0=ALU.is_equal, op1=ALU.mult), waits=[t] + rtw.barrier(), inc="dve")
                t2 = em.emit("vector", lambda E, lg=lg: E.scalar_tensor_tensor(
                    out=rtw.ap[:, 8:16], in0=rtw.ap[:, 0:8], scalar=NEG, in1=lg,
                    op0=ALU.mult, op1=ALU.add), waits=[t1], inc="dve")
                t3 = em.emit("vector", lambda E: E.reduce_max(stat.ap[:, 1:2], rtw.ap[:, 8:16], axis=AX.X),
                             waits=[t2], inc="dve")
                t4 = em.emit("vector", lambda E: E.tensor_sub(stat.ap[:, 2:3], stat.ap[:, 0:1], stat.ap[:, 1:2]),
                             waits=[t3], inc="dve")
                t5 = em.emit("scalar", lambda E: E.activation(out=stat.ap[:, 3:4], in_=stat.ap[:, 2:3],
                                                             func=AF.Sigmoid), waits=[t4], inc="act")
                t6 = em.emit("vector", lambda E: E.tensor_scalar(
                    out=stat.ap[:, 4:5], in0=stat.ap[:, 3:4], scalar1=-1.0, scalar2=1.0,
                    op0=ALU.mult, op1=ALU.add), waits=[t5], inc="dve")
                t7 = em.emit("vector", lambda E: E.tensor_scalar(
                    out=rtw.ap[:, 8:16], in0=rtw.ap[:, 8:16], scalar1=stat.ap[:, 1:2], scalar2=1.0,
                    op0=ALU.is_equal, op1=ALU.mult), waits=[t6], inc="dve")
                last = None
                for e in range(NE):
                    ta = em.emit("vector", lambda E, e=e: E.tensor_scalar(
                        out=rtw.ap[:, 8 + e:9 + e], in0=rtw.ap[:, 8 + e:9 + e],
                        scalar1=stat.ap[:, 4:5], scalar2=1.0, op0=ALU.mult, op1=ALU.mult),
                        waits=[t7], inc="dve")
                    last = em.emit("vector", lambda E, e=e, tc=tc: E.scalar_tensor_tensor(
                        out=wtm.ap[:, tc * 8 + e: tc * 8 + e + 1], in0=rtw.ap[:, e:e + 1],
                        scalar=stat.ap[:, 3:4], in1=rtw.ap[:, 8 + e:9 + e],
                        op0=ALU.mult, op1=ALU.add), waits=[ta], inc="dve")
                rtw.produced(last)
                stat.used(last)
                emit_router.mtoks.append(last)
            wtm.produced(last)

        def emit_wbc(e):
            last = None
            for tc in range(NCH):
                pt = pTa if tc % 2 == 0 else pTb
                tt = em.emit("tensor", lambda E, e=e, tc=tc, pt=pt: E.transpose(
                    pt.ap[0:1, 0:128], wtm.ap[:, tc * 8 + e: tc * 8 + e + 1], identS.ap[:]),
                    waits=[emit_router.mtoks[tc], identS.ready] + ARENA + pt.barrier(), inc="pe")
                pt.produced(tt)
                tc_ = em.emit("scalar", lambda E, pt=pt: E.copy(out=wrow.ap[0:1, :], in_=pt.ap[0:1, 0:128]),
                              waits=[tt] + wrow.barrier(), inc="act")
                pt.used(tc_)
                wrow.produced(tc_)
                b = pb[0] if tc % 2 == 0 else pb[1]
                tm = em.emit("tensor", lambda E, b=b: E.matmul(
                    b.ap[:, 0:128], ones1.ap[:], wrow.ap[0:1, :], start=True, stop=True),
                    waits=[tc_, ones1.ready] + b.barrier(), inc="pe")
                b.produced(tm)
                wrow.used(tm)
                last = em.emit("scalar", lambda E, tc=tc, b=b: E.copy(
                    out=wbc.ap[:, tc * 128:(tc + 1) * 128], in_=b.ap[:, 0:128]),
                    waits=([tm] + wbc.barrier() + ARENA) if tc == 0 else [tm], inc="act")
                b.used(last)
            wbc.produced(last)

        def emit_moe():
            call = emit_moe.call
            emit_moe.call += 1
            cached = {0, 2} <= need
            emit_router()
            emit_wbc(0)
            for e in range(NE):
                for hh in range(2):
                    if cached and call > 0:
                        g_aps = [dict(hbm_ap=None, cols=D, cache_from=W["wg_c"][e, hh * 8 + mi])
                                 for mi in range(8)]
                        u_aps = [dict(hbm_ap=None, cols=D, cache_from=W["wu_c"][e, hh * 8 + mi])
                                 for mi in range(8)]
                        d_aps = [dict(hbm_ap=None, cols=2048,
                                      cache_from=W["wd_c"][e, hh, dm2 * 2:(dm2 + 1) * 2]
                                      .transpose([1, 0, 2]))
                                 for dm2 in range(8)]
                    else:
                        g_aps = [dict(hbm_ap=W["wg"][e, hh * 8 + mi], cols=D,
                                      cache_to=W["wg_c"][e, hh * 8 + mi] if cached else None)
                                 for mi in range(8)]
                        u_aps = [dict(hbm_ap=W["wu"][e, hh * 8 + mi], cols=D,
                                      cache_to=W["wu_c"][e, hh * 8 + mi] if cached else None)
                                 for mi in range(8)]
                        d_aps = [dict(hbm_ap=W["wd"][e, hh, dm2 * 2:(dm2 + 1) * 2]
                                      .transpose([1, 0, 2]), cols=2048, scale=0.5,
                                      cache_to=W["wd_c"][e, hh, dm2 * 2:(dm2 + 1) * 2]
                                      .transpose([1, 0, 2]) if cached else None)
                                 for dm2 in range(8)]
                    hook = (lambda e=e: emit_wbc(e + 1)) if (hh == 1 and e + 1 < NE) else None
                    emit_glu(g_aps, u_aps, d_aps, 8, 0, True, wbc, 0.5, pre_d_hook=hook)
        emit_moe.call = 0

        def emit_ffn1():
            call = emit_ffn1.call
            emit_ffn1.call += 1
            cached = {3, 5} <= need
            for grp in range(8):
                if cached and call > 0:
                    g_aps = [dict(hbm_ap=None, cols=D, cache_from=W["l1_wg_c"][grp * 8 + mi])
                             for mi in range(8)]
                    u_aps = [dict(hbm_ap=None, cols=D, cache_from=W["l1_wu_c"][grp * 8 + mi])
                             for mi in range(8)]
                    d_aps = [dict(hbm_ap=None, cols=2048,
                                  cache_from=W["l1_wd_c"][grp, dm2 * 2:(dm2 + 1) * 2]
                                  .transpose([1, 0, 2]))
                             for dm2 in range(8)]
                else:
                    g_aps = [dict(hbm_ap=W["l1_wg"][grp * 8 + mi], cols=D,
                                  cache_to=W["l1_wg_c"][grp * 8 + mi] if cached else None)
                             for mi in range(8)]
                    u_aps = [dict(hbm_ap=W["l1_wu"][grp * 8 + mi], cols=D,
                                  cache_to=W["l1_wu_c"][grp * 8 + mi] if cached else None)
                             for mi in range(8)]
                    d_aps = [dict(hbm_ap=W["l1_wd"][grp, dm2 * 2:(dm2 + 1) * 2]
                                  .transpose([1, 0, 2]), cols=2048, scale=0.5,
                                  cache_to=W["l1_wd_c"][grp, dm2 * 2:(dm2 + 1) * 2]
                                  .transpose([1, 0, 2]) if cached else None)
                             for dm2 in range(8)]
                emit_glu(g_aps, u_aps, d_aps, 8, 0, False, None, 0.5)
        emit_ffn1.call = 0

        bigB = [Buf(bigT[:, i * T:(i + 1) * T]) for i in range(8)]

        def proj_head(panel, dest, bias_col, pbanks):
            toks = []
            for tgh in range(2):
                for kc in range(NCH):
                    for j in range(2):
                        tg = tgh * 2 + j
                        b = pbanks[tgh * 2 + j] if len(pbanks) == 4 else pbanks[j]
                        t = em.emit("tensor", lambda E, panel=panel, kc=kc, tg=tg, b=b: E.matmul(
                            b.ap[:], panel.ap[:, kc * 128:(kc + 1) * 128], rsg(xnorm, kc, tg),
                            start=(kc == 0), stop=(kc == NCH - 1)),
                            waits=([panel.ready, xnorm.ready] + (b.barrier() if kc == 0 else [])),
                            inc="pe" if kc == NCH - 1 else None)
                        if kc == NCH - 1:
                            b.produced(t)
                            panel.used(t)
                            xnorm.used(t)
                for j in range(2):
                    tg = tgh * 2 + j
                    b = pbanks[tgh * 2 + j] if len(pbanks) == 4 else pbanks[j]
                    if bias_col is None:
                        te = em.emit("scalar", lambda E, b=b, tg=tg: E.copy(
                            out=dest.ap[:, tg * 512:(tg + 1) * 512], in_=b.ap[:]),
                            waits=[b.ready] + ARENA + (dest.barrier() if tg == 0 else []), inc="act")
                    else:
                        te = em.emit("scalar", lambda E, b=b, tg=tg: E.activation(
                            out=dest.ap[:, tg * 512:(tg + 1) * 512], in_=b.ap[:], func=AF.Identity,
                            bias=bias_col, scale=1.0),
                            waits=[b.ready, bqkS.ready] + ARENA + (dest.barrier() if tg == 0 else []), inc="act")
                    b.used(te)
                    anote(te)
                    toks.append(te)
            dest.produced(toks[-1])
            for t in toks[:-1]:
                dest.used(t)

        def emit_cbound(hh, srcs, probs_scr, cbanks):
            sq = bigB[3]
            mtoks = []
            for oi, srcb in enumerate(srcs):
                tq = em.emit("scalar", lambda E, s=srcb: E.activation(
                    out=sq.ap[:, 0:T], in_=s.ap[:, 0:T], func=AF.Square),
                    waits=[srcb.ready] + sq.barrier() + ARENA, inc="act")
                sq.produced(tq)
                srcb.used(tq)
                for kg in range(4):
                    b = cbanks[kg % 2]
                    t = em.emit("tensor", lambda E, kg=kg, b=b: E.matmul(
                        b.ap[:], ones128.ap[:], sq.ap[:, kg * 512:(kg + 1) * 512],
                        start=True, stop=True),
                        waits=[tq, ones128.ready] + b.barrier(), inc="pe")
                    b.produced(t)
                    sq.used(t)
                    ta = em.emit("vector", lambda E, b=b, c=oi * 4 + kg, p=probs_scr: E.tensor_scalar(
                        out=p.ap[:, 0:512], in0=b.ap[:], scalar1=1.0, scalar2=0.0,
                        op0=ALU.mult, op1=ALU.max, accum_out=nrm.ap[:, c:c + 1]),
                        waits=[t] + probs_scr.barrier(), inc="dve")
                    b.used(ta)
                    probs_scr.used(ta)
                    mtoks.append(ta)
            t = em.emit("vector", lambda E: E.tensor_max(nrm.ap[:, 0:4], nrm.ap[:, 0:4], nrm.ap[:, 4:8]),
                        waits=mtoks, inc="dve")
            t = em.emit("vector", lambda E: E.tensor_max(nrm.ap[:, 0:2], nrm.ap[:, 0:2], nrm.ap[:, 2:4]),
                        waits=[t], inc="dve")
            t = em.emit("vector", lambda E: E.tensor_mul(nrm.ap[:, 8:9], nrm.ap[:, 0:1], nrm.ap[:, 1:2]),
                        waits=[t], inc="dve")
            t = em.emit("scalar", lambda E: E.activation(out=nrm.ap[:, 9:10], in_=nrm.ap[:, 8:9],
                                                         func=AF.Sqrt), waits=[t], inc="act")
            t = em.emit("vector", lambda E, hh=hh: E.tensor_scalar_mul(
                bcol.ap[:, hh:hh + 1], nrm.ap[:, 9:10], -SM_SCALE), waits=[t], inc="dve")
            return t

        def emit_attn0():
            q_h, k_h, v_h = bigB[0], bigB[1], bigB[2]
            prbs0 = [bigB[5], Buf(auxT[:, 0:T])]
            rrep = Buf(scrT[:, 0:128])
            btoks = [None, None]

            def proj_qk0(h):
                qp = load_panel(W["l0_wqkv"][h], D)
                proj_head(qp, q_h, None, [pb[0], pb[1]])
                kp = load_panel(W["l0_wqkv"][16 + h], D)
                proj_head(kp, k_h, None, [pb[0], pb[1]])

            def proj_v0(h):
                vp = load_panel(W["l0_wqkv"][32 + h], D)
                proj_head(vp, v_h, None, [pb[0], pb[1]])

            proj_qk0(0)
            btoks[0] = emit_cbound(0, (q_h, k_h), prbs0[0], [pb[2], pb[3]])
            proj_v0(0)
            for h in range(NH):

                def do_sexp(qc):
                    probs = prbs0[qc % 2]
                    sp = 6 * (qc % 2)
                    sbanks = [pb[0], pb[1]] if qc % 2 == 0 else [pTaF, pTbF]
                    extoks = []
                    for kg in range(4):
                        b = sbanks[kg % 2]
                        t = em.emit("tensor", lambda E, qc=qc, kg=kg, b=b: E.matmul(
                            b.ap[:], q_h.ap[:, qc * 128:(qc + 1) * 128],
                            k_h.ap[:, kg * 512:(kg + 1) * 512], start=True, stop=True),
                            waits=[q_h.ready, k_h.ready] + b.barrier(), inc="pe")
                        b.produced(t)
                        q_h.used(t)
                        k_h.used(t)
                        anote(t)
                        tex = em.emit("scalar", lambda E, kg=kg, b=b, probs=probs, sp=sp, hh=h % 2:
                                      E.activation(
                            out=probs.ap[:, kg * 512:(kg + 1) * 512], in_=b.ap[:], func=AF.Exp,
                            bias=bcol.ap[:, hh:hh + 1], scale=SM_SCALE,
                            accum_out=stat.ap[:, sp + kg:sp + kg + 1]),
                            waits=[t, btoks[h % 2]] + ARENA + (probs.barrier() if kg == 0 else []),
                            inc="act")
                        b.used(tex)
                        anote(tex)
                        extoks.append(tex)
                    probs.produced(extoks[-1])
                    for t in extoks[:-1]:
                        probs.used(t)

                do_sexp(0)
                for qc in range(NCH):
                    probs = prbs0[qc % 2]
                    sp = 6 * (qc % 2)
                    tr1 = em.emit("vector", lambda E, sp=sp: E.tensor_scalar(
                        out=nrm.ap[:, 8:12], in0=stat.ap[:, sp:sp + 4], scalar1=1.0, scalar2=0.0,
                        op0=ALU.mult, op1=ALU.add, accum_out=stat.ap[:, sp + 4:sp + 5]),
                        waits=probs.barrier() + [probs.ready], inc="dve")
                    tr2 = em.emit("vector", lambda E, sp=sp: E.reciprocal(
                        stat.ap[:, sp + 5:sp + 6], stat.ap[:, sp + 4:sp + 5]),
                        waits=[tr1], inc="dve")
                    t6 = em.emit("vector", lambda E, r=rrep, sp=sp: E.tensor_scalar_mul(
                        r.ap, ones128.ap[:], stat.ap[:, sp + 5:sp + 6]),
                        waits=[tr2] + rrep.barrier(), inc="dve")
                    rrep.produced(t6)
                    if qc + 1 < NCH:
                        do_sexp(qc + 1)
                    for kg in range(4):
                        b = pb[2 + kg]
                        t = em.emit("tensor", lambda E, kg=kg, qc=qc, b=b, r=rrep, probs=probs: E.matmul(
                            b.ap[:], r.ap, probs.ap[:, kg * 512:(kg + 1) * 512],
                            start=(qc == 0), stop=(qc == NCH - 1)),
                            waits=([t6, probs.ready] + (b.barrier() if qc == 0 else [])),
                            inc="pe")
                        if qc == NCH - 1:
                            b.produced(t)
                        rrep.used(t)
                        probs.used(t)
                        anote(t)
                if h + 1 < NH:
                    proj_qk0(h + 1)
                vtoks = []
                for kg in range(4):
                    b = pb[2 + kg]
                    tv = em.emit("vector", lambda E, kg=kg, b=b: E.tensor_mul(
                        v_h.ap[:, kg * 512:(kg + 1) * 512], v_h.ap[:, kg * 512:(kg + 1) * 512], b.ap[:]),
                        waits=[b.ready, v_h.ready], inc="dve")
                    b.used(tv)
                    v_h.used(tv)
                    anote(tv)
                    vtoks.append(tv)
                td = em.dma(W["vals"][h], v_h.ap[:, 0:T], "dV", waits=vtoks, queue="gpsimd")
                v_h.used(td)
                anote(td)
                em.dv_toks.append(td)
                if h + 1 < NH:
                    btoks[(h + 1) % 2] = emit_cbound((h + 1) % 2, (q_h, k_h), prbs0[0],
                                                     [pb[2], pb[3]])
                    proj_v0(h + 1)
            out_proj(W["l0_wo"], None)

        def out_proj(panels_ap, bias_cols):
            xbar = xnorm.barrier()
            pre = [load_panel(panels_ap[0], D), load_panel(panels_ap[1], D)]
            vload = []
            for c in range(NCH):
                t = em.dma(rch(xnorm, c), W["vals"][c], "dVL",
                           waits=xbar + [em.dv_toks[c]], queue="gpsimd")
                vload.append(t)
            xnorm.produced(vload[-1])
            for t in vload[:-1]:
                xnorm.used(t)
            for dm in range(16):
                op = pre[dm] if dm < 2 else load_panel(panels_ap[dm], D)
                for tgh in range(2):
                    bd = [pb[2], pb[3]] if tgh == 0 else [pb[4], pb[5]]
                    for kc in range(NCH):
                        for j in range(2):
                            tg = tgh * 2 + j
                            t = em.emit("tensor", lambda E, op=op, kc=kc, tg=tg, b=bd[j]: E.matmul(
                                b.ap[:], op.ap[:, kc * 128:(kc + 1) * 128], rsg(xnorm, kc, tg),
                                start=(kc == 0), stop=(kc == NCH - 1)),
                                waits=([op.ready, vload[kc]] + (bd[j].barrier() if kc == 0 else [])),
                                inc="pe" if kc == NCH - 1 else None)
                            if kc == NCH - 1:
                                bd[j].produced(t)
                                op.used(t)
                                xnorm.used(t)
                    for j in range(2):
                        tg = tgh * 2 + j
                        if bias_cols is None:
                            te = em.emit("vector", lambda E, dm=dm, tg=tg, b=bd[j]: E.tensor_add(
                                rsg(resid, dm, tg), rsg(resid, dm, tg), b.ap[:]),
                                waits=[bd[j].ready], inc="dve")
                        else:
                            te = em.emit("vector", lambda E, dm=dm, tg=tg, b=bd[j]: E.scalar_tensor_tensor(
                                out=rsg(resid, dm, tg), in0=b.ap[:], scalar=bias_cols[:, dm:dm + 1],
                                in1=rsg(resid, dm, tg), op0=ALU.add, op1=ALU.add),
                                waits=[bd[j].ready, boutS.ready], inc="dve")
                        bd[j].used(te)
                        emit_glu.rtoks.append(te)
                        _rchunk[dm].append(te)

        def emit_attn1():
            q_h, k_h, vtok = bigB[0], bigB[1], bigB[2]
            prbs = [bigB[5], bigB[7]]
            vals = bigB[6]
            btoks = [None, None]

            def proj_qk1(h):
                qp = load_panel(W["l1_wqkv"][h], D)
                proj_head(qp, q_h, bqkS.ap[:, h:h + 1], [pb[0], pb[1], pb[2], pb[5]])
                kp = load_panel(W["l1_wqkv"][16 + h], D)
                proj_head(kp, k_h, bqkS.ap[:, 16 + h:17 + h], [pb[0], pb[1], pb[2], pb[5]])

            def proj_v1(h):
                vp = load_panel(W["l1_wqkv"][32 + h], D)
                vtoks = []
                for tc in range(NCH):
                    b = pb[0] if tc % 2 == 0 else pb[1]
                    for kc in range(NCH):
                        t = em.emit("tensor", lambda E, vp=vp, tc=tc, kc=kc, b=b: E.matmul(
                            b.ap[:, 0:128], xnorm.ap[:, kc * T + tc * 128: kc * T + tc * 128 + 128],
                            vp.ap[:, kc * 128:(kc + 1) * 128],
                            start=(kc == 0), stop=(kc == NCH - 1)),
                            waits=([vp.ready, xnorm.ready] + (b.barrier() if kc == 0 else [])),
                            inc="pe" if kc == NCH - 1 else None)
                    b.produced(t)
                    vp.used(t)
                    xnorm.used(t)
                    te = em.emit("scalar", lambda E, tc=tc, b=b: E.copy(
                        out=vtok.ap[:, tc * 128:(tc + 1) * 128], in_=b.ap[:, 0:128]),
                        waits=[b.ready] + ARENA + (vtok.barrier() if tc == 0 else []), inc="act")
                    b.used(te)
                    anote(te)
                    vtoks.append(te)
                vtok.produced(vtoks[-1])
                for t in vtoks[:-1]:
                    vtok.used(t)

            proj_qk1(0)
            btoks[0] = emit_cbound(0, (q_h, k_h), prbs[0], [pb[3], pb[4]])
            proj_v1(0)
            for h in range(NH):

                def do_sexp1(qc):
                    probs = prbs[qc % 2]
                    sp = 6 * (qc % 2)
                    L = (qc + 1) * 128
                    nkg = (L + 511) // 512
                    sbanks = [pb[0], pb[1]] if qc % 2 == 0 else [pb[2], pb[5]]
                    extoks = []
                    for kg in range(nkg):
                        w = min(512, L - kg * 512)
                        b = sbanks[kg % 2]
                        t = em.emit("tensor", lambda E, qc=qc, kg=kg, w=w, b=b: E.matmul(
                            b.ap[:, 0:w], q_h.ap[:, qc * 128:(qc + 1) * 128],
                            k_h.ap[:, kg * 512: kg * 512 + w], start=True, stop=True),
                            waits=[q_h.ready, k_h.ready] + b.barrier(), inc="pe")
                        b.produced(t)
                        q_h.used(t)
                        k_h.used(t)
                        anote(t)
                        if kg < nkg - 1:
                            tex = em.emit("scalar", lambda E, kg=kg, b=b, probs=probs, sp=sp,
                                          hh=h % 2: E.activation(
                                out=probs.ap[:, kg * 512:(kg + 1) * 512], in_=b.ap[:], func=AF.Exp,
                                bias=bcol.ap[:, hh:hh + 1], scale=SM_SCALE,
                                accum_out=stat.ap[:, sp + kg:sp + kg + 1]),
                                waits=[t, btoks[h % 2]] + ARENA + (probs.barrier() if kg == 0 else []),
                                inc="act")
                        else:
                            tex = em.emit("scalar", lambda E, kg=kg, w=w, b=b, probs=probs,
                                          hh=h % 2: E.activation(
                                out=probs.ap[:, kg * 512: kg * 512 + w], in_=b.ap[:, 0:w], func=AF.Exp,
                                bias=bcol.ap[:, hh:hh + 1], scale=SM_SCALE),
                                waits=[t, btoks[h % 2]] + ARENA + (probs.barrier() if kg == 0 else []),
                                inc="act")
                        b.used(tex)
                        anote(tex)
                        extoks.append(tex)
                    tm = em.emit("vector", lambda E, qc=qc, probs=probs: E.tensor_mul(
                        probs.ap[:, qc * 128:(qc + 1) * 128], probs.ap[:, qc * 128:(qc + 1) * 128],
                        triS.ap[:]), waits=[extoks[-1], triS.ready], inc="dve")
                    tps = em.emit("vector", lambda E, qc=qc, probs=probs, sp=sp, L=L, nkg=nkg:
                                  E.tensor_scalar(
                        out=bigB[3].ap[:, 0:(L - (nkg - 1) * 512)],
                        in0=probs.ap[:, (nkg - 1) * 512: L], scalar1=1.0, scalar2=0.0,
                        op0=ALU.mult, op1=ALU.add, accum_out=stat.ap[:, sp + 3:sp + 4]),
                        waits=[tm] + bigB[3].barrier(), inc="dve")
                    bigB[3].used(tps)
                    tr = tps
                    for kg in range(nkg - 1):
                        tr = em.emit("vector", lambda E, sp=sp, kg=kg: E.tensor_add(
                            stat.ap[:, sp + 3:sp + 4], stat.ap[:, sp + 3:sp + 4],
                            stat.ap[:, sp + kg:sp + kg + 1]),
                            waits=[tr, extoks[kg]], inc="dve")
                    tr2 = em.emit("vector", lambda E, sp=sp: E.reciprocal(
                        stat.ap[:, sp + 5:sp + 6], stat.ap[:, sp + 3:sp + 4]),
                        waits=[tr], inc="dve")
                    tn = em.emit("vector", lambda E, L=L, sp=sp, probs=probs: E.tensor_scalar_mul(
                        probs.ap[:, 0:L], probs.ap[:, 0:L], stat.ap[:, sp + 5:sp + 6]),
                        waits=[tr2] + ([t for t in extoks[:-1]]), inc="dve")
                    probs.produced(tn)
                    return tn

                tns = [None] * NCH
                tns[0] = do_sexp1(0)
                for qc in range(NCH):
                    probs = prbs[qc % 2]
                    tn = tns[qc]
                    if qc + 1 < NCH:
                        tns[qc + 1] = do_sexp1(qc + 1)
                    bav = pb[3] if qc % 2 == 0 else pb[4]
                    cptoks = [None] * (qc + 1)

                    def emit_T(kc, probs=probs, tn=tn):
                        pt = pTa if kc % 2 == 0 else pTb
                        tt = em.emit("tensor", lambda E, kc=kc, pt=pt: E.transpose(
                            pt.ap[:, 0:128], probs.ap[:, kc * 128:(kc + 1) * 128], identS.ap[:]),
                            waits=[tn, identS.ready] + ARENA + pt.barrier(), inc="pe")
                        pt.produced(tt)
                        probs.used(tt)
                        ps = pTs[kc % 4]
                        tc_ = em.emit("scalar", lambda E, pt=pt, ps=ps: E.copy(
                            out=ps.ap[:], in_=pt.ap[:, 0:128]),
                            waits=[tt] + ARENA + ps.barrier(), inc="act")
                        pt.used(tc_)
                        ps.produced(tc_)
                        cptoks[kc] = tc_

                    emit_T(0)
                    if qc >= 1:
                        emit_T(1)
                    for kc in range(qc + 1):
                        if kc + 2 <= qc:
                            emit_T(kc + 2)
                        ps = pTs[kc % 4]
                        t = em.emit("tensor", lambda E, kc=kc, qc=qc, b=bav, ps=ps: E.matmul(
                            b.ap[:, 0:128], vtok.ap[:, kc * 128:(kc + 1) * 128], ps.ap[:],
                            start=(kc == 0), stop=(kc == qc)),
                            waits=([cptoks[kc], vtok.ready] + (bav.barrier() if kc == 0 else [])),
                            inc="pe")
                        ps.used(t)
                        vtok.used(t)
                        anote(t)
                    bav.produced(t)
                    te = em.emit("scalar", lambda E, qc=qc, b=bav, h=h: E.activation(
                        out=vals.ap[:, qc * 128:(qc + 1) * 128], in_=b.ap[:, 0:128], func=AF.Identity,
                        bias=bqkS.ap[:, 32 + h:33 + h], scale=1.0),
                        waits=[bav.ready, bqkS.ready] + (vals.barrier() if qc == 0 else []), inc="act")
                    bav.used(te)
                    vals.used(te)
                if h + 1 < NH:
                    proj_qk1(h + 1)
                td = em.dma(W["vals"][h], vals.ap[:, 0:T], "dV", waits=vals.barrier(), queue="gpsimd")
                vals.produced(None)
                vals.used(td)
                anote(td)
                em.dv_toks.append(td)
                if h + 1 < NH:
                    btoks[(h + 1) % 2] = emit_cbound((h + 1) % 2, (q_h, k_h), prbs[0],
                                                     [pb[3], pb[4]])
                    proj_v1(h + 1)
            out_proj(W["l1_wout"], boutS.ap)

        tap_done = False
        for si, sub in enumerate(sublayers):
            arena_flush()
            emit_ln()
            if tap == "big" and sub == 1:
                em.dv_toks = []
                emit_attn0()
                col = {}
                for s_, v_ in _anotes:
                    col[s_] = max(col.get(s_, 0), v_)
                bw = list(col.items())
                for c in range(8):
                    em.dma(y_out[c], bigT[:, c * T:(c + 1) * T], "dO", waits=bw)
                em.emit("sync", lambda E: E.nop(), waits=[("dO", 16 * 8)])
                tap_done = True
                break
            if tap == "rbc":
                for kg in range(4):
                    em.emit("scalar", lambda E, kg=kg: E.copy(
                        out=rch(xnorm, 0)[:, kg * 512:(kg + 1) * 512], in_=pb[2 + kg].ap[:]),
                        waits=[pb[2 + kg].ready], inc="act")
                tmp = em.emit("scalar", lambda E: E.nop(), waits=[], inc="act")
                em.dma(y_out[0], rbc.ap[:], "dO", waits=[rbc.ready])
                em.dma(y_out[1], rch(xnorm, 0), "dO", waits=[("act", em.cnt["act"])])
                em.emit("sync", lambda E: E.nop(), waits=[("dO", 32)])
                tap_done = True
                break
            if tap == "xnorm":
                for c in range(NCH):
                    em.dma(y_out[c], rch(xnorm, c), "dO", waits=[xnorm.ready])
                em.emit("sync", lambda E: E.nop(), waits=[("dO", 16 * NCH)])
                tap_done = True
                break
            if tap in ("logits", "wtm") and sub in (0, 2):
                emit_router()
                if tap == "logits":
                    tw = [rtr.users[i] for i in range(NCH)]
                    em.dma(y_out[0][:, 0:128], rtr.ap[:], "dO", waits=tw)
                else:
                    em.dma(y_out[0][:, 0:128].rearrange("p t -> p t"), wtm.ap[:], "dO",
                           waits=[wtm.ready])
                em.emit("sync", lambda E: E.nop(), waits=[("dO", 16)])
                tap_done = True
                break
            if tap == "wbc0" and sub in (0, 2):
                emit_router()
                emit_wbc(0)
                em.dma(y_out[0], wbc.ap[:], "dO", waits=[wbc.ready])
                em.emit("sync", lambda E: E.nop(), waits=[("dO", 16)])
                tap_done = True
                break
            if sub in (0, 2):
                emit_moe()
            elif sub == 1:
                em.dv_toks = []
                emit_attn0()
            elif sub == 3 or sub == 5:
                emit_ffn1()
            elif sub == 4:
                em.dv_toks = []
                emit_attn1()
            RESID_TOKS[:] = [t for t in emit_glu.rtoks]
            emit_glu.rtoks = []
            col = {}
            for s_, v_ in RESID_TOKS:
                col[s_] = max(col.get(s_, 0), v_)
            RESID_TOKS[:] = list(col.items())
            for c in range(NCH):
                cc = {}
                for s_, v_ in _rchunk[c]:
                    cc[s_] = max(cc.get(s_, 0), v_)
                RESID_CTOKS[c] = list(cc.items()) if cc else list(RESID_TOKS)
                _rchunk[c] = []
            xnorm.produced(None)
            xnorm.users = []

        if tap_done:
            block = ctx.enter_context(nc.Block())
            for eng in ENGINES:
                ops = em.ops[eng]
                if not ops:
                    continue

                def body(E, ops=ops):
                    for op in ops:
                        op(E)
                getattr(block, eng)(body)
            return nc
        st_toks = []
        for c in range(NCH):
            if last:
                lb = wland[c % 2]
                tc_ = em.emit("gpsimd", lambda E, c=c, lb=lb: E.tensor_copy(
                    out=lb.ap[:, 0:T], in_=rch(resid, c)),
                    waits=resid_wait() + lb.barrier(), inc="pool")
                lb.produced(tc_)
                td = em.dma(y_out[c], lb.ap[:, 0:T], "dO", waits=[tc_])
                lb.used(td)
            else:
                td = em.dma(y_out[c], rch(resid, c), "dO", waits=resid_wait())
            st_toks.append(td)
        em.emit("sync", lambda E: E.nop(), waits=[("dO", 16 * NCH)])

        block = ctx.enter_context(nc.Block())
        for eng in ENGINES:
            ops = em.ops[eng]
            if not ops:
                continue

            def body(E, ops=ops):
                for op in ops:
                    op(E)
            getattr(block, eng)(body)

    return nc



GROUPS = [[0, 1, 2, 3, 4, 5]]

_cache = {}


def _get_prog(key, subs, first, last):
    if key not in _cache:
        _cache[key] = build_program(subs, first, last)
    return _cache[key]


def kernel(**inputs):
    h = prep_inputs(inputs)
    shared = {
        "wr": h["wr"], "wg": h["wg"], "wu": h["wu"], "wd": h["wd"],
        "l0_wqkv": h["l0_wqkv"], "l0_wo": h["l0_wo"],
        "l1_wqkv": h["l1_wqkv"], "l1_wout": h["l1_wout"],
        "bqk": h["bqk"], "bout": h["bout"], "tri": h["tri"],
        "l1_wg": h["l1_wg"], "l1_wu": h["l1_wu"], "l1_wd": h["l1_wd"],
        "ident": h["ident"],
    }

    def in_map_for(subs, first, core, cur):
        need = set(subs)
        m = {}
        m["x"] = h["x"][core] if first else cur[core]
        if need & {0, 2}:
            for k in ("wr", "wg", "wu", "wd"):
                m[k] = shared[k]
        if 1 in need:
            m["l0_wqkv"] = shared["l0_wqkv"]
            m["l0_wo"] = shared["l0_wo"]
        if 4 in need:
            for k in ("l1_wqkv", "l1_wout", "bqk", "bout", "tri"):
                m[k] = shared[k]
        if need & {3, 5}:
            for k in ("l1_wg", "l1_wu", "l1_wd"):
                m[k] = shared[k]
        m["ident"] = shared["ident"]
        return m

    cur = None
    for gi, subs in enumerate(GROUPS):
        first = gi == 0
        last = gi == len(GROUPS) - 1
        nc = _get_prog((tuple(subs), first, last), subs, first, last)
        in_maps = [in_map_for(subs, first, c, cur) for c in range(N_CORES)]
        res = run_bass_kernel_spmd(nc, in_maps, core_ids=list(range(N_CORES)))
        cur = [res.results[c]["y"] for c in range(N_CORES)]

    out = np.empty((N_CORES, T, D), np.float32)
    for c in range(N_CORES):
        out[c] = cur[c].reshape(D, T).T
    return out


# version 20 (final)
# speedup vs baseline: 1.0403x; 1.0403x over previous
import math
import numpy as np
import ml_dtypes
from contextlib import ExitStack

import concourse.bass as bass
import concourse.mybir as mybir
from concourse.bass_utils import run_bass_kernel_spmd

F32 = mybir.dt.float32
BF16 = mybir.dt.bfloat16
AF = mybir.ActivationFunctionType
ALU = mybir.AluOpType
AX = mybir.AxisListType
BF = ml_dtypes.bfloat16

N_CORES = 8
D = 2048
T = 2048
NCH = 16
NH = 16
NE = 8
F1 = 8192
EPS = 1e-5
SM_SCALE = 1.0 / math.sqrt(128.0)
NEG = -30000.0

ENGINES = ("sync", "tensor", "vector", "scalar", "gpsimd")
DEBUG = {}
SUBLAYERS = (0, 1, 2, 3, 4, 5)


def _panels(W):
    K, M = W.shape
    return np.ascontiguousarray(
        W.reshape(K // 128, 128, M // 128, 128).transpose(2, 1, 0, 3).reshape(M // 128, 128, K))


def prep_inputs(inp):
    h = {}
    x = np.asarray(inp["L0_attn_x"], np.float32)
    h["x"] = [np.ascontiguousarray(x[c].T).reshape(NCH, 128, T) for c in range(N_CORES)]
    h["l0_wqkv"] = _panels(np.asarray(inp["L0_attn_w_qkv"], np.float32))
    h["l0_wo"] = _panels(np.asarray(inp["L0_attn_w_o"], np.float32))
    h["wr"] = np.ascontiguousarray(
        np.asarray(inp["L0_ffn_Wr"], np.float32).reshape(NCH, 128, NE).transpose(1, 0, 2)).reshape(128, NCH * NE)
    for nm, key in (("wg", "L0_ffn_Wg"), ("wu", "L0_ffn_Wu"), ("wd", "L0_ffn_Wd")):
        h[nm] = np.ascontiguousarray(np.stack(
            [_panels(np.asarray(inp[key], np.float32)[e]) for e in range(NE)]))
    h["wd"] = np.ascontiguousarray(
        h["wd"].reshape(NE, 16, 128, 2, 1024).transpose(0, 3, 1, 2, 4))
    h["l1_wqkv"] = _panels(np.ascontiguousarray(np.asarray(inp["L1_attn_w_qkv"], np.float32).T))
    h["l1_wout"] = _panels(np.ascontiguousarray(np.asarray(inp["L1_attn_w_out"], np.float32).T))
    h["l1_wg"] = _panels(np.ascontiguousarray(np.asarray(inp["L1_ffn_w_gate"], np.float32).T))
    h["l1_wu"] = _panels(np.ascontiguousarray(np.asarray(inp["L1_ffn_w_up"], np.float32).T))
    l1wd = _panels(np.ascontiguousarray(np.asarray(inp["L1_ffn_w_down"], np.float32).T))
    h["l1_wd"] = np.ascontiguousarray(l1wd.reshape(16, 128, 8, 1024).transpose(2, 0, 1, 3))
    h["bqk"] = np.ascontiguousarray(np.asarray(inp["L1_attn_b_qkv"], np.float32).reshape(48, 128).T)
    h["bout"] = np.ascontiguousarray(np.asarray(inp["L1_attn_b_out"], np.float32).reshape(16, 128).T)
    h["ident"] = np.eye(128, dtype=np.float32).astype(BF)
    h["tri"] = np.tril(np.ones((128, 128), np.float32)).astype(BF)
    return h



class Em:
    def __init__(self):
        self.ops = {e: [] for e in ENGINES}
        self.cnt = {}
        self.hw = {}
        self.sems = {}
        self.dv_toks = []

    def _wait(self, eng, tok):
        if tok is None:
            return
        sem, val = tok
        if val <= 0 or val <= self.hw.get((eng, sem), 0):
            return
        self.hw[(eng, sem)] = val

        def cl(E, s=sem, v=val):
            E.wait_ge(self.sems[s], v)
        self.ops[eng].append(cl)

    def emit(self, eng, fn, waits=(), inc=None, n=1):
        for t in waits:
            self._wait(eng, t)
        if inc is None:
            self.ops[eng].append(fn)
            return None
        self.cnt[inc] = self.cnt.get(inc, 0) + n
        tgt = self.cnt[inc]

        def cl(E, fn=fn, s=inc, k=n):
            fn(E).then_inc(self.sems[s], k)
        self.ops[eng].append(cl)
        return (inc, tgt)

    def dma(self, out, in_, sem, waits=(), queue="sync"):
        return self.emit(queue, lambda E, o=out, i=in_: E.dma_start(out=o, in_=i),
                         waits=waits, inc=sem, n=16)


class Buf:
    __slots__ = ("ap", "ready", "users")

    def __init__(self, ap):
        self.ap = ap
        self.ready = None
        self.users = []

    def barrier(self):
        toks = {}
        for t in self.users + ([self.ready] if self.ready else []):
            if t and t[1] > toks.get(t[0], -1):
                toks[t[0]] = t[1]
        return [(s, v) for s, v in toks.items()]

    def produced(self, tok):
        self.ready = tok
        self.users = []

    def used(self, tok):
        if tok is not None:
            self.users.append(tok)



def build_program(sublayers, first, last, tap=None):
    nc = bass.Bass("TRN2", target_bir_lowering=False)
    em = Em()
    need = set(sublayers)

    x_in = nc.dram_tensor("x", [NCH, 128, T], F32 if first else BF16, kind="ExternalInput").ap()
    y_out = nc.dram_tensor("y", [NCH, 128, T], F32 if last else BF16, kind="ExternalOutput").ap()

    W = {}
    if need & {0, 2}:
        W["wr"] = nc.dram_tensor("wr", [128, NCH * NE], F32, kind="ExternalInput").ap()
        for nm in ("wg", "wu"):
            W[nm] = nc.dram_tensor(nm, [NE, 16, 128, D], F32, kind="ExternalInput").ap()
        W["wd"] = nc.dram_tensor("wd", [NE, 2, 16, 128, 1024], F32, kind="ExternalInput").ap()
    if 1 in need:
        W["l0_wqkv"] = nc.dram_tensor("l0_wqkv", [48, 128, D], F32, kind="ExternalInput").ap()
        W["l0_wo"] = nc.dram_tensor("l0_wo", [16, 128, D], F32, kind="ExternalInput").ap()
    if 4 in need:
        W["l1_wqkv"] = nc.dram_tensor("l1_wqkv", [48, 128, D], F32, kind="ExternalInput").ap()
        W["l1_wout"] = nc.dram_tensor("l1_wout", [16, 128, D], F32, kind="ExternalInput").ap()
        W["bqk"] = nc.dram_tensor("bqk", [128, 48], F32, kind="ExternalInput").ap()
        W["bout"] = nc.dram_tensor("bout", [128, 16], F32, kind="ExternalInput").ap()
        W["tri"] = nc.dram_tensor("tri", [128, 128], BF16, kind="ExternalInput").ap()
    if need & {3, 5}:
        W["l1_wg"] = nc.dram_tensor("l1_wg", [64, 128, D], F32, kind="ExternalInput").ap()
        W["l1_wu"] = nc.dram_tensor("l1_wu", [64, 128, D], F32, kind="ExternalInput").ap()
        W["l1_wd"] = nc.dram_tensor("l1_wd", [8, 16, 128, 1024], F32, kind="ExternalInput").ap()
    W["ident"] = nc.dram_tensor("ident", [128, 128], BF16, kind="ExternalInput").ap()
    if {0, 2} <= need:
        for nm in ("wg", "wu"):
            W[nm + "_c"] = nc.dram_tensor(nm + "_c", [NE, 16, 128, D], BF16, kind="Internal").ap()
        W["wd_c"] = nc.dram_tensor("wd_c", [NE, 2, 16, 128, 1024], BF16, kind="Internal").ap()
    if {3, 5} <= need:
        W["l1_wg_c"] = nc.dram_tensor("l1_wg_c", [64, 128, D], BF16, kind="Internal").ap()
        W["l1_wu_c"] = nc.dram_tensor("l1_wu_c", [64, 128, D], BF16, kind="Internal").ap()
        W["l1_wd_c"] = nc.dram_tensor("l1_wd_c", [8, 16, 128, 1024], BF16, kind="Internal").ap()
    if need & {1, 4}:
        W["vals"] = nc.dram_tensor("vals", [NCH, 128, T], BF16, kind="Internal").ap()

    ctx = ExitStack()
    with ctx:
        def S(name, shape, dt):
            return Buf(ctx.enter_context(nc.sbuf_tensor(name, shape, dt)).ap())

        resid = S("resid", [128, NCH * T], BF16)
        xnorm = S("xnorm", [128, NCH * T], BF16)
        bigT = ctx.enter_context(nc.sbuf_tensor("big", [128, 16384], BF16)).ap()
        wland = [S(f"wland{i}", [128, 2048], F32) for i in range(2)]
        wbf = [S(f"wbf{i}", [128, 2048], BF16) for i in range(2)]
        scrT = ctx.enter_context(nc.sbuf_tensor("scr", [128, 2048], BF16)).ap()
        auxT = ctx.enter_context(nc.sbuf_tensor("aux", [128, 4096], BF16)).ap()
        rbc = Buf(auxT[:, 0:T])
        wbc = Buf(auxT[:, T:2 * T]) if need & {0, 2} else None
        aux_f32 = auxT.bitcast(F32)
        rowA = S("rowA", [1, 512], F32)
        rowB = S("rowB", [1, 512], F32)
        rowT = S("rowT", [1, 512], F32)
        rowC = S("rowC", [1, 512], BF16)
        rowD = S("rowD", [1, 512], BF16)
        wrow = S("wrow", [1, 128], BF16)
        ones1 = S("ones1", [1, 128], BF16)
        ones128 = S("ones128", [128, 128], BF16)
        stat = S("stat", [128, 12], F32)
        nrm = S("nrm", [128, 12], F32)
        bcol = S("bcol", [128, 2], F32)
        rtr = S("rtr", [128, 128], F32) if need & {0, 2} else None
        rtw = S("rtw", [128, 16], F32) if need & {0, 2} else None
        wtm = S("wtm", [128, 128], BF16) if need & {0, 2} else None
        wrB = S("wrB", [128, NCH * NE], BF16) if need & {0, 2} else None
        identS = S("identS", [128, 128], BF16)
        triS = S("triS", [128, 128], BF16) if 4 in need else None
        bqkS = S("bqkS", [128, 48], F32) if 4 in need else None
        boutS = S("boutS", [128, 16], F32) if 4 in need else None
        pTs = ([S(f"pTs{i}", [128, 128], BF16) for i in range(2)]
               + [Buf(scrT[:, 1024:1152]), Buf(scrT[:, 1152:1280])]) if 4 in need else None

        pb = [Buf(ctx.enter_context(nc.psum_tensor(f"pb{i}", [128, 512], F32)).ap())
              for i in range(6)]
        pTa = Buf(ctx.enter_context(nc.psum_tensor("pTa", [128, 1024], BF16)).ap())
        pTb = Buf(ctx.enter_context(nc.psum_tensor("pTb", [128, 1024], BF16)).ap())
        pTaF = Buf(pTa.ap.bitcast(F32))
        pTbF = Buf(pTb.ap.bitcast(F32))

        for s in ("dL", "dV", "dVL", "dO", "dC", "pe", "act", "dve", "pool"):
            em.sems[s] = ctx.enter_context(nc.semaphore(s))

        def rch(buf, c):
            return buf.ap[:, c * T:(c + 1) * T]

        def rsg(buf, c, tg, w=512):
            return buf.ap[:, c * T + tg * w: c * T + (tg + 1) * w]

        ARENA = []
        _anotes = []
        RESID_CTOKS = [[] for _ in range(NCH)]
        _rchunk = [[] for _ in range(NCH)]

        def anote(tok):
            if tok is not None:
                _anotes.append(tok)
            return tok

        def arena_flush():
            col = {}
            for s_, v_ in _anotes:
                col[s_] = max(col.get(s_, 0), v_)
            ARENA[:] = list(col.items())
            _anotes.clear()

        tok0 = em.emit("vector", lambda E: E.memset(ones1.ap[:], 1.0), inc="dve")
        ones1.produced(tok0)
        tok0 = em.emit("vector", lambda E: E.memset(ones128.ap[:], 1.0), inc="dve")
        ones128.produced(tok0)
        identS.produced(em.dma(identS.ap[:], W["ident"], "dL"))
        if 4 in need:
            triS.produced(em.dma(triS.ap[:], W["tri"], "dL"))
            bqkS.produced(em.dma(bqkS.ap[:], W["bqk"], "dL"))
            boutS.produced(em.dma(boutS.ap[:], W["bout"], "dL"))
        if need & {0, 2}:
            wrS = S("wrS", [128, NCH * NE], F32)
            wrS.produced(em.dma(wrS.ap[:], W["wr"], "dL"))
            t = em.emit("gpsimd", lambda E: E.tensor_copy(out=wrB.ap[:], in_=wrS.ap[:]),
                        waits=[wrS.ready], inc="pool")
            wrB.produced(t)

        in_toks = []
        for c in range(NCH):
            if first:
                lb = wland[c % 2]
                tl = em.dma(lb.ap[:, 0:T], x_in[c], "dL", waits=lb.barrier())
                lb.produced(tl)
                tc_ = em.emit("gpsimd",
                              lambda E, c=c, lb=lb: E.tensor_copy(out=rch(resid, c), in_=lb.ap[:, 0:T]),
                              waits=[tl], inc="pool")
                lb.used(tc_)
                in_toks.append(tc_)
            else:
                in_toks.append(em.dma(rch(resid, c), x_in[c], "dL"))
        resid.produced(in_toks[-1])
        for t in in_toks[:-1]:
            resid.used(t)
        resid_ready = resid.barrier()
        resid.users = []
        resid.ready = None
        RESID_TOKS = list(resid_ready)
        for c in range(NCH):
            RESID_CTOKS[c] = [in_toks[c]]

        def resid_wait():
            return list(RESID_TOKS)

        _pl = {"i": 0}

        def load_panel(hbm_ap, cols, scale=None, cache_to=None, cache_from=None):
            bb = wbf[_pl["i"] % 2]
            if cache_from is not None:
                _pl["i"] += 1
                tl = em.dma(bb.ap[:, 0:cols], cache_from, "dL",
                            waits=bb.barrier() + [("dC", em.cnt.get("dC", 0))])
                bb.produced(tl)
                return bb
            lb = wland[_pl["i"] % 2]
            _pl["i"] += 1
            tl = em.dma(lb.ap[:, 0:cols], hbm_ap, "dL", waits=lb.barrier())
            lb.produced(tl)
            if scale is None:
                if _pl["i"] % 2 == 0:
                    fn = lambda E, lb=lb, bb=bb, n=cols: E.tensor_copy(out=bb.ap[:, 0:n], in_=lb.ap[:, 0:n])
                    tc_ = em.emit("gpsimd", fn, waits=[tl] + bb.barrier(), inc="pool")
                else:
                    fn = lambda E, lb=lb, bb=bb, n=cols: E.activation(
                        out=bb.ap[:, 0:n], in_=lb.ap[:, 0:n], func=AF.Identity, bias=0.0, scale=1.0)
                    tc_ = em.emit("scalar", fn, waits=[tl] + bb.barrier(), inc="act")
            else:
                fn = lambda E, lb=lb, bb=bb, n=cols, s=scale: E.activation(
                    out=bb.ap[:, 0:n], in_=lb.ap[:, 0:n], func=AF.Identity, bias=0.0, scale=s)
                tc_ = em.emit("scalar", fn, waits=[tl] + bb.barrier(), inc="act")
            lb.used(tc_)
            bb.produced(tc_)
            if cache_to is not None:
                tw = em.dma(cache_to, bb.ap[:, 0:cols], "dC", waits=[tc_])
                bb.used(tw)
            return bb

        def emit_ln():
            xq = [Buf(scrT[:, 0:512]), Buf(scrT[:, 512:1024])]
            for kg in range(4):
                b_sum, b_sq = pb[0], pb[1]
                for c in range(NCH):
                    t = em.emit("tensor",
                                lambda E, c=c, kg=kg, b=b_sum: E.matmul(
                                    b.ap[:], ones128.ap[:], rsg(resid, c, kg),
                                    start=(c == 0), stop=(c == NCH - 1)),
                                waits=(RESID_CTOKS[c] + [ones128.ready] + (b_sum.barrier() if c == 0 else [])),
                                inc="pe" if c == NCH - 1 else None)
                b_sum.produced(t)
                ta = em.emit("scalar", lambda E, b=b_sum: E.copy(out=rowA.ap[0:1, :], in_=b.ap[0:1, :]),
                             waits=[b_sum.ready] + rowA.barrier(), inc="act")
                b_sum.used(ta)
                rowA.produced(ta)
                for c in range(NCH):
                    xb = xq[c % 2]
                    ts_ = em.emit("scalar",
                                  lambda E, c=c, kg=kg, xb=xb: E.activation(
                                      out=xb.ap[:], in_=rsg(resid, c, kg), func=AF.Square),
                                  waits=RESID_CTOKS[c] + xb.barrier() + ARENA, inc="act")
                    xb.produced(ts_)
                    t = em.emit("tensor",
                                lambda E, xb=xb, b=b_sq, c=c: E.matmul(
                                    b.ap[:], ones128.ap[:], xb.ap[:],
                                    start=(c == 0), stop=(c == NCH - 1)),
                                waits=[ts_] + (b_sq.barrier() if c == 0 else []),
                                inc="pe")
                    xb.used(t)
                    anote(t)
                b_sq.produced(t)
                tb = em.emit("scalar", lambda E, b=b_sq: E.copy(out=rowB.ap[0:1, :], in_=b.ap[0:1, :]),
                             waits=[b_sq.ready] + rowB.barrier(), inc="act")
                b_sq.used(tb)
                rowB.produced(tb)
                t1 = em.emit("vector", lambda E: E.tensor_scalar_mul(rowA.ap[0:1, :], rowA.ap[0:1, :], 1.0 / D),
                             waits=[rowA.ready], inc="dve")
                rowA.produced(t1)
                t2 = em.emit("vector", lambda E: E.scalar_tensor_tensor(
                    out=rowT.ap[0:1, :], in0=rowA.ap[0:1, :], scalar=-1.0, in1=rowA.ap[0:1, :],
                    op0=ALU.mult, op1=ALU.mult), waits=[t1] + rowT.barrier(), inc="dve")
                rowT.produced(t2)
                rowA.used(t2)
                t3 = em.emit("vector", lambda E: E.scalar_tensor_tensor(
                    out=rowB.ap[0:1, :], in0=rowB.ap[0:1, :], scalar=1.0 / D, in1=rowT.ap[0:1, :],
                    op0=ALU.mult, op1=ALU.add), waits=[rowB.ready, t2], inc="dve")
                rowB.produced(t3)
                rowT.used(t3)
                t3b = em.emit("vector", lambda E: E.tensor_scalar_add(rowB.ap[0:1, :], rowB.ap[0:1, :], EPS),
                              waits=[t3], inc="dve")
                t4 = em.emit("scalar", lambda E: E.activation(
                    out=rowB.ap[0:1, :], in_=rowB.ap[0:1, :], func=AF.Sqrt, bias=0.0, scale=1.0),
                    waits=[t3b], inc="act")
                rowB.produced(t4)
                t5 = em.emit("vector", lambda E: E.reciprocal(rowT.ap[0:1, :], rowB.ap[0:1, :]),
                             waits=[t4] + rowT.barrier(), inc="dve")
                rowT.produced(t5)
                rowB.used(t5)
                t5b = em.emit("vector", lambda E: E.tensor_copy(out=rowC.ap[0:1, :], in_=rowT.ap[0:1, :]),
                              waits=[t5] + rowC.barrier(), inc="dve")
                rowC.produced(t5b)
                rowT.used(t5b)
                t6 = em.emit("vector", lambda E: E.tensor_mul(rowD.ap[0:1, :], rowA.ap[0:1, :], rowT.ap[0:1, :]),
                             waits=[t5, rowA.ready] + rowD.barrier(), inc="dve")
                rowD.produced(t6)
                rowA.used(t6)
                rowT.used(t6)
                tr = em.emit("tensor", lambda E, b=b_sum: E.matmul(
                    b.ap[:], ones1.ap[:], rowC.ap[0:1, :], start=True, stop=True),
                    waits=[t5b, ones1.ready] + b_sum.barrier(), inc="pe")
                b_sum.produced(tr)
                rowC.used(tr)
                te = em.emit("scalar", lambda E, b=b_sum, kg=kg: E.copy(
                    out=rbc.ap[:, kg * 512:(kg + 1) * 512], in_=b.ap[:]),
                    waits=([tr] + rbc.barrier() + ARENA) if kg == 0 else [tr], inc="act")
                b_sum.used(te)
                if kg == 3:
                    rbc.produced(te)
                else:
                    rbc.used(te)
                bmr = pb[2 + kg]
                tm = em.emit("tensor", lambda E, b=bmr: E.matmul(
                    b.ap[:], ones1.ap[:], rowD.ap[0:1, :], start=True, stop=True),
                    waits=[t6, ones1.ready] + bmr.barrier(), inc="pe")
                bmr.produced(tm)
                rowD.used(tm)
            last = None
            for c in range(NCH):
                tmu = em.emit("vector", lambda E, c=c: E.tensor_mul(rch(xnorm, c), rch(resid, c), rbc.ap[:]),
                              waits=RESID_CTOKS[c] + [rbc.ready] + (xnorm.barrier() if c == 0 else []),
                              inc="dve")
                rbc.used(tmu)
                anote(tmu)
                for kg in range(4):
                    last = em.emit("vector", lambda E, c=c, kg=kg: E.tensor_sub(
                        rsg(xnorm, c, kg), rsg(xnorm, c, kg), pb[2 + kg].ap[:]),
                        waits=[tmu, pb[2 + kg].ready], inc="dve")
            for kg in range(4):
                pb[2 + kg].used(last)
            xnorm.produced(last)
            return last

        tbs = [Buf(scrT[:, 1024:1536]), Buf(scrT[:, 1536:2048])]

        def emit_glu(g_aps, u_aps, d_aps, d_kc, gbuf_base, clip7, use_wbc, d_scale,
                     pre_d_hook=None):
            nm = len(g_aps)
            for mi in range(nm):
                gp = load_panel(**g_aps[mi])
                up = load_panel(**u_aps[mi])
                for tgh in range(2):
                    bg = [pb[0], pb[1]]
                    bu = [pb[2], pb[3]]
                    for kc in range(NCH):
                        for j in range(2):
                            tg = tgh * 2 + j
                            t = em.emit("tensor", lambda E, gp=gp, kc=kc, tg=tg, b=bg[j]: E.matmul(
                                b.ap[:], gp.ap[:, kc * 128:(kc + 1) * 128], rsg(xnorm, kc, tg),
                                start=(kc == 0), stop=(kc == NCH - 1)),
                                waits=([gp.ready, xnorm.ready] + (bg[j].barrier() if kc == 0 else [])),
                                inc="pe" if kc == NCH - 1 else None)
                            if kc == NCH - 1:
                                bg[j].produced(t)
                                gp.used(t)
                                xnorm.used(t)
                    for kc in range(NCH):
                        for j in range(2):
                            tg = tgh * 2 + j
                            t = em.emit("tensor", lambda E, up=up, kc=kc, tg=tg, b=bu[j]: E.matmul(
                                b.ap[:], up.ap[:, kc * 128:(kc + 1) * 128], rsg(xnorm, kc, tg),
                                start=(kc == 0), stop=(kc == NCH - 1)),
                                waits=([up.ready, xnorm.ready] + (bu[j].barrier() if kc == 0 else [])),
                                inc="pe" if kc == NCH - 1 else None)
                            if kc == NCH - 1:
                                bu[j].produced(t)
                                up.used(t)
                                xnorm.used(t)
                    for j in range(2):
                        tg = tgh * 2 + j
                        tpos = (gbuf_base + mi) * T + tg * 512
                        tb = tbs[j]
                        ts_ = em.emit("scalar", lambda E, b=bg[j], tb=tb: E.activation(
                            out=tb.ap[:], in_=b.ap[:], func=AF.Silu),
                            waits=[bg[j].ready] + tb.barrier() + ARENA, inc="act")
                        tb.produced(ts_)
                        bg[j].used(ts_)
                        if use_wbc is not None:
                            tu = em.emit("vector", lambda E, b=bu[j], o=tpos, tg=tg: E.tensor_mul(
                                bigT[:, o:o + 512], b.ap[:],
                                use_wbc.ap[:, tg * 512:(tg + 1) * 512]),
                                waits=[bu[j].ready, use_wbc.ready, emit_glu.prev_d] + ARENA, inc="dve")
                            use_wbc.used(tu)
                            anote(tu)
                            bu[j].used(tu)
                            td = em.emit("vector", lambda E, tb=tb, o=tpos: E.scalar_tensor_tensor(
                                out=bigT[:, o:o + 512], in0=tb.ap[:], scalar=7.0,
                                in1=bigT[:, o:o + 512], op0=ALU.min, op1=ALU.mult),
                                waits=[ts_, tu], inc="dve")
                        else:
                            td = em.emit("vector", lambda E, tb=tb, b=bu[j], o=tpos: E.tensor_mul(
                                bigT[:, o:o + 512], tb.ap[:], b.ap[:]),
                                waits=[ts_, bu[j].ready, emit_glu.prev_d] + ARENA, inc="dve")
                            bu[j].used(td)
                        tb.used(td)
                        emit_glu.gtoks.append(td)
            gtok = emit_glu.gtoks[-1]
            gdeps = list(emit_glu.gtoks)
            emit_glu.gtoks = []
            if pre_d_hook is not None:
                pre_d_hook()
            for dm2 in range(8):
                dp = load_panel(**d_aps[dm2])
                for sub in range(2):
                    dm = dm2 * 2 + sub
                    for tgh in range(2):
                        bd = [pb[2], pb[3]] if tgh == 0 else [pb[4], pb[5]]
                        for kc in range(d_kc):
                            for j in range(2):
                                tg = tgh * 2 + j
                                t = em.emit("tensor", lambda E, dp=dp, kc=kc, tg=tg, b=bd[j],
                                            gb=gbuf_base, sub=sub: E.matmul(
                                    b.ap[:], dp.ap[:, sub * 1024 + kc * 128: sub * 1024 + (kc + 1) * 128],
                                    bigT[:, (gb + kc) * T + tg * 512: (gb + kc) * T + tg * 512 + 512],
                                    start=(kc == 0), stop=(kc == d_kc - 1)),
                                    waits=([dp.ready] + gdeps + (bd[j].barrier() if kc == 0 else [])),
                                    inc="pe" if kc == d_kc - 1 else None)
                                if kc == d_kc - 1:
                                    bd[j].produced(t)
                                    dp.used(t)
                                    emit_glu.prev_d = t
                                    anote(t)
                        for j in range(2):
                            tg = tgh * 2 + j
                            te = em.emit("vector", lambda E, dm=dm, tg=tg, b=bd[j]: E.tensor_add(
                                rsg(resid, dm, tg), rsg(resid, dm, tg), b.ap[:]),
                                waits=[bd[j].ready], inc="dve")
                            bd[j].used(te)
                            emit_glu.rtoks.append(te)
                            _rchunk[dm].append(te)
        emit_glu.gtoks = []
        emit_glu.rtoks = []
        emit_glu.prev_d = None

        def emit_router():
            emit_router.mtoks = []
            ev_toks = []
            for tc in range(NCH):
                b = pb[0] if tc % 2 == 0 else pb[1]
                for kc in range(NCH):
                    t = em.emit("tensor", lambda E, tc=tc, kc=kc, b=b: E.matmul(
                        b.ap[:, 0:8], xnorm.ap[:, kc * T + tc * 128: kc * T + tc * 128 + 128],
                        wrB.ap[:, kc * 8:(kc + 1) * 8],
                        start=(kc == 0), stop=(kc == NCH - 1)),
                        waits=([xnorm.ready, wrB.ready] + (b.barrier() if kc == 0 else [])),
                        inc="pe" if kc == NCH - 1 else None)
                b.produced(t)
                xnorm.used(t)
                te = em.emit("scalar", lambda E, tc=tc, b=b: E.copy(
                    out=rtr.ap[:, tc * 8:(tc + 1) * 8], in_=b.ap[:, 0:8]),
                    waits=[b.ready], inc="act")
                b.used(te)
                rtr.used(te)
                ev_toks.append(te)
            for tc in range(NCH):
                lg = rtr.ap[:, tc * 8:(tc + 1) * 8]
                t = em.emit("vector", lambda E, lg=lg: E.reduce_max(stat.ap[:, 0:1], lg, axis=AX.X),
                            waits=[ev_toks[tc]] + stat.barrier(), inc="dve")
                stat.produced(t)
                t1 = em.emit("vector", lambda E, lg=lg: E.tensor_scalar(
                    out=rtw.ap[:, 0:8], in0=lg, scalar1=stat.ap[:, 0:1], scalar2=1.0,
                    op0=ALU.is_equal, op1=ALU.mult), waits=[t] + rtw.barrier(), inc="dve")
                t2 = em.emit("vector", lambda E, lg=lg: E.scalar_tensor_tensor(
                    out=rtw.ap[:, 8:16], in0=rtw.ap[:, 0:8], scalar=NEG, in1=lg,
                    op0=ALU.mult, op1=ALU.add), waits=[t1], inc="dve")
                t3 = em.emit("vector", lambda E: E.reduce_max(stat.ap[:, 1:2], rtw.ap[:, 8:16], axis=AX.X),
                             waits=[t2], inc="dve")
                t4 = em.emit("vector", lambda E: E.tensor_sub(stat.ap[:, 2:3], stat.ap[:, 0:1], stat.ap[:, 1:2]),
                             waits=[t3], inc="dve")
                t5 = em.emit("scalar", lambda E: E.activation(out=stat.ap[:, 3:4], in_=stat.ap[:, 2:3],
                                                             func=AF.Sigmoid), waits=[t4], inc="act")
                t6 = em.emit("vector", lambda E: E.tensor_scalar(
                    out=stat.ap[:, 4:5], in0=stat.ap[:, 3:4], scalar1=-1.0, scalar2=1.0,
                    op0=ALU.mult, op1=ALU.add), waits=[t5], inc="dve")
                t7 = em.emit("vector", lambda E: E.tensor_scalar(
                    out=rtw.ap[:, 8:16], in0=rtw.ap[:, 8:16], scalar1=stat.ap[:, 1:2], scalar2=1.0,
                    op0=ALU.is_equal, op1=ALU.mult), waits=[t6], inc="dve")
                last = None
                for e in range(NE):
                    ta = em.emit("vector", lambda E, e=e: E.tensor_scalar(
                        out=rtw.ap[:, 8 + e:9 + e], in0=rtw.ap[:, 8 + e:9 + e],
                        scalar1=stat.ap[:, 4:5], scalar2=1.0, op0=ALU.mult, op1=ALU.mult),
                        waits=[t7], inc="dve")
                    last = em.emit("vector", lambda E, e=e, tc=tc: E.scalar_tensor_tensor(
                        out=wtm.ap[:, tc * 8 + e: tc * 8 + e + 1], in0=rtw.ap[:, e:e + 1],
                        scalar=stat.ap[:, 3:4], in1=rtw.ap[:, 8 + e:9 + e],
                        op0=ALU.mult, op1=ALU.add), waits=[ta], inc="dve")
                rtw.produced(last)
                stat.used(last)
                emit_router.mtoks.append(last)
            wtm.produced(last)

        def emit_wbc(e):
            last = None
            for tc in range(NCH):
                pt = pTa if tc % 2 == 0 else pTb
                tt = em.emit("tensor", lambda E, e=e, tc=tc, pt=pt: E.transpose(
                    pt.ap[0:1, 0:128], wtm.ap[:, tc * 8 + e: tc * 8 + e + 1], identS.ap[:]),
                    waits=[emit_router.mtoks[tc], identS.ready] + ARENA + pt.barrier(), inc="pe")
                pt.produced(tt)
                tc_ = em.emit("scalar", lambda E, pt=pt: E.copy(out=wrow.ap[0:1, :], in_=pt.ap[0:1, 0:128]),
                              waits=[tt] + wrow.barrier(), inc="act")
                pt.used(tc_)
                wrow.produced(tc_)
                b = pb[0] if tc % 2 == 0 else pb[1]
                tm = em.emit("tensor", lambda E, b=b: E.matmul(
                    b.ap[:, 0:128], ones1.ap[:], wrow.ap[0:1, :], start=True, stop=True),
                    waits=[tc_, ones1.ready] + b.barrier(), inc="pe")
                b.produced(tm)
                wrow.used(tm)
                last = em.emit("scalar", lambda E, tc=tc, b=b: E.copy(
                    out=wbc.ap[:, tc * 128:(tc + 1) * 128], in_=b.ap[:, 0:128]),
                    waits=([tm] + wbc.barrier() + ARENA) if tc == 0 else [tm], inc="act")
                b.used(last)
            wbc.produced(last)

        def emit_moe():
            call = emit_moe.call
            emit_moe.call += 1
            cached = {0, 2} <= need
            emit_router()
            emit_wbc(0)
            for e in range(NE):
                for hh in range(2):
                    if cached and call > 0:
                        g_aps = [dict(hbm_ap=None, cols=D, cache_from=W["wg_c"][e, hh * 8 + mi])
                                 for mi in range(8)]
                        u_aps = [dict(hbm_ap=None, cols=D, cache_from=W["wu_c"][e, hh * 8 + mi])
                                 for mi in range(8)]
                        d_aps = [dict(hbm_ap=None, cols=2048,
                                      cache_from=W["wd_c"][e, hh, dm2 * 2:(dm2 + 1) * 2]
                                      .transpose([1, 0, 2]))
                                 for dm2 in range(8)]
                    else:
                        g_aps = [dict(hbm_ap=W["wg"][e, hh * 8 + mi], cols=D,
                                      cache_to=W["wg_c"][e, hh * 8 + mi] if cached else None)
                                 for mi in range(8)]
                        u_aps = [dict(hbm_ap=W["wu"][e, hh * 8 + mi], cols=D,
                                      cache_to=W["wu_c"][e, hh * 8 + mi] if cached else None)
                                 for mi in range(8)]
                        d_aps = [dict(hbm_ap=W["wd"][e, hh, dm2 * 2:(dm2 + 1) * 2]
                                      .transpose([1, 0, 2]), cols=2048, scale=0.5,
                                      cache_to=W["wd_c"][e, hh, dm2 * 2:(dm2 + 1) * 2]
                                      .transpose([1, 0, 2]) if cached else None)
                                 for dm2 in range(8)]
                    hook = (lambda e=e: emit_wbc(e + 1)) if (hh == 1 and e + 1 < NE) else None
                    emit_glu(g_aps, u_aps, d_aps, 8, 0, True, wbc, 0.5, pre_d_hook=hook)
        emit_moe.call = 0

        def emit_ffn1():
            call = emit_ffn1.call
            emit_ffn1.call += 1
            cached = {3, 5} <= need
            for grp in range(8):
                if cached and call > 0:
                    g_aps = [dict(hbm_ap=None, cols=D, cache_from=W["l1_wg_c"][grp * 8 + mi])
                             for mi in range(8)]
                    u_aps = [dict(hbm_ap=None, cols=D, cache_from=W["l1_wu_c"][grp * 8 + mi])
                             for mi in range(8)]
                    d_aps = [dict(hbm_ap=None, cols=2048,
                                  cache_from=W["l1_wd_c"][grp, dm2 * 2:(dm2 + 1) * 2]
                                  .transpose([1, 0, 2]))
                             for dm2 in range(8)]
                else:
                    g_aps = [dict(hbm_ap=W["l1_wg"][grp * 8 + mi], cols=D,
                                  cache_to=W["l1_wg_c"][grp * 8 + mi] if cached else None)
                             for mi in range(8)]
                    u_aps = [dict(hbm_ap=W["l1_wu"][grp * 8 + mi], cols=D,
                                  cache_to=W["l1_wu_c"][grp * 8 + mi] if cached else None)
                             for mi in range(8)]
                    d_aps = [dict(hbm_ap=W["l1_wd"][grp, dm2 * 2:(dm2 + 1) * 2]
                                  .transpose([1, 0, 2]), cols=2048, scale=0.5,
                                  cache_to=W["l1_wd_c"][grp, dm2 * 2:(dm2 + 1) * 2]
                                  .transpose([1, 0, 2]) if cached else None)
                             for dm2 in range(8)]
                emit_glu(g_aps, u_aps, d_aps, 8, 0, False, None, 0.5)
        emit_ffn1.call = 0

        bigB = [Buf(bigT[:, i * T:(i + 1) * T]) for i in range(8)]

        def proj_head(panel, dest, bias_col, pbanks):
            toks = []
            for tgh in range(2):
                for kc in range(NCH):
                    for j in range(2):
                        tg = tgh * 2 + j
                        b = pbanks[tgh * 2 + j] if len(pbanks) == 4 else pbanks[j]
                        t = em.emit("tensor", lambda E, panel=panel, kc=kc, tg=tg, b=b: E.matmul(
                            b.ap[:], panel.ap[:, kc * 128:(kc + 1) * 128], rsg(xnorm, kc, tg),
                            start=(kc == 0), stop=(kc == NCH - 1)),
                            waits=([panel.ready, xnorm.ready] + (b.barrier() if kc == 0 else [])),
                            inc="pe" if kc == NCH - 1 else None)
                        if kc == NCH - 1:
                            b.produced(t)
                            panel.used(t)
                            xnorm.used(t)
                for j in range(2):
                    tg = tgh * 2 + j
                    b = pbanks[tgh * 2 + j] if len(pbanks) == 4 else pbanks[j]
                    if bias_col is None:
                        te = em.emit("scalar", lambda E, b=b, tg=tg: E.copy(
                            out=dest.ap[:, tg * 512:(tg + 1) * 512], in_=b.ap[:]),
                            waits=[b.ready] + ARENA + (dest.barrier() if tg == 0 else []), inc="act")
                    else:
                        te = em.emit("scalar", lambda E, b=b, tg=tg: E.activation(
                            out=dest.ap[:, tg * 512:(tg + 1) * 512], in_=b.ap[:], func=AF.Identity,
                            bias=bias_col, scale=1.0),
                            waits=[b.ready, bqkS.ready] + ARENA + (dest.barrier() if tg == 0 else []), inc="act")
                    b.used(te)
                    anote(te)
                    toks.append(te)
            dest.produced(toks[-1])
            for t in toks[:-1]:
                dest.used(t)

        def emit_cbound(hh, srcs, probs_scr, cbanks):
            sq = bigB[3]
            mtoks = []
            for oi, srcb in enumerate(srcs):
                tq = em.emit("scalar", lambda E, s=srcb: E.activation(
                    out=sq.ap[:, 0:T], in_=s.ap[:, 0:T], func=AF.Square),
                    waits=[srcb.ready] + sq.barrier() + ARENA, inc="act")
                sq.produced(tq)
                srcb.used(tq)
                for kg in range(4):
                    b = cbanks[kg % 2]
                    t = em.emit("tensor", lambda E, kg=kg, b=b: E.matmul(
                        b.ap[:], ones128.ap[:], sq.ap[:, kg * 512:(kg + 1) * 512],
                        start=True, stop=True),
                        waits=[tq, ones128.ready] + b.barrier(), inc="pe")
                    b.produced(t)
                    sq.used(t)
                    ta = em.emit("vector", lambda E, b=b, c=oi * 4 + kg, p=probs_scr: E.tensor_scalar(
                        out=p.ap[:, 0:512], in0=b.ap[:], scalar1=1.0, scalar2=0.0,
                        op0=ALU.mult, op1=ALU.max, accum_out=nrm.ap[:, c:c + 1]),
                        waits=[t] + probs_scr.barrier(), inc="dve")
                    b.used(ta)
                    probs_scr.used(ta)
                    mtoks.append(ta)
            t = em.emit("vector", lambda E: E.tensor_max(nrm.ap[:, 0:4], nrm.ap[:, 0:4], nrm.ap[:, 4:8]),
                        waits=mtoks, inc="dve")
            t = em.emit("vector", lambda E: E.tensor_max(nrm.ap[:, 0:2], nrm.ap[:, 0:2], nrm.ap[:, 2:4]),
                        waits=[t], inc="dve")
            t = em.emit("vector", lambda E: E.tensor_mul(nrm.ap[:, 8:9], nrm.ap[:, 0:1], nrm.ap[:, 1:2]),
                        waits=[t], inc="dve")
            t = em.emit("scalar", lambda E: E.activation(out=nrm.ap[:, 9:10], in_=nrm.ap[:, 8:9],
                                                         func=AF.Sqrt), waits=[t], inc="act")
            t = em.emit("vector", lambda E, hh=hh: E.tensor_scalar_mul(
                bcol.ap[:, hh:hh + 1], nrm.ap[:, 9:10], -SM_SCALE), waits=[t], inc="dve")
            return t

        def emit_attn0():
            q_h, k_h, v_h = bigB[0], bigB[1], bigB[2]
            prbs0 = [bigB[5], Buf(auxT[:, 0:T])]
            rrep = Buf(scrT[:, 0:128])
            btoks = [None, None]

            def proj_qk0(h):
                qp = load_panel(W["l0_wqkv"][h], D)
                proj_head(qp, q_h, None, [pb[0], pb[1]])
                kp = load_panel(W["l0_wqkv"][16 + h], D)
                proj_head(kp, k_h, None, [pb[0], pb[1]])

            def proj_v0(h):
                vp = load_panel(W["l0_wqkv"][32 + h], D)
                proj_head(vp, v_h, None, [pb[0], pb[1]])

            proj_qk0(0)
            btoks[0] = emit_cbound(0, (q_h, k_h), prbs0[0], [pb[2], pb[3]])
            proj_v0(0)
            for h in range(NH):

                def do_sexp(qc):
                    probs = prbs0[qc % 2]
                    sp = 6 * (qc % 2)
                    sbanks = [pb[0], pb[1]] if qc % 2 == 0 else [pTaF, pTbF]
                    extoks = []
                    for kg in range(4):
                        b = sbanks[kg % 2]
                        t = em.emit("tensor", lambda E, qc=qc, kg=kg, b=b: E.matmul(
                            b.ap[:], q_h.ap[:, qc * 128:(qc + 1) * 128],
                            k_h.ap[:, kg * 512:(kg + 1) * 512], start=True, stop=True),
                            waits=[q_h.ready, k_h.ready] + b.barrier(), inc="pe")
                        b.produced(t)
                        q_h.used(t)
                        k_h.used(t)
                        anote(t)
                        tex = em.emit("scalar", lambda E, kg=kg, b=b, probs=probs, sp=sp, hh=h % 2:
                                      E.activation(
                            out=probs.ap[:, kg * 512:(kg + 1) * 512], in_=b.ap[:], func=AF.Exp,
                            bias=bcol.ap[:, hh:hh + 1], scale=SM_SCALE,
                            accum_out=stat.ap[:, sp + kg:sp + kg + 1]),
                            waits=[t, btoks[h % 2]] + ARENA + (probs.barrier() if kg == 0 else []),
                            inc="act")
                        b.used(tex)
                        anote(tex)
                        extoks.append(tex)
                    probs.produced(extoks[-1])
                    for t in extoks[:-1]:
                        probs.used(t)

                do_sexp(0)
                for qc in range(NCH):
                    probs = prbs0[qc % 2]
                    sp = 6 * (qc % 2)
                    tr1 = em.emit("vector", lambda E, sp=sp: E.tensor_scalar(
                        out=nrm.ap[:, 8:12], in0=stat.ap[:, sp:sp + 4], scalar1=1.0, scalar2=0.0,
                        op0=ALU.mult, op1=ALU.add, accum_out=stat.ap[:, sp + 4:sp + 5]),
                        waits=probs.barrier() + [probs.ready], inc="dve")
                    tr2 = em.emit("vector", lambda E, sp=sp: E.reciprocal(
                        stat.ap[:, sp + 5:sp + 6], stat.ap[:, sp + 4:sp + 5]),
                        waits=[tr1], inc="dve")
                    t6 = em.emit("vector", lambda E, r=rrep, sp=sp: E.tensor_scalar_mul(
                        r.ap, ones128.ap[:], stat.ap[:, sp + 5:sp + 6]),
                        waits=[tr2] + rrep.barrier(), inc="dve")
                    rrep.produced(t6)
                    if qc + 1 < NCH:
                        do_sexp(qc + 1)
                    for kg in range(4):
                        b = pb[2 + kg]
                        t = em.emit("tensor", lambda E, kg=kg, qc=qc, b=b, r=rrep, probs=probs: E.matmul(
                            b.ap[:], r.ap, probs.ap[:, kg * 512:(kg + 1) * 512],
                            start=(qc == 0), stop=(qc == NCH - 1)),
                            waits=([t6, probs.ready] + (b.barrier() if qc == 0 else [])),
                            inc="pe")
                        if qc == NCH - 1:
                            b.produced(t)
                        rrep.used(t)
                        probs.used(t)
                        anote(t)
                if h + 1 < NH:
                    proj_qk0(h + 1)
                vtoks = []
                for kg in range(4):
                    b = pb[2 + kg]
                    tv = em.emit("vector", lambda E, kg=kg, b=b: E.tensor_mul(
                        v_h.ap[:, kg * 512:(kg + 1) * 512], v_h.ap[:, kg * 512:(kg + 1) * 512], b.ap[:]),
                        waits=[b.ready, v_h.ready], inc="dve")
                    b.used(tv)
                    v_h.used(tv)
                    anote(tv)
                    vtoks.append(tv)
                td = em.dma(W["vals"][h], v_h.ap[:, 0:T], "dV", waits=vtoks, queue="gpsimd")
                v_h.used(td)
                anote(td)
                em.dv_toks.append(td)
                if h + 1 < NH:
                    proj_v0(h + 1)
                    btoks[(h + 1) % 2] = emit_cbound((h + 1) % 2, (q_h, k_h), prbs0[0],
                                                     [pb[2], pb[3]])
            out_proj(W["l0_wo"], None)

        def out_proj(panels_ap, bias_cols):
            xbar = xnorm.barrier()
            pre = [load_panel(panels_ap[0], D), load_panel(panels_ap[1], D)]
            vload = []
            for c in range(NCH):
                t = em.dma(rch(xnorm, c), W["vals"][c], "dVL",
                           waits=xbar + [em.dv_toks[c]], queue="gpsimd")
                vload.append(t)
            xnorm.produced(vload[-1])
            for t in vload[:-1]:
                xnorm.used(t)
            for dm in range(16):
                op = pre[dm] if dm < 2 else load_panel(panels_ap[dm], D)
                for tgh in range(2):
                    bd = [pb[2], pb[3]] if tgh == 0 else [pb[4], pb[5]]
                    for kc in range(NCH):
                        for j in range(2):
                            tg = tgh * 2 + j
                            t = em.emit("tensor", lambda E, op=op, kc=kc, tg=tg, b=bd[j]: E.matmul(
                                b.ap[:], op.ap[:, kc * 128:(kc + 1) * 128], rsg(xnorm, kc, tg),
                                start=(kc == 0), stop=(kc == NCH - 1)),
                                waits=([op.ready, vload[kc]] + (bd[j].barrier() if kc == 0 else [])),
                                inc="pe" if kc == NCH - 1 else None)
                            if kc == NCH - 1:
                                bd[j].produced(t)
                                op.used(t)
                                xnorm.used(t)
                    for j in range(2):
                        tg = tgh * 2 + j
                        if bias_cols is None:
                            te = em.emit("vector", lambda E, dm=dm, tg=tg, b=bd[j]: E.tensor_add(
                                rsg(resid, dm, tg), rsg(resid, dm, tg), b.ap[:]),
                                waits=[bd[j].ready], inc="dve")
                        else:
                            te = em.emit("vector", lambda E, dm=dm, tg=tg, b=bd[j]: E.scalar_tensor_tensor(
                                out=rsg(resid, dm, tg), in0=b.ap[:], scalar=bias_cols[:, dm:dm + 1],
                                in1=rsg(resid, dm, tg), op0=ALU.add, op1=ALU.add),
                                waits=[bd[j].ready, boutS.ready], inc="dve")
                        bd[j].used(te)
                        emit_glu.rtoks.append(te)
                        _rchunk[dm].append(te)

        def emit_attn1():
            q_h, k_h, vtok = bigB[0], bigB[1], bigB[2]
            prbs = [bigB[5], bigB[7]]
            vals = bigB[6]
            btoks = [None, None]

            def proj_qk1(h):
                qp = load_panel(W["l1_wqkv"][h], D)
                proj_head(qp, q_h, bqkS.ap[:, h:h + 1], [pb[0], pb[1], pb[2], pb[5]])
                kp = load_panel(W["l1_wqkv"][16 + h], D)
                proj_head(kp, k_h, bqkS.ap[:, 16 + h:17 + h], [pb[0], pb[1], pb[2], pb[5]])

            def proj_v1(h):
                vp = load_panel(W["l1_wqkv"][32 + h], D)
                vtoks = []
                for tc in range(NCH):
                    b = pb[0] if tc % 2 == 0 else pb[1]
                    for kc in range(NCH):
                        t = em.emit("tensor", lambda E, vp=vp, tc=tc, kc=kc, b=b: E.matmul(
                            b.ap[:, 0:128], xnorm.ap[:, kc * T + tc * 128: kc * T + tc * 128 + 128],
                            vp.ap[:, kc * 128:(kc + 1) * 128],
                            start=(kc == 0), stop=(kc == NCH - 1)),
                            waits=([vp.ready, xnorm.ready] + (b.barrier() if kc == 0 else [])),
                            inc="pe" if kc == NCH - 1 else None)
                    b.produced(t)
                    vp.used(t)
                    xnorm.used(t)
                    te = em.emit("scalar", lambda E, tc=tc, b=b: E.copy(
                        out=vtok.ap[:, tc * 128:(tc + 1) * 128], in_=b.ap[:, 0:128]),
                        waits=[b.ready] + ARENA + (vtok.barrier() if tc == 0 else []), inc="act")
                    b.used(te)
                    anote(te)
                    vtoks.append(te)
                vtok.produced(vtoks[-1])
                for t in vtoks[:-1]:
                    vtok.used(t)

            proj_qk1(0)
            btoks[0] = emit_cbound(0, (q_h, k_h), prbs[0], [pb[3], pb[4]])
            proj_v1(0)
            for h in range(NH):

                def do_sexp1(qc):
                    probs = prbs[qc % 2]
                    sp = 6 * (qc % 2)
                    L = (qc + 1) * 128
                    nkg = (L + 511) // 512
                    sbanks = [pb[0], pb[1]] if qc % 2 == 0 else [pb[2], pb[5]]
                    extoks = []
                    for kg in range(nkg):
                        w = min(512, L - kg * 512)
                        b = sbanks[kg % 2]
                        t = em.emit("tensor", lambda E, qc=qc, kg=kg, w=w, b=b: E.matmul(
                            b.ap[:, 0:w], q_h.ap[:, qc * 128:(qc + 1) * 128],
                            k_h.ap[:, kg * 512: kg * 512 + w], start=True, stop=True),
                            waits=[q_h.ready, k_h.ready] + b.barrier(), inc="pe")
                        b.produced(t)
                        q_h.used(t)
                        k_h.used(t)
                        anote(t)
                        if kg < nkg - 1:
                            tex = em.emit("scalar", lambda E, kg=kg, b=b, probs=probs, sp=sp,
                                          hh=h % 2: E.activation(
                                out=probs.ap[:, kg * 512:(kg + 1) * 512], in_=b.ap[:], func=AF.Exp,
                                bias=bcol.ap[:, hh:hh + 1], scale=SM_SCALE,
                                accum_out=stat.ap[:, sp + kg:sp + kg + 1]),
                                waits=[t, btoks[h % 2]] + ARENA + (probs.barrier() if kg == 0 else []),
                                inc="act")
                        else:
                            tex = em.emit("scalar", lambda E, kg=kg, w=w, b=b, probs=probs,
                                          hh=h % 2: E.activation(
                                out=probs.ap[:, kg * 512: kg * 512 + w], in_=b.ap[:, 0:w], func=AF.Exp,
                                bias=bcol.ap[:, hh:hh + 1], scale=SM_SCALE),
                                waits=[t, btoks[h % 2]] + ARENA + (probs.barrier() if kg == 0 else []),
                                inc="act")
                        b.used(tex)
                        anote(tex)
                        extoks.append(tex)
                    tm = em.emit("vector", lambda E, qc=qc, probs=probs: E.tensor_mul(
                        probs.ap[:, qc * 128:(qc + 1) * 128], probs.ap[:, qc * 128:(qc + 1) * 128],
                        triS.ap[:]), waits=[extoks[-1], triS.ready], inc="dve")
                    tps = em.emit("vector", lambda E, qc=qc, probs=probs, sp=sp, L=L, nkg=nkg:
                                  E.tensor_scalar(
                        out=bigB[3].ap[:, 0:(L - (nkg - 1) * 512)],
                        in0=probs.ap[:, (nkg - 1) * 512: L], scalar1=1.0, scalar2=0.0,
                        op0=ALU.mult, op1=ALU.add, accum_out=stat.ap[:, sp + 3:sp + 4]),
                        waits=[tm] + bigB[3].barrier(), inc="dve")
                    bigB[3].used(tps)
                    tr = tps
                    for kg in range(nkg - 1):
                        tr = em.emit("vector", lambda E, sp=sp, kg=kg: E.tensor_add(
                            stat.ap[:, sp + 3:sp + 4], stat.ap[:, sp + 3:sp + 4],
                            stat.ap[:, sp + kg:sp + kg + 1]),
                            waits=[tr, extoks[kg]], inc="dve")
                    tr2 = em.emit("vector", lambda E, sp=sp: E.reciprocal(
                        stat.ap[:, sp + 5:sp + 6], stat.ap[:, sp + 3:sp + 4]),
                        waits=[tr], inc="dve")
                    trep = em.emit("vector", lambda E, sp=sp, par=qc % 2: E.tensor_scalar_mul(
                        scrT[:, 1280 + par * 128: 1408 + par * 128], ones128.ap[:],
                        stat.ap[:, sp + 5:sp + 6]),
                        waits=[tr2, ones128.ready] + ARENA, inc="dve")
                    probs.produced(tm)
                    return tm, trep

                tns = [None] * NCH
                tns[0] = do_sexp1(0)
                for qc in range(NCH):
                    probs = prbs[qc % 2]
                    tn, trep = tns[qc]
                    if qc + 1 < NCH:
                        tns[qc + 1] = do_sexp1(qc + 1)
                    bav = pb[3] if qc % 2 == 0 else pb[4]
                    cptoks = [None] * (qc + 1)

                    def emit_T(kc, probs=probs, tn=tn):
                        pt = pTa if kc % 2 == 0 else pTb
                        tt = em.emit("tensor", lambda E, kc=kc, pt=pt: E.transpose(
                            pt.ap[:, 0:128], probs.ap[:, kc * 128:(kc + 1) * 128], identS.ap[:]),
                            waits=[tn, identS.ready] + ARENA + pt.barrier(), inc="pe")
                        pt.produced(tt)
                        probs.used(tt)
                        ps = pTs[kc % 4]
                        tc_ = em.emit("scalar", lambda E, pt=pt, ps=ps: E.copy(
                            out=ps.ap[:], in_=pt.ap[:, 0:128]),
                            waits=[tt] + ARENA + ps.barrier(), inc="act")
                        pt.used(tc_)
                        ps.produced(tc_)
                        cptoks[kc] = tc_

                    emit_T(0)
                    if qc >= 1:
                        emit_T(1)
                    for kc in range(qc + 1):
                        if kc + 2 <= qc:
                            emit_T(kc + 2)
                        ps = pTs[kc % 4]
                        t = em.emit("tensor", lambda E, kc=kc, qc=qc, b=bav, ps=ps: E.matmul(
                            b.ap[:, 0:128], vtok.ap[:, kc * 128:(kc + 1) * 128], ps.ap[:],
                            start=(kc == 0), stop=(kc == qc)),
                            waits=([cptoks[kc], vtok.ready] + (bav.barrier() if kc == 0 else [])),
                            inc="pe")
                        ps.used(t)
                        vtok.used(t)
                        anote(t)
                    bav.produced(t)
                    ptr = pTb if qc % 2 == 0 else pTa
                    ttr = em.emit("tensor", lambda E, ptr=ptr, par=qc % 2: E.transpose(
                        ptr.ap[:, 0:128], scrT[:, 1280 + par * 128: 1408 + par * 128],
                        identS.ap[:]),
                        waits=[trep, identS.ready] + ptr.barrier(), inc="pe")
                    ptr.produced(ttr)
                    tcr = em.emit("scalar", lambda E, ptr=ptr, par=qc % 2: E.copy(
                        out=scrT[:, 1536 + par * 128: 1664 + par * 128], in_=ptr.ap[:, 0:128]),
                        waits=[ttr] + ARENA, inc="act")
                    ptr.used(tcr)
                    te1 = em.emit("vector", lambda E, qc=qc, b=bav, par=qc % 2: E.tensor_mul(
                        vals.ap[:, qc * 128:(qc + 1) * 128], b.ap[:, 0:128],
                        scrT[:, 1536 + par * 128: 1664 + par * 128]),
                        waits=[bav.ready, tcr] + (vals.barrier() if qc == 0 else []), inc="dve")
                    te = em.emit("vector", lambda E, qc=qc, h=h: E.tensor_scalar_add(
                        vals.ap[:, qc * 128:(qc + 1) * 128], vals.ap[:, qc * 128:(qc + 1) * 128],
                        bqkS.ap[:, 32 + h:33 + h]),
                        waits=[te1, bqkS.ready], inc="dve")
                    bav.used(te1)
                    vals.used(te)
                    anote(te)
                if h + 1 < NH:
                    proj_qk1(h + 1)
                td = em.dma(W["vals"][h], vals.ap[:, 0:T], "dV", waits=vals.barrier(), queue="gpsimd")
                vals.produced(None)
                vals.used(td)
                anote(td)
                em.dv_toks.append(td)
                if h + 1 < NH:
                    proj_v1(h + 1)
                    btoks[(h + 1) % 2] = emit_cbound((h + 1) % 2, (q_h, k_h), prbs[0],
                                                     [pb[3], pb[4]])
            out_proj(W["l1_wout"], boutS.ap)

        tap_done = False
        for si, sub in enumerate(sublayers):
            arena_flush()
            emit_ln()
            if tap == "big" and sub == 1:
                em.dv_toks = []
                emit_attn0()
                col = {}
                for s_, v_ in _anotes:
                    col[s_] = max(col.get(s_, 0), v_)
                bw = list(col.items())
                for c in range(8):
                    em.dma(y_out[c], bigT[:, c * T:(c + 1) * T], "dO", waits=bw)
                em.emit("sync", lambda E: E.nop(), waits=[("dO", 16 * 8)])
                tap_done = True
                break
            if tap == "rbc":
                for kg in range(4):
                    em.emit("scalar", lambda E, kg=kg: E.copy(
                        out=rch(xnorm, 0)[:, kg * 512:(kg + 1) * 512], in_=pb[2 + kg].ap[:]),
                        waits=[pb[2 + kg].ready], inc="act")
                tmp = em.emit("scalar", lambda E: E.nop(), waits=[], inc="act")
                em.dma(y_out[0], rbc.ap[:], "dO", waits=[rbc.ready])
                em.dma(y_out[1], rch(xnorm, 0), "dO", waits=[("act", em.cnt["act"])])
                em.emit("sync", lambda E: E.nop(), waits=[("dO", 32)])
                tap_done = True
                break
            if tap == "xnorm":
                for c in range(NCH):
                    em.dma(y_out[c], rch(xnorm, c), "dO", waits=[xnorm.ready])
                em.emit("sync", lambda E: E.nop(), waits=[("dO", 16 * NCH)])
                tap_done = True
                break
            if tap in ("logits", "wtm") and sub in (0, 2):
                emit_router()
                if tap == "logits":
                    tw = [rtr.users[i] for i in range(NCH)]
                    em.dma(y_out[0][:, 0:128], rtr.ap[:], "dO", waits=tw)
                else:
                    em.dma(y_out[0][:, 0:128].rearrange("p t -> p t"), wtm.ap[:], "dO",
                           waits=[wtm.ready])
                em.emit("sync", lambda E: E.nop(), waits=[("dO", 16)])
                tap_done = True
                break
            if tap == "wbc0" and sub in (0, 2):
                emit_router()
                emit_wbc(0)
                em.dma(y_out[0], wbc.ap[:], "dO", waits=[wbc.ready])
                em.emit("sync", lambda E: E.nop(), waits=[("dO", 16)])
                tap_done = True
                break
            if sub in (0, 2):
                emit_moe()
            elif sub == 1:
                em.dv_toks = []
                emit_attn0()
            elif sub == 3 or sub == 5:
                emit_ffn1()
            elif sub == 4:
                em.dv_toks = []
                emit_attn1()
            RESID_TOKS[:] = [t for t in emit_glu.rtoks]
            emit_glu.rtoks = []
            col = {}
            for s_, v_ in RESID_TOKS:
                col[s_] = max(col.get(s_, 0), v_)
            RESID_TOKS[:] = list(col.items())
            for c in range(NCH):
                cc = {}
                for s_, v_ in _rchunk[c]:
                    cc[s_] = max(cc.get(s_, 0), v_)
                RESID_CTOKS[c] = list(cc.items()) if cc else list(RESID_TOKS)
                _rchunk[c] = []
            xnorm.produced(None)
            xnorm.users = []

        if tap_done:
            block = ctx.enter_context(nc.Block())
            for eng in ENGINES:
                ops = em.ops[eng]
                if not ops:
                    continue

                def body(E, ops=ops):
                    for op in ops:
                        op(E)
                getattr(block, eng)(body)
            return nc
        st_toks = []
        for c in range(NCH):
            if last:
                lb = wland[c % 2]
                tc_ = em.emit("gpsimd", lambda E, c=c, lb=lb: E.tensor_copy(
                    out=lb.ap[:, 0:T], in_=rch(resid, c)),
                    waits=resid_wait() + lb.barrier(), inc="pool")
                lb.produced(tc_)
                td = em.dma(y_out[c], lb.ap[:, 0:T], "dO", waits=[tc_])
                lb.used(td)
            else:
                td = em.dma(y_out[c], rch(resid, c), "dO", waits=resid_wait())
            st_toks.append(td)
        em.emit("sync", lambda E: E.nop(), waits=[("dO", 16 * NCH)])

        block = ctx.enter_context(nc.Block())
        for eng in ENGINES:
            ops = em.ops[eng]
            if not ops:
                continue

            def body(E, ops=ops):
                for op in ops:
                    op(E)
            getattr(block, eng)(body)

    return nc



GROUPS = [[0, 1, 2, 3, 4, 5]]

_cache = {}


def _get_prog(key, subs, first, last):
    if key not in _cache:
        _cache[key] = build_program(subs, first, last)
    return _cache[key]


def kernel(**inputs):
    h = prep_inputs(inputs)
    shared = {
        "wr": h["wr"], "wg": h["wg"], "wu": h["wu"], "wd": h["wd"],
        "l0_wqkv": h["l0_wqkv"], "l0_wo": h["l0_wo"],
        "l1_wqkv": h["l1_wqkv"], "l1_wout": h["l1_wout"],
        "bqk": h["bqk"], "bout": h["bout"], "tri": h["tri"],
        "l1_wg": h["l1_wg"], "l1_wu": h["l1_wu"], "l1_wd": h["l1_wd"],
        "ident": h["ident"],
    }

    def in_map_for(subs, first, core, cur):
        need = set(subs)
        m = {}
        m["x"] = h["x"][core] if first else cur[core]
        if need & {0, 2}:
            for k in ("wr", "wg", "wu", "wd"):
                m[k] = shared[k]
        if 1 in need:
            m["l0_wqkv"] = shared["l0_wqkv"]
            m["l0_wo"] = shared["l0_wo"]
        if 4 in need:
            for k in ("l1_wqkv", "l1_wout", "bqk", "bout", "tri"):
                m[k] = shared[k]
        if need & {3, 5}:
            for k in ("l1_wg", "l1_wu", "l1_wd"):
                m[k] = shared[k]
        m["ident"] = shared["ident"]
        return m

    cur = None
    for gi, subs in enumerate(GROUPS):
        first = gi == 0
        last = gi == len(GROUPS) - 1
        nc = _get_prog((tuple(subs), first, last), subs, first, last)
        in_maps = [in_map_for(subs, first, c, cur) for c in range(N_CORES)]
        res = run_bass_kernel_spmd(nc, in_maps, core_ids=list(range(N_CORES)))
        cur = [res.results[c]["y"] for c in range(N_CORES)]

    out = np.empty((N_CORES, T, D), np.float32)
    for c in range(N_CORES):
        out[c] = cur[c].reshape(D, T).T
    return out


# version 21
# speedup vs baseline: 1.0531x; 1.0123x over previous
import math
import numpy as np
import ml_dtypes
from contextlib import ExitStack

import concourse.bass as bass
import concourse.mybir as mybir
from concourse.bass_utils import run_bass_kernel_spmd

F32 = mybir.dt.float32
BF16 = mybir.dt.bfloat16
AF = mybir.ActivationFunctionType
ALU = mybir.AluOpType
AX = mybir.AxisListType
BF = ml_dtypes.bfloat16

N_CORES = 8
D = 2048
T = 2048
NCH = 16
NH = 16
NE = 8
F1 = 8192
EPS = 1e-5
SM_SCALE = 1.0 / math.sqrt(128.0)
NEG = -30000.0

ENGINES = ("sync", "tensor", "vector", "scalar", "gpsimd")
DEBUG = {}
SUBLAYERS = (0, 1, 2, 3, 4, 5)


def _panels(W):
    K, M = W.shape
    return np.ascontiguousarray(
        W.reshape(K // 128, 128, M // 128, 128).transpose(2, 1, 0, 3).reshape(M // 128, 128, K))


def prep_inputs(inp):
    h = {}
    x = np.asarray(inp["L0_attn_x"], np.float32)
    h["x"] = [np.ascontiguousarray(x[c].T).reshape(NCH, 128, T) for c in range(N_CORES)]
    h["l0_wqkv"] = _panels(np.asarray(inp["L0_attn_w_qkv"], np.float32))
    h["l0_wo"] = _panels(np.asarray(inp["L0_attn_w_o"], np.float32))
    h["wr"] = np.ascontiguousarray(
        np.asarray(inp["L0_ffn_Wr"], np.float32).reshape(NCH, 128, NE).transpose(1, 0, 2)).reshape(128, NCH * NE)
    for nm, key in (("wg", "L0_ffn_Wg"), ("wu", "L0_ffn_Wu"), ("wd", "L0_ffn_Wd")):
        h[nm] = np.ascontiguousarray(np.stack(
            [_panels(np.asarray(inp[key], np.float32)[e]) for e in range(NE)]))
    h["wd"] = np.ascontiguousarray(
        h["wd"].reshape(NE, 16, 128, 2, 1024).transpose(0, 3, 1, 2, 4))
    h["l1_wqkv"] = _panels(np.ascontiguousarray(np.asarray(inp["L1_attn_w_qkv"], np.float32).T))
    h["l1_wout"] = _panels(np.ascontiguousarray(np.asarray(inp["L1_attn_w_out"], np.float32).T))
    h["l1_wg"] = _panels(np.ascontiguousarray(np.asarray(inp["L1_ffn_w_gate"], np.float32).T))
    h["l1_wu"] = _panels(np.ascontiguousarray(np.asarray(inp["L1_ffn_w_up"], np.float32).T))
    l1wd = _panels(np.ascontiguousarray(np.asarray(inp["L1_ffn_w_down"], np.float32).T))
    h["l1_wd"] = np.ascontiguousarray(l1wd.reshape(16, 128, 8, 1024).transpose(2, 0, 1, 3))
    h["bqk"] = np.ascontiguousarray(np.asarray(inp["L1_attn_b_qkv"], np.float32).reshape(48, 128).T)
    h["bout"] = np.ascontiguousarray(np.asarray(inp["L1_attn_b_out"], np.float32).reshape(16, 128).T)
    h["ident"] = np.eye(128, dtype=np.float32).astype(BF)
    h["tri"] = np.tril(np.ones((128, 128), np.float32)).astype(BF)
    return h



class Em:
    def __init__(self):
        self.ops = {e: [] for e in ENGINES}
        self.cnt = {}
        self.hw = {}
        self.sems = {}
        self.dv_toks = []

    def _wait(self, eng, tok):
        if tok is None:
            return
        sem, val = tok
        if val <= 0 or val <= self.hw.get((eng, sem), 0):
            return
        self.hw[(eng, sem)] = val

        def cl(E, s=sem, v=val):
            E.wait_ge(self.sems[s], v)
        self.ops[eng].append(cl)

    def emit(self, eng, fn, waits=(), inc=None, n=1):
        for t in waits:
            self._wait(eng, t)
        if inc is None:
            self.ops[eng].append(fn)
            return None
        self.cnt[inc] = self.cnt.get(inc, 0) + n
        tgt = self.cnt[inc]

        def cl(E, fn=fn, s=inc, k=n):
            fn(E).then_inc(self.sems[s], k)
        self.ops[eng].append(cl)
        return (inc, tgt)

    def dma(self, out, in_, sem, waits=(), queue="sync"):
        return self.emit(queue, lambda E, o=out, i=in_: E.dma_start(out=o, in_=i),
                         waits=waits, inc=sem, n=16)


class Buf:
    __slots__ = ("ap", "ready", "users")

    def __init__(self, ap):
        self.ap = ap
        self.ready = None
        self.users = []

    def barrier(self):
        toks = {}
        for t in self.users + ([self.ready] if self.ready else []):
            if t and t[1] > toks.get(t[0], -1):
                toks[t[0]] = t[1]
        return [(s, v) for s, v in toks.items()]

    def produced(self, tok):
        self.ready = tok
        self.users = []

    def used(self, tok):
        if tok is not None:
            self.users.append(tok)



def build_program(sublayers, first, last, tap=None):
    nc = bass.Bass("TRN2", target_bir_lowering=False)
    em = Em()
    need = set(sublayers)

    x_in = nc.dram_tensor("x", [NCH, 128, T], F32 if first else BF16, kind="ExternalInput").ap()
    y_out = nc.dram_tensor("y", [NCH, 128, T], F32 if last else BF16, kind="ExternalOutput").ap()

    W = {}
    if need & {0, 2}:
        W["wr"] = nc.dram_tensor("wr", [128, NCH * NE], F32, kind="ExternalInput").ap()
        for nm in ("wg", "wu"):
            W[nm] = nc.dram_tensor(nm, [NE, 16, 128, D], F32, kind="ExternalInput").ap()
        W["wd"] = nc.dram_tensor("wd", [NE, 2, 16, 128, 1024], F32, kind="ExternalInput").ap()
    if 1 in need:
        W["l0_wqkv"] = nc.dram_tensor("l0_wqkv", [48, 128, D], F32, kind="ExternalInput").ap()
        W["l0_wo"] = nc.dram_tensor("l0_wo", [16, 128, D], F32, kind="ExternalInput").ap()
    if 4 in need:
        W["l1_wqkv"] = nc.dram_tensor("l1_wqkv", [48, 128, D], F32, kind="ExternalInput").ap()
        W["l1_wout"] = nc.dram_tensor("l1_wout", [16, 128, D], F32, kind="ExternalInput").ap()
        W["bqk"] = nc.dram_tensor("bqk", [128, 48], F32, kind="ExternalInput").ap()
        W["bout"] = nc.dram_tensor("bout", [128, 16], F32, kind="ExternalInput").ap()
        W["tri"] = nc.dram_tensor("tri", [128, 128], BF16, kind="ExternalInput").ap()
    if need & {3, 5}:
        W["l1_wg"] = nc.dram_tensor("l1_wg", [64, 128, D], F32, kind="ExternalInput").ap()
        W["l1_wu"] = nc.dram_tensor("l1_wu", [64, 128, D], F32, kind="ExternalInput").ap()
        W["l1_wd"] = nc.dram_tensor("l1_wd", [8, 16, 128, 1024], F32, kind="ExternalInput").ap()
    W["ident"] = nc.dram_tensor("ident", [128, 128], BF16, kind="ExternalInput").ap()
    if {0, 2} <= need:
        for nm in ("wg", "wu"):
            W[nm + "_c"] = nc.dram_tensor(nm + "_c", [NE, 16, 128, D], BF16, kind="Internal").ap()
        W["wd_c"] = nc.dram_tensor("wd_c", [NE, 2, 16, 128, 1024], BF16, kind="Internal").ap()
    if {3, 5} <= need:
        W["l1_wg_c"] = nc.dram_tensor("l1_wg_c", [64, 128, D], BF16, kind="Internal").ap()
        W["l1_wu_c"] = nc.dram_tensor("l1_wu_c", [64, 128, D], BF16, kind="Internal").ap()
        W["l1_wd_c"] = nc.dram_tensor("l1_wd_c", [8, 16, 128, 1024], BF16, kind="Internal").ap()
    if need & {1, 4}:
        W["vals"] = nc.dram_tensor("vals", [NCH, 128, T], BF16, kind="Internal").ap()

    ctx = ExitStack()
    with ctx:
        def S(name, shape, dt):
            return Buf(ctx.enter_context(nc.sbuf_tensor(name, shape, dt)).ap())

        resid = S("resid", [128, NCH * T], BF16)
        xnorm = S("xnorm", [128, NCH * T], BF16)
        bigT = ctx.enter_context(nc.sbuf_tensor("big", [128, 16384], BF16)).ap()
        wland = [S(f"wland{i}", [128, 2048], F32) for i in range(2)]
        wbf = [S(f"wbf{i}", [128, 2048], BF16) for i in range(2)]
        scrT = ctx.enter_context(nc.sbuf_tensor("scr", [128, 2048], BF16)).ap()
        auxT = ctx.enter_context(nc.sbuf_tensor("aux", [128, 4096], BF16)).ap()
        rbc = Buf(auxT[:, 0:T])
        wbc = Buf(auxT[:, T:2 * T]) if need & {0, 2} else None
        aux_f32 = auxT.bitcast(F32)
        rowA = S("rowA", [1, 512], F32)
        rowB = S("rowB", [1, 512], F32)
        rowT = S("rowT", [1, 512], F32)
        rowC = S("rowC", [1, 512], BF16)
        rowD = S("rowD", [1, 512], BF16)
        wrow = S("wrow", [1, 128], BF16)
        ones1 = S("ones1", [1, 128], BF16)
        ones128 = S("ones128", [128, 128], BF16)
        stat = S("stat", [128, 12], F32)
        nrm = S("nrm", [128, 12], F32)
        bcol = S("bcol", [128, 2], F32)
        rtr = S("rtr", [128, 128], F32) if need & {0, 2} else None
        rtw = S("rtw", [128, 16], F32) if need & {0, 2} else None
        wtm = S("wtm", [128, 128], BF16) if need & {0, 2} else None
        wrB = S("wrB", [128, NCH * NE], BF16) if need & {0, 2} else None
        identS = S("identS", [128, 128], BF16)
        triS = S("triS", [128, 128], BF16) if 4 in need else None
        bqkS = S("bqkS", [128, 48], F32) if 4 in need else None
        boutS = S("boutS", [128, 16], F32) if 4 in need else None
        pTs = ([S(f"pTs{i}", [128, 128], BF16) for i in range(2)]
               + [Buf(scrT[:, 1024:1152]), Buf(scrT[:, 1152:1280])]) if 4 in need else None

        pb = [Buf(ctx.enter_context(nc.psum_tensor(f"pb{i}", [128, 512], F32)).ap())
              for i in range(6)]
        pTa = Buf(ctx.enter_context(nc.psum_tensor("pTa", [128, 1024], BF16)).ap())
        pTb = Buf(ctx.enter_context(nc.psum_tensor("pTb", [128, 1024], BF16)).ap())
        pTaF = Buf(pTa.ap.bitcast(F32))
        pTbF = Buf(pTb.ap.bitcast(F32))

        for s in ("dL", "dV", "dVL", "dO", "dC", "pe", "act", "dve", "pool"):
            em.sems[s] = ctx.enter_context(nc.semaphore(s))

        def rch(buf, c):
            return buf.ap[:, c * T:(c + 1) * T]

        def rsg(buf, c, tg, w=512):
            return buf.ap[:, c * T + tg * w: c * T + (tg + 1) * w]

        ARENA = []
        _anotes = []
        RESID_CTOKS = [[] for _ in range(NCH)]
        _rchunk = [[] for _ in range(NCH)]

        def anote(tok):
            if tok is not None:
                _anotes.append(tok)
            return tok

        def arena_flush():
            col = {}
            for s_, v_ in _anotes:
                col[s_] = max(col.get(s_, 0), v_)
            ARENA[:] = list(col.items())
            _anotes.clear()

        tok0 = em.emit("vector", lambda E: E.memset(ones1.ap[:], 1.0), inc="dve")
        ones1.produced(tok0)
        tok0 = em.emit("vector", lambda E: E.memset(ones128.ap[:], 1.0), inc="dve")
        ones128.produced(tok0)
        identS.produced(em.dma(identS.ap[:], W["ident"], "dL"))
        if 4 in need:
            triS.produced(em.dma(triS.ap[:], W["tri"], "dL"))
            bqkS.produced(em.dma(bqkS.ap[:], W["bqk"], "dL"))
            boutS.produced(em.dma(boutS.ap[:], W["bout"], "dL"))
        if need & {0, 2}:
            wrS = S("wrS", [128, NCH * NE], F32)
            wrS.produced(em.dma(wrS.ap[:], W["wr"], "dL"))
            t = em.emit("gpsimd", lambda E: E.tensor_copy(out=wrB.ap[:], in_=wrS.ap[:]),
                        waits=[wrS.ready], inc="pool")
            wrB.produced(t)

        in_toks = []
        bigF = bigT.bitcast(F32)
        for c in range(NCH):
            if first:
                off = (c % 4) * T
                tl = em.dma(bigF[:, off:off + T], x_in[c], "dL",
                            waits=([in_toks[c - 4]] if c >= 4 else []))
                tc_ = em.emit("gpsimd",
                              lambda E, c=c, off=off: E.tensor_copy(
                                  out=rch(resid, c), in_=bigF[:, off:off + T]),
                              waits=[tl], inc="pool")
                in_toks.append(tc_)
            else:
                in_toks.append(em.dma(rch(resid, c), x_in[c], "dL"))
        resid.produced(in_toks[-1])
        for t in in_toks[:-1]:
            resid.used(t)
        resid_ready = resid.barrier()
        resid.users = []
        resid.ready = None
        RESID_TOKS = list(resid_ready)
        for c in range(NCH):
            RESID_CTOKS[c] = [in_toks[c]]
        if first:
            for t in in_toks:
                anote(t)

        def resid_wait():
            return list(RESID_TOKS)

        _pl = {"i": 0}

        def load_panel(hbm_ap, cols, scale=None, cache_to=None, cache_from=None):
            bb = wbf[_pl["i"] % 2]
            if cache_from is not None:
                _pl["i"] += 1
                tl = em.dma(bb.ap[:, 0:cols], cache_from, "dL",
                            waits=bb.barrier() + [("dC", em.cnt.get("dC", 0))])
                bb.produced(tl)
                return bb
            lb = wland[_pl["i"] % 2]
            _pl["i"] += 1
            tl = em.dma(lb.ap[:, 0:cols], hbm_ap, "dL", waits=lb.barrier())
            lb.produced(tl)
            if scale is None:
                if _pl["i"] % 2 == 0:
                    fn = lambda E, lb=lb, bb=bb, n=cols: E.tensor_copy(out=bb.ap[:, 0:n], in_=lb.ap[:, 0:n])
                    tc_ = em.emit("gpsimd", fn, waits=[tl] + bb.barrier(), inc="pool")
                else:
                    fn = lambda E, lb=lb, bb=bb, n=cols: E.activation(
                        out=bb.ap[:, 0:n], in_=lb.ap[:, 0:n], func=AF.Identity, bias=0.0, scale=1.0)
                    tc_ = em.emit("scalar", fn, waits=[tl] + bb.barrier(), inc="act")
            else:
                fn = lambda E, lb=lb, bb=bb, n=cols, s=scale: E.activation(
                    out=bb.ap[:, 0:n], in_=lb.ap[:, 0:n], func=AF.Identity, bias=0.0, scale=s)
                tc_ = em.emit("scalar", fn, waits=[tl] + bb.barrier(), inc="act")
            lb.used(tc_)
            bb.produced(tc_)
            if cache_to is not None:
                tw = em.dma(cache_to, bb.ap[:, 0:cols], "dC", waits=[tc_])
                bb.used(tw)
            return bb

        def emit_ln():
            xq = [Buf(scrT[:, 0:512]), Buf(scrT[:, 512:1024])]
            for kg in range(4):
                b_sum, b_sq = pb[0], pb[1]
                for c in range(NCH):
                    t = em.emit("tensor",
                                lambda E, c=c, kg=kg, b=b_sum: E.matmul(
                                    b.ap[:], ones128.ap[:], rsg(resid, c, kg),
                                    start=(c == 0), stop=(c == NCH - 1)),
                                waits=(RESID_CTOKS[c] + [ones128.ready] + (b_sum.barrier() if c == 0 else [])),
                                inc="pe" if c == NCH - 1 else None)
                b_sum.produced(t)
                ta = em.emit("scalar", lambda E, b=b_sum: E.copy(out=rowA.ap[0:1, :], in_=b.ap[0:1, :]),
                             waits=[b_sum.ready] + rowA.barrier(), inc="act")
                b_sum.used(ta)
                rowA.produced(ta)
                for c in range(NCH):
                    xb = xq[c % 2]
                    ts_ = em.emit("scalar",
                                  lambda E, c=c, kg=kg, xb=xb: E.activation(
                                      out=xb.ap[:], in_=rsg(resid, c, kg), func=AF.Square),
                                  waits=RESID_CTOKS[c] + xb.barrier() + ARENA, inc="act")
                    xb.produced(ts_)
                    t = em.emit("tensor",
                                lambda E, xb=xb, b=b_sq, c=c: E.matmul(
                                    b.ap[:], ones128.ap[:], xb.ap[:],
                                    start=(c == 0), stop=(c == NCH - 1)),
                                waits=[ts_] + (b_sq.barrier() if c == 0 else []),
                                inc="pe")
                    xb.used(t)
                    anote(t)
                b_sq.produced(t)
                tb = em.emit("scalar", lambda E, b=b_sq: E.copy(out=rowB.ap[0:1, :], in_=b.ap[0:1, :]),
                             waits=[b_sq.ready] + rowB.barrier(), inc="act")
                b_sq.used(tb)
                rowB.produced(tb)
                t1 = em.emit("vector", lambda E: E.tensor_scalar_mul(rowA.ap[0:1, :], rowA.ap[0:1, :], 1.0 / D),
                             waits=[rowA.ready], inc="dve")
                rowA.produced(t1)
                t2 = em.emit("vector", lambda E: E.scalar_tensor_tensor(
                    out=rowT.ap[0:1, :], in0=rowA.ap[0:1, :], scalar=-1.0, in1=rowA.ap[0:1, :],
                    op0=ALU.mult, op1=ALU.mult), waits=[t1] + rowT.barrier(), inc="dve")
                rowT.produced(t2)
                rowA.used(t2)
                t3 = em.emit("vector", lambda E: E.scalar_tensor_tensor(
                    out=rowB.ap[0:1, :], in0=rowB.ap[0:1, :], scalar=1.0 / D, in1=rowT.ap[0:1, :],
                    op0=ALU.mult, op1=ALU.add), waits=[rowB.ready, t2], inc="dve")
                rowB.produced(t3)
                rowT.used(t3)
                t3b = em.emit("vector", lambda E: E.tensor_scalar_add(rowB.ap[0:1, :], rowB.ap[0:1, :], EPS),
                              waits=[t3], inc="dve")
                t4 = em.emit("scalar", lambda E: E.activation(
                    out=rowB.ap[0:1, :], in_=rowB.ap[0:1, :], func=AF.Sqrt, bias=0.0, scale=1.0),
                    waits=[t3b], inc="act")
                rowB.produced(t4)
                t5 = em.emit("vector", lambda E: E.reciprocal(rowT.ap[0:1, :], rowB.ap[0:1, :]),
                             waits=[t4] + rowT.barrier(), inc="dve")
                rowT.produced(t5)
                rowB.used(t5)
                t5b = em.emit("vector", lambda E: E.tensor_copy(out=rowC.ap[0:1, :], in_=rowT.ap[0:1, :]),
                              waits=[t5] + rowC.barrier(), inc="dve")
                rowC.produced(t5b)
                rowT.used(t5b)
                t6 = em.emit("vector", lambda E: E.tensor_mul(rowD.ap[0:1, :], rowA.ap[0:1, :], rowT.ap[0:1, :]),
                             waits=[t5, rowA.ready] + rowD.barrier(), inc="dve")
                rowD.produced(t6)
                rowA.used(t6)
                rowT.used(t6)
                tr = em.emit("tensor", lambda E, b=b_sum: E.matmul(
                    b.ap[:], ones1.ap[:], rowC.ap[0:1, :], start=True, stop=True),
                    waits=[t5b, ones1.ready] + b_sum.barrier(), inc="pe")
                b_sum.produced(tr)
                rowC.used(tr)
                te = em.emit("scalar", lambda E, b=b_sum, kg=kg: E.copy(
                    out=rbc.ap[:, kg * 512:(kg + 1) * 512], in_=b.ap[:]),
                    waits=([tr] + rbc.barrier() + ARENA) if kg == 0 else [tr], inc="act")
                b_sum.used(te)
                if kg == 3:
                    rbc.produced(te)
                else:
                    rbc.used(te)
                bmr = pb[2 + kg]
                tm = em.emit("tensor", lambda E, b=bmr: E.matmul(
                    b.ap[:], ones1.ap[:], rowD.ap[0:1, :], start=True, stop=True),
                    waits=[t6, ones1.ready] + bmr.barrier(), inc="pe")
                bmr.produced(tm)
                rowD.used(tm)
            last = None
            for c in range(NCH):
                tmu = em.emit("vector", lambda E, c=c: E.tensor_mul(rch(xnorm, c), rch(resid, c), rbc.ap[:]),
                              waits=RESID_CTOKS[c] + [rbc.ready] + (xnorm.barrier() if c == 0 else []),
                              inc="dve")
                rbc.used(tmu)
                anote(tmu)
                for kg in range(4):
                    last = em.emit("vector", lambda E, c=c, kg=kg: E.tensor_sub(
                        rsg(xnorm, c, kg), rsg(xnorm, c, kg), pb[2 + kg].ap[:]),
                        waits=[tmu, pb[2 + kg].ready], inc="dve")
            for kg in range(4):
                pb[2 + kg].used(last)
            xnorm.produced(last)
            return last

        tbs = [Buf(scrT[:, 1024:1536]), Buf(scrT[:, 1536:2048])]

        def emit_glu(g_aps, u_aps, d_aps, d_kc, gbuf_base, clip7, use_wbc, d_scale,
                     pre_d_hook=None):
            nm = len(g_aps)
            for mi in range(nm):
                gp = load_panel(**g_aps[mi])
                up = load_panel(**u_aps[mi])
                for tgh in range(2):
                    bg = [pb[0], pb[1]]
                    bu = [pb[2], pb[3]]
                    for kc in range(NCH):
                        for j in range(2):
                            tg = tgh * 2 + j
                            t = em.emit("tensor", lambda E, gp=gp, kc=kc, tg=tg, b=bg[j]: E.matmul(
                                b.ap[:], gp.ap[:, kc * 128:(kc + 1) * 128], rsg(xnorm, kc, tg),
                                start=(kc == 0), stop=(kc == NCH - 1)),
                                waits=([gp.ready, xnorm.ready] + (bg[j].barrier() if kc == 0 else [])),
                                inc="pe" if kc == NCH - 1 else None)
                            if kc == NCH - 1:
                                bg[j].produced(t)
                                gp.used(t)
                                xnorm.used(t)
                    for kc in range(NCH):
                        for j in range(2):
                            tg = tgh * 2 + j
                            t = em.emit("tensor", lambda E, up=up, kc=kc, tg=tg, b=bu[j]: E.matmul(
                                b.ap[:], up.ap[:, kc * 128:(kc + 1) * 128], rsg(xnorm, kc, tg),
                                start=(kc == 0), stop=(kc == NCH - 1)),
                                waits=([up.ready, xnorm.ready] + (bu[j].barrier() if kc == 0 else [])),
                                inc="pe" if kc == NCH - 1 else None)
                            if kc == NCH - 1:
                                bu[j].produced(t)
                                up.used(t)
                                xnorm.used(t)
                    for j in range(2):
                        tg = tgh * 2 + j
                        tpos = (gbuf_base + mi) * T + tg * 512
                        tb = tbs[j]
                        ts_ = em.emit("scalar", lambda E, b=bg[j], tb=tb: E.activation(
                            out=tb.ap[:], in_=b.ap[:], func=AF.Silu),
                            waits=[bg[j].ready] + tb.barrier() + ARENA, inc="act")
                        tb.produced(ts_)
                        bg[j].used(ts_)
                        if use_wbc is not None:
                            tu = em.emit("vector", lambda E, b=bu[j], o=tpos, tg=tg: E.tensor_mul(
                                bigT[:, o:o + 512], b.ap[:],
                                use_wbc.ap[:, tg * 512:(tg + 1) * 512]),
                                waits=[bu[j].ready, use_wbc.ready, emit_glu.prev_d] + ARENA, inc="dve")
                            use_wbc.used(tu)
                            anote(tu)
                            bu[j].used(tu)
                            td = em.emit("vector", lambda E, tb=tb, o=tpos: E.scalar_tensor_tensor(
                                out=bigT[:, o:o + 512], in0=tb.ap[:], scalar=7.0,
                                in1=bigT[:, o:o + 512], op0=ALU.min, op1=ALU.mult),
                                waits=[ts_, tu], inc="dve")
                        else:
                            td = em.emit("vector", lambda E, tb=tb, b=bu[j], o=tpos: E.tensor_mul(
                                bigT[:, o:o + 512], tb.ap[:], b.ap[:]),
                                waits=[ts_, bu[j].ready, emit_glu.prev_d] + ARENA, inc="dve")
                            bu[j].used(td)
                        tb.used(td)
                        emit_glu.gtoks.append(td)
            gtok = emit_glu.gtoks[-1]
            gdeps = list(emit_glu.gtoks)
            emit_glu.gtoks = []
            if pre_d_hook is not None:
                pre_d_hook()
            for dm2 in range(8):
                dp = load_panel(**d_aps[dm2])
                for sub in range(2):
                    dm = dm2 * 2 + sub
                    for tgh in range(2):
                        bd = [pb[2], pb[3]] if tgh == 0 else [pb[4], pb[5]]
                        for kc in range(d_kc):
                            for j in range(2):
                                tg = tgh * 2 + j
                                t = em.emit("tensor", lambda E, dp=dp, kc=kc, tg=tg, b=bd[j],
                                            gb=gbuf_base, sub=sub: E.matmul(
                                    b.ap[:], dp.ap[:, sub * 1024 + kc * 128: sub * 1024 + (kc + 1) * 128],
                                    bigT[:, (gb + kc) * T + tg * 512: (gb + kc) * T + tg * 512 + 512],
                                    start=(kc == 0), stop=(kc == d_kc - 1)),
                                    waits=([dp.ready] + gdeps + (bd[j].barrier() if kc == 0 else [])),
                                    inc="pe" if kc == d_kc - 1 else None)
                                if kc == d_kc - 1:
                                    bd[j].produced(t)
                                    dp.used(t)
                                    emit_glu.prev_d = t
                                    anote(t)
                        for j in range(2):
                            tg = tgh * 2 + j
                            te = em.emit("vector", lambda E, dm=dm, tg=tg, b=bd[j]: E.tensor_add(
                                rsg(resid, dm, tg), rsg(resid, dm, tg), b.ap[:]),
                                waits=[bd[j].ready], inc="dve")
                            bd[j].used(te)
                            emit_glu.rtoks.append(te)
                            _rchunk[dm].append(te)
        emit_glu.gtoks = []
        emit_glu.rtoks = []
        emit_glu.prev_d = None

        def emit_router():
            emit_router.mtoks = []
            ev_toks = []
            for tc in range(NCH):
                b = pb[0] if tc % 2 == 0 else pb[1]
                for kc in range(NCH):
                    t = em.emit("tensor", lambda E, tc=tc, kc=kc, b=b: E.matmul(
                        b.ap[:, 0:8], xnorm.ap[:, kc * T + tc * 128: kc * T + tc * 128 + 128],
                        wrB.ap[:, kc * 8:(kc + 1) * 8],
                        start=(kc == 0), stop=(kc == NCH - 1)),
                        waits=([xnorm.ready, wrB.ready] + (b.barrier() if kc == 0 else [])),
                        inc="pe" if kc == NCH - 1 else None)
                b.produced(t)
                xnorm.used(t)
                te = em.emit("scalar", lambda E, tc=tc, b=b: E.copy(
                    out=rtr.ap[:, tc * 8:(tc + 1) * 8], in_=b.ap[:, 0:8]),
                    waits=[b.ready], inc="act")
                b.used(te)
                rtr.used(te)
                ev_toks.append(te)
            for tc in range(NCH):
                lg = rtr.ap[:, tc * 8:(tc + 1) * 8]
                t = em.emit("vector", lambda E, lg=lg: E.reduce_max(stat.ap[:, 0:1], lg, axis=AX.X),
                            waits=[ev_toks[tc]] + stat.barrier(), inc="dve")
                stat.produced(t)
                t1 = em.emit("vector", lambda E, lg=lg: E.tensor_scalar(
                    out=rtw.ap[:, 0:8], in0=lg, scalar1=stat.ap[:, 0:1], scalar2=1.0,
                    op0=ALU.is_equal, op1=ALU.mult), waits=[t] + rtw.barrier(), inc="dve")
                t2 = em.emit("vector", lambda E, lg=lg: E.scalar_tensor_tensor(
                    out=rtw.ap[:, 8:16], in0=rtw.ap[:, 0:8], scalar=NEG, in1=lg,
                    op0=ALU.mult, op1=ALU.add), waits=[t1], inc="dve")
                t3 = em.emit("vector", lambda E: E.reduce_max(stat.ap[:, 1:2], rtw.ap[:, 8:16], axis=AX.X),
                             waits=[t2], inc="dve")
                t4 = em.emit("vector", lambda E: E.tensor_sub(stat.ap[:, 2:3], stat.ap[:, 0:1], stat.ap[:, 1:2]),
                             waits=[t3], inc="dve")
                t5 = em.emit("scalar", lambda E: E.activation(out=stat.ap[:, 3:4], in_=stat.ap[:, 2:3],
                                                             func=AF.Sigmoid), waits=[t4], inc="act")
                t6 = em.emit("vector", lambda E: E.tensor_scalar(
                    out=stat.ap[:, 4:5], in0=stat.ap[:, 3:4], scalar1=-1.0, scalar2=1.0,
                    op0=ALU.mult, op1=ALU.add), waits=[t5], inc="dve")
                t7 = em.emit("vector", lambda E: E.tensor_scalar(
                    out=rtw.ap[:, 8:16], in0=rtw.ap[:, 8:16], scalar1=stat.ap[:, 1:2], scalar2=1.0,
                    op0=ALU.is_equal, op1=ALU.mult), waits=[t6], inc="dve")
                last = None
                for e in range(NE):
                    ta = em.emit("vector", lambda E, e=e: E.tensor_scalar(
                        out=rtw.ap[:, 8 + e:9 + e], in0=rtw.ap[:, 8 + e:9 + e],
                        scalar1=stat.ap[:, 4:5], scalar2=1.0, op0=ALU.mult, op1=ALU.mult),
                        waits=[t7], inc="dve")
                    last = em.emit("vector", lambda E, e=e, tc=tc: E.scalar_tensor_tensor(
                        out=wtm.ap[:, tc * 8 + e: tc * 8 + e + 1], in0=rtw.ap[:, e:e + 1],
                        scalar=stat.ap[:, 3:4], in1=rtw.ap[:, 8 + e:9 + e],
                        op0=ALU.mult, op1=ALU.add), waits=[ta], inc="dve")
                rtw.produced(last)
                stat.used(last)
                emit_router.mtoks.append(last)
            wtm.produced(last)

        def emit_wbc(e):
            last = None
            for tc in range(NCH):
                pt = pTa if tc % 2 == 0 else pTb
                tt = em.emit("tensor", lambda E, e=e, tc=tc, pt=pt: E.transpose(
                    pt.ap[0:1, 0:128], wtm.ap[:, tc * 8 + e: tc * 8 + e + 1], identS.ap[:]),
                    waits=[emit_router.mtoks[tc], identS.ready] + ARENA + pt.barrier(), inc="pe")
                pt.produced(tt)
                tc_ = em.emit("scalar", lambda E, pt=pt: E.copy(out=wrow.ap[0:1, :], in_=pt.ap[0:1, 0:128]),
                              waits=[tt] + wrow.barrier(), inc="act")
                pt.used(tc_)
                wrow.produced(tc_)
                b = pb[0] if tc % 2 == 0 else pb[1]
                tm = em.emit("tensor", lambda E, b=b: E.matmul(
                    b.ap[:, 0:128], ones1.ap[:], wrow.ap[0:1, :], start=True, stop=True),
                    waits=[tc_, ones1.ready] + b.barrier(), inc="pe")
                b.produced(tm)
                wrow.used(tm)
                last = em.emit("scalar", lambda E, tc=tc, b=b: E.copy(
                    out=wbc.ap[:, tc * 128:(tc + 1) * 128], in_=b.ap[:, 0:128]),
                    waits=([tm] + wbc.barrier() + ARENA) if tc == 0 else [tm], inc="act")
                b.used(last)
            wbc.produced(last)

        def emit_moe():
            call = emit_moe.call
            emit_moe.call += 1
            cached = {0, 2} <= need
            emit_router()
            emit_wbc(0)
            for e in range(NE):
                for hh in range(2):
                    if cached and call > 0:
                        g_aps = [dict(hbm_ap=None, cols=D, cache_from=W["wg_c"][e, hh * 8 + mi])
                                 for mi in range(8)]
                        u_aps = [dict(hbm_ap=None, cols=D, cache_from=W["wu_c"][e, hh * 8 + mi])
                                 for mi in range(8)]
                        d_aps = [dict(hbm_ap=None, cols=2048,
                                      cache_from=W["wd_c"][e, hh, dm2 * 2:(dm2 + 1) * 2]
                                      .transpose([1, 0, 2]))
                                 for dm2 in range(8)]
                    else:
                        g_aps = [dict(hbm_ap=W["wg"][e, hh * 8 + mi], cols=D,
                                      cache_to=W["wg_c"][e, hh * 8 + mi] if cached else None)
                                 for mi in range(8)]
                        u_aps = [dict(hbm_ap=W["wu"][e, hh * 8 + mi], cols=D,
                                      cache_to=W["wu_c"][e, hh * 8 + mi] if cached else None)
                                 for mi in range(8)]
                        d_aps = [dict(hbm_ap=W["wd"][e, hh, dm2 * 2:(dm2 + 1) * 2]
                                      .transpose([1, 0, 2]), cols=2048, scale=0.5,
                                      cache_to=W["wd_c"][e, hh, dm2 * 2:(dm2 + 1) * 2]
                                      .transpose([1, 0, 2]) if cached else None)
                                 for dm2 in range(8)]
                    hook = (lambda e=e: emit_wbc(e + 1)) if (hh == 1 and e + 1 < NE) else None
                    emit_glu(g_aps, u_aps, d_aps, 8, 0, True, wbc, 0.5, pre_d_hook=hook)
        emit_moe.call = 0

        def emit_ffn1():
            call = emit_ffn1.call
            emit_ffn1.call += 1
            cached = {3, 5} <= need
            for grp in range(8):
                if cached and call > 0:
                    g_aps = [dict(hbm_ap=None, cols=D, cache_from=W["l1_wg_c"][grp * 8 + mi])
                             for mi in range(8)]
                    u_aps = [dict(hbm_ap=None, cols=D, cache_from=W["l1_wu_c"][grp * 8 + mi])
                             for mi in range(8)]
                    d_aps = [dict(hbm_ap=None, cols=2048,
                                  cache_from=W["l1_wd_c"][grp, dm2 * 2:(dm2 + 1) * 2]
                                  .transpose([1, 0, 2]))
                             for dm2 in range(8)]
                else:
                    g_aps = [dict(hbm_ap=W["l1_wg"][grp * 8 + mi], cols=D,
                                  cache_to=W["l1_wg_c"][grp * 8 + mi] if cached else None)
                             for mi in range(8)]
                    u_aps = [dict(hbm_ap=W["l1_wu"][grp * 8 + mi], cols=D,
                                  cache_to=W["l1_wu_c"][grp * 8 + mi] if cached else None)
                             for mi in range(8)]
                    d_aps = [dict(hbm_ap=W["l1_wd"][grp, dm2 * 2:(dm2 + 1) * 2]
                                  .transpose([1, 0, 2]), cols=2048, scale=0.5,
                                  cache_to=W["l1_wd_c"][grp, dm2 * 2:(dm2 + 1) * 2]
                                  .transpose([1, 0, 2]) if cached else None)
                             for dm2 in range(8)]
                emit_glu(g_aps, u_aps, d_aps, 8, 0, False, None, 0.5)
        emit_ffn1.call = 0

        bigB = [Buf(bigT[:, i * T:(i + 1) * T]) for i in range(8)]

        def proj_head(panel, dest, bias_col, pbanks):
            toks = []
            for tgh in range(2):
                for kc in range(NCH):
                    for j in range(2):
                        tg = tgh * 2 + j
                        b = pbanks[tgh * 2 + j] if len(pbanks) == 4 else pbanks[j]
                        t = em.emit("tensor", lambda E, panel=panel, kc=kc, tg=tg, b=b: E.matmul(
                            b.ap[:], panel.ap[:, kc * 128:(kc + 1) * 128], rsg(xnorm, kc, tg),
                            start=(kc == 0), stop=(kc == NCH - 1)),
                            waits=([panel.ready, xnorm.ready] + (b.barrier() if kc == 0 else [])),
                            inc="pe" if kc == NCH - 1 else None)
                        if kc == NCH - 1:
                            b.produced(t)
                            panel.used(t)
                            xnorm.used(t)
                for j in range(2):
                    tg = tgh * 2 + j
                    b = pbanks[tgh * 2 + j] if len(pbanks) == 4 else pbanks[j]
                    if bias_col is None:
                        te = em.emit("scalar", lambda E, b=b, tg=tg: E.copy(
                            out=dest.ap[:, tg * 512:(tg + 1) * 512], in_=b.ap[:]),
                            waits=[b.ready] + ARENA + (dest.barrier() if tg == 0 else []), inc="act")
                    else:
                        te = em.emit("scalar", lambda E, b=b, tg=tg: E.activation(
                            out=dest.ap[:, tg * 512:(tg + 1) * 512], in_=b.ap[:], func=AF.Identity,
                            bias=bias_col, scale=1.0),
                            waits=[b.ready, bqkS.ready] + ARENA + (dest.barrier() if tg == 0 else []), inc="act")
                    b.used(te)
                    anote(te)
                    toks.append(te)
            dest.produced(toks[-1])
            for t in toks[:-1]:
                dest.used(t)

        def emit_cbound(hh, srcs, probs_scr, cbanks):
            sq = bigB[3]
            mtoks = []
            for oi, srcb in enumerate(srcs):
                tq = em.emit("scalar", lambda E, s=srcb: E.activation(
                    out=sq.ap[:, 0:T], in_=s.ap[:, 0:T], func=AF.Square),
                    waits=[srcb.ready] + sq.barrier() + ARENA, inc="act")
                sq.produced(tq)
                srcb.used(tq)
                for kg in range(4):
                    b = cbanks[kg % 2]
                    t = em.emit("tensor", lambda E, kg=kg, b=b: E.matmul(
                        b.ap[:], ones128.ap[:], sq.ap[:, kg * 512:(kg + 1) * 512],
                        start=True, stop=True),
                        waits=[tq, ones128.ready] + b.barrier(), inc="pe")
                    b.produced(t)
                    sq.used(t)
                    ta = em.emit("vector", lambda E, b=b, c=oi * 4 + kg, p=probs_scr: E.tensor_scalar(
                        out=p.ap[:, 0:512], in0=b.ap[:], scalar1=1.0, scalar2=0.0,
                        op0=ALU.mult, op1=ALU.max, accum_out=nrm.ap[:, c:c + 1]),
                        waits=[t] + probs_scr.barrier(), inc="dve")
                    b.used(ta)
                    probs_scr.used(ta)
                    mtoks.append(ta)
            t = em.emit("vector", lambda E: E.tensor_max(nrm.ap[:, 0:4], nrm.ap[:, 0:4], nrm.ap[:, 4:8]),
                        waits=mtoks, inc="dve")
            t = em.emit("vector", lambda E: E.tensor_max(nrm.ap[:, 0:2], nrm.ap[:, 0:2], nrm.ap[:, 2:4]),
                        waits=[t], inc="dve")
            t = em.emit("vector", lambda E: E.tensor_mul(nrm.ap[:, 8:9], nrm.ap[:, 0:1], nrm.ap[:, 1:2]),
                        waits=[t], inc="dve")
            t = em.emit("scalar", lambda E: E.activation(out=nrm.ap[:, 9:10], in_=nrm.ap[:, 8:9],
                                                         func=AF.Sqrt), waits=[t], inc="act")
            t = em.emit("vector", lambda E, hh=hh: E.tensor_scalar_mul(
                bcol.ap[:, hh:hh + 1], nrm.ap[:, 9:10], -SM_SCALE), waits=[t], inc="dve")
            return t

        def emit_attn0():
            q_h, k_h, v_h = bigB[0], bigB[1], bigB[2]
            prbs0 = [bigB[5], Buf(auxT[:, 0:T])]
            rrep = Buf(scrT[:, 0:128])
            btoks = [None, None]

            def proj_qk0(h):
                qp = load_panel(W["l0_wqkv"][h], D)
                proj_head(qp, q_h, None, [pb[0], pb[1]])
                kp = load_panel(W["l0_wqkv"][16 + h], D)
                proj_head(kp, k_h, None, [pb[0], pb[1]])

            def proj_v0(h):
                vp = load_panel(W["l0_wqkv"][32 + h], D)
                proj_head(vp, v_h, None, [pb[0], pb[1]])

            proj_qk0(0)
            btoks[0] = emit_cbound(0, (q_h, k_h), prbs0[0], [pb[2], pb[3]])
            proj_v0(0)
            for h in range(NH):

                def do_sexp(qc):
                    probs = prbs0[qc % 2]
                    sp = 6 * (qc % 2)
                    sbanks = [pb[0], pb[1]] if qc % 2 == 0 else [pTaF, pTbF]
                    extoks = []
                    for kg in range(4):
                        b = sbanks[kg % 2]
                        t = em.emit("tensor", lambda E, qc=qc, kg=kg, b=b: E.matmul(
                            b.ap[:], q_h.ap[:, qc * 128:(qc + 1) * 128],
                            k_h.ap[:, kg * 512:(kg + 1) * 512], start=True, stop=True),
                            waits=[q_h.ready, k_h.ready] + b.barrier(), inc="pe")
                        b.produced(t)
                        q_h.used(t)
                        k_h.used(t)
                        anote(t)
                        tex = em.emit("scalar", lambda E, kg=kg, b=b, probs=probs, sp=sp, hh=h % 2:
                                      E.activation(
                            out=probs.ap[:, kg * 512:(kg + 1) * 512], in_=b.ap[:], func=AF.Exp,
                            bias=bcol.ap[:, hh:hh + 1], scale=SM_SCALE,
                            accum_out=stat.ap[:, sp + kg:sp + kg + 1]),
                            waits=[t, btoks[h % 2]] + ARENA + (probs.barrier() if kg == 0 else []),
                            inc="act")
                        b.used(tex)
                        anote(tex)
                        extoks.append(tex)
                    probs.produced(extoks[-1])
                    for t in extoks[:-1]:
                        probs.used(t)

                do_sexp(0)
                for qc in range(NCH):
                    probs = prbs0[qc % 2]
                    sp = 6 * (qc % 2)
                    tr1 = em.emit("vector", lambda E, sp=sp: E.tensor_scalar(
                        out=nrm.ap[:, 8:12], in0=stat.ap[:, sp:sp + 4], scalar1=1.0, scalar2=0.0,
                        op0=ALU.mult, op1=ALU.add, accum_out=stat.ap[:, sp + 4:sp + 5]),
                        waits=probs.barrier() + [probs.ready], inc="dve")
                    tr2 = em.emit("vector", lambda E, sp=sp: E.reciprocal(
                        stat.ap[:, sp + 5:sp + 6], stat.ap[:, sp + 4:sp + 5]),
                        waits=[tr1], inc="dve")
                    t6 = em.emit("vector", lambda E, r=rrep, sp=sp: E.tensor_scalar_mul(
                        r.ap, ones128.ap[:], stat.ap[:, sp + 5:sp + 6]),
                        waits=[tr2] + rrep.barrier(), inc="dve")
                    rrep.produced(t6)
                    if qc + 1 < NCH:
                        do_sexp(qc + 1)
                    for kg in range(4):
                        b = pb[2 + kg]
                        t = em.emit("tensor", lambda E, kg=kg, qc=qc, b=b, r=rrep, probs=probs: E.matmul(
                            b.ap[:], r.ap, probs.ap[:, kg * 512:(kg + 1) * 512],
                            start=(qc == 0), stop=(qc == NCH - 1)),
                            waits=([t6, probs.ready] + (b.barrier() if qc == 0 else [])),
                            inc="pe")
                        if qc == NCH - 1:
                            b.produced(t)
                        rrep.used(t)
                        probs.used(t)
                        anote(t)
                if h + 1 < NH:
                    proj_qk0(h + 1)
                vtoks = []
                for kg in range(4):
                    b = pb[2 + kg]
                    tv = em.emit("vector", lambda E, kg=kg, b=b: E.tensor_mul(
                        v_h.ap[:, kg * 512:(kg + 1) * 512], v_h.ap[:, kg * 512:(kg + 1) * 512], b.ap[:]),
                        waits=[b.ready, v_h.ready], inc="dve")
                    b.used(tv)
                    v_h.used(tv)
                    anote(tv)
                    vtoks.append(tv)
                td = em.dma(W["vals"][h], v_h.ap[:, 0:T], "dV", waits=vtoks, queue="gpsimd")
                v_h.used(td)
                anote(td)
                em.dv_toks.append(td)
                if h + 1 < NH:
                    proj_v0(h + 1)
                    btoks[(h + 1) % 2] = emit_cbound((h + 1) % 2, (q_h, k_h), prbs0[0],
                                                     [pb[2], pb[3]])
            out_proj(W["l0_wo"], None)

        def out_proj(panels_ap, bias_cols):
            xbar = xnorm.barrier()
            pre = [load_panel(panels_ap[0], D), load_panel(panels_ap[1], D)]
            vload = []
            for c in range(NCH):
                t = em.dma(rch(xnorm, c), W["vals"][c], "dVL",
                           waits=xbar + [em.dv_toks[c]], queue="gpsimd")
                vload.append(t)
            xnorm.produced(vload[-1])
            for t in vload[:-1]:
                xnorm.used(t)
            for dm in range(16):
                op = pre[dm] if dm < 2 else load_panel(panels_ap[dm], D)
                for tgh in range(2):
                    bd = [pb[2], pb[3]] if tgh == 0 else [pb[4], pb[5]]
                    for kc in range(NCH):
                        for j in range(2):
                            tg = tgh * 2 + j
                            t = em.emit("tensor", lambda E, op=op, kc=kc, tg=tg, b=bd[j]: E.matmul(
                                b.ap[:], op.ap[:, kc * 128:(kc + 1) * 128], rsg(xnorm, kc, tg),
                                start=(kc == 0), stop=(kc == NCH - 1)),
                                waits=([op.ready, vload[kc]] + (bd[j].barrier() if kc == 0 else [])),
                                inc="pe" if kc == NCH - 1 else None)
                            if kc == NCH - 1:
                                bd[j].produced(t)
                                op.used(t)
                                xnorm.used(t)
                    for j in range(2):
                        tg = tgh * 2 + j
                        if bias_cols is None:
                            te = em.emit("vector", lambda E, dm=dm, tg=tg, b=bd[j]: E.tensor_add(
                                rsg(resid, dm, tg), rsg(resid, dm, tg), b.ap[:]),
                                waits=[bd[j].ready], inc="dve")
                        else:
                            te = em.emit("vector", lambda E, dm=dm, tg=tg, b=bd[j]: E.scalar_tensor_tensor(
                                out=rsg(resid, dm, tg), in0=b.ap[:], scalar=bias_cols[:, dm:dm + 1],
                                in1=rsg(resid, dm, tg), op0=ALU.add, op1=ALU.add),
                                waits=[bd[j].ready, boutS.ready], inc="dve")
                        bd[j].used(te)
                        emit_glu.rtoks.append(te)
                        _rchunk[dm].append(te)

        def emit_attn1():
            q_h, k_h, vtok = bigB[0], bigB[1], bigB[2]
            prbs = [bigB[5], bigB[7]]
            vals = bigB[6]
            btoks = [None, None]

            def proj_qk1(h):
                qp = load_panel(W["l1_wqkv"][h], D)
                proj_head(qp, q_h, bqkS.ap[:, h:h + 1], [pb[0], pb[1], pb[2], pb[5]])
                kp = load_panel(W["l1_wqkv"][16 + h], D)
                proj_head(kp, k_h, bqkS.ap[:, 16 + h:17 + h], [pb[0], pb[1], pb[2], pb[5]])

            def proj_v1(h):
                vp = load_panel(W["l1_wqkv"][32 + h], D)
                vtoks = []
                for tc in range(NCH):
                    b = pb[0] if tc % 2 == 0 else pb[1]
                    for kc in range(NCH):
                        t = em.emit("tensor", lambda E, vp=vp, tc=tc, kc=kc, b=b: E.matmul(
                            b.ap[:, 0:128], xnorm.ap[:, kc * T + tc * 128: kc * T + tc * 128 + 128],
                            vp.ap[:, kc * 128:(kc + 1) * 128],
                            start=(kc == 0), stop=(kc == NCH - 1)),
                            waits=([vp.ready, xnorm.ready] + (b.barrier() if kc == 0 else [])),
                            inc="pe" if kc == NCH - 1 else None)
                    b.produced(t)
                    vp.used(t)
                    xnorm.used(t)
                    te = em.emit("scalar", lambda E, tc=tc, b=b: E.copy(
                        out=vtok.ap[:, tc * 128:(tc + 1) * 128], in_=b.ap[:, 0:128]),
                        waits=[b.ready] + ARENA + (vtok.barrier() if tc == 0 else []), inc="act")
                    b.used(te)
                    anote(te)
                    vtoks.append(te)
                vtok.produced(vtoks[-1])
                for t in vtoks[:-1]:
                    vtok.used(t)

            proj_qk1(0)
            btoks[0] = emit_cbound(0, (q_h, k_h), prbs[0], [pb[3], pb[4]])
            proj_v1(0)
            for h in range(NH):

                def do_sexp1(qc):
                    probs = prbs[qc % 2]
                    sp = 6 * (qc % 2)
                    L = (qc + 1) * 128
                    nkg = (L + 511) // 512
                    sbanks = [pb[0], pb[1]] if qc % 2 == 0 else [pb[2], pb[5]]
                    extoks = []
                    for kg in range(nkg):
                        w = min(512, L - kg * 512)
                        b = sbanks[kg % 2]
                        t = em.emit("tensor", lambda E, qc=qc, kg=kg, w=w, b=b: E.matmul(
                            b.ap[:, 0:w], q_h.ap[:, qc * 128:(qc + 1) * 128],
                            k_h.ap[:, kg * 512: kg * 512 + w], start=True, stop=True),
                            waits=[q_h.ready, k_h.ready] + b.barrier(), inc="pe")
                        b.produced(t)
                        q_h.used(t)
                        k_h.used(t)
                        anote(t)
                        if kg < nkg - 1:
                            tex = em.emit("scalar", lambda E, kg=kg, b=b, probs=probs, sp=sp,
                                          hh=h % 2: E.activation(
                                out=probs.ap[:, kg * 512:(kg + 1) * 512], in_=b.ap[:], func=AF.Exp,
                                bias=bcol.ap[:, hh:hh + 1], scale=SM_SCALE,
                                accum_out=stat.ap[:, sp + kg:sp + kg + 1]),
                                waits=[t, btoks[h % 2]] + ARENA + (probs.barrier() if kg == 0 else []),
                                inc="act")
                        else:
                            tex = em.emit("scalar", lambda E, kg=kg, w=w, b=b, probs=probs,
                                          hh=h % 2: E.activation(
                                out=probs.ap[:, kg * 512: kg * 512 + w], in_=b.ap[:, 0:w], func=AF.Exp,
                                bias=bcol.ap[:, hh:hh + 1], scale=SM_SCALE),
                                waits=[t, btoks[h % 2]] + ARENA + (probs.barrier() if kg == 0 else []),
                                inc="act")
                        b.used(tex)
                        anote(tex)
                        extoks.append(tex)
                    tm = em.emit("vector", lambda E, qc=qc, probs=probs: E.tensor_mul(
                        probs.ap[:, qc * 128:(qc + 1) * 128], probs.ap[:, qc * 128:(qc + 1) * 128],
                        triS.ap[:]), waits=[extoks[-1], triS.ready], inc="dve")
                    tps = em.emit("vector", lambda E, qc=qc, probs=probs, sp=sp, L=L, nkg=nkg:
                                  E.tensor_scalar(
                        out=bigB[3].ap[:, 0:(L - (nkg - 1) * 512)],
                        in0=probs.ap[:, (nkg - 1) * 512: L], scalar1=1.0, scalar2=0.0,
                        op0=ALU.mult, op1=ALU.add, accum_out=stat.ap[:, sp + 3:sp + 4]),
                        waits=[tm] + bigB[3].barrier(), inc="dve")
                    bigB[3].used(tps)
                    tr = tps
                    for kg in range(nkg - 1):
                        tr = em.emit("vector", lambda E, sp=sp, kg=kg: E.tensor_add(
                            stat.ap[:, sp + 3:sp + 4], stat.ap[:, sp + 3:sp + 4],
                            stat.ap[:, sp + kg:sp + kg + 1]),
                            waits=[tr, extoks[kg]], inc="dve")
                    tr2 = em.emit("vector", lambda E, sp=sp: E.reciprocal(
                        stat.ap[:, sp + 5:sp + 6], stat.ap[:, sp + 3:sp + 4]),
                        waits=[tr], inc="dve")
                    trep = em.emit("vector", lambda E, sp=sp, par=qc % 2: E.tensor_scalar_mul(
                        scrT[:, 1280 + par * 128: 1408 + par * 128], ones128.ap[:],
                        stat.ap[:, sp + 5:sp + 6]),
                        waits=[tr2, ones128.ready] + ARENA, inc="dve")
                    probs.produced(tm)
                    return tm, trep

                tns = [None] * NCH
                tns[0] = do_sexp1(0)
                for qc in range(NCH):
                    probs = prbs[qc % 2]
                    tn, trep = tns[qc]
                    if qc + 1 < NCH:
                        tns[qc + 1] = do_sexp1(qc + 1)
                    bav = pb[3] if qc % 2 == 0 else pb[4]
                    cptoks = [None] * (qc + 1)

                    def emit_T(kc, probs=probs, tn=tn):
                        pt = pTa if kc % 2 == 0 else pTb
                        tt = em.emit("tensor", lambda E, kc=kc, pt=pt: E.transpose(
                            pt.ap[:, 0:128], probs.ap[:, kc * 128:(kc + 1) * 128], identS.ap[:]),
                            waits=[tn, identS.ready] + ARENA + pt.barrier(), inc="pe")
                        pt.produced(tt)
                        probs.used(tt)
                        ps = pTs[kc % 4]
                        tc_ = em.emit("scalar", lambda E, pt=pt, ps=ps: E.copy(
                            out=ps.ap[:], in_=pt.ap[:, 0:128]),
                            waits=[tt] + ARENA + ps.barrier(), inc="act")
                        pt.used(tc_)
                        ps.produced(tc_)
                        cptoks[kc] = tc_

                    emit_T(0)
                    if qc >= 1:
                        emit_T(1)
                    for kc in range(qc + 1):
                        if kc + 2 <= qc:
                            emit_T(kc + 2)
                        ps = pTs[kc % 4]
                        t = em.emit("tensor", lambda E, kc=kc, qc=qc, b=bav, ps=ps: E.matmul(
                            b.ap[:, 0:128], vtok.ap[:, kc * 128:(kc + 1) * 128], ps.ap[:],
                            start=(kc == 0), stop=(kc == qc)),
                            waits=([cptoks[kc], vtok.ready] + (bav.barrier() if kc == 0 else [])),
                            inc="pe")
                        ps.used(t)
                        vtok.used(t)
                        anote(t)
                    bav.produced(t)
                    ptr = pTb if qc % 2 == 0 else pTa
                    ttr = em.emit("tensor", lambda E, ptr=ptr, par=qc % 2: E.transpose(
                        ptr.ap[:, 0:128], scrT[:, 1280 + par * 128: 1408 + par * 128],
                        identS.ap[:]),
                        waits=[trep, identS.ready] + ptr.barrier(), inc="pe")
                    ptr.produced(ttr)
                    tcr = em.emit("scalar", lambda E, ptr=ptr, par=qc % 2: E.copy(
                        out=scrT[:, 1536 + par * 128: 1664 + par * 128], in_=ptr.ap[:, 0:128]),
                        waits=[ttr] + ARENA, inc="act")
                    ptr.used(tcr)
                    te1 = em.emit("vector", lambda E, qc=qc, b=bav, par=qc % 2: E.tensor_mul(
                        vals.ap[:, qc * 128:(qc + 1) * 128], b.ap[:, 0:128],
                        scrT[:, 1536 + par * 128: 1664 + par * 128]),
                        waits=[bav.ready, tcr] + (vals.barrier() if qc == 0 else []), inc="dve")
                    te = em.emit("vector", lambda E, qc=qc, h=h: E.tensor_scalar_add(
                        vals.ap[:, qc * 128:(qc + 1) * 128], vals.ap[:, qc * 128:(qc + 1) * 128],
                        bqkS.ap[:, 32 + h:33 + h]),
                        waits=[te1, bqkS.ready], inc="dve")
                    bav.used(te1)
                    vals.used(te)
                    anote(te)
                if h + 1 < NH:
                    proj_qk1(h + 1)
                td = em.dma(W["vals"][h], vals.ap[:, 0:T], "dV", waits=vals.barrier(), queue="gpsimd")
                vals.produced(None)
                vals.used(td)
                anote(td)
                em.dv_toks.append(td)
                if h + 1 < NH:
                    proj_v1(h + 1)
                    btoks[(h + 1) % 2] = emit_cbound((h + 1) % 2, (q_h, k_h), prbs[0],
                                                     [pb[3], pb[4]])
            out_proj(W["l1_wout"], boutS.ap)

        tap_done = False
        for si, sub in enumerate(sublayers):
            arena_flush()
            emit_ln()
            if tap == "big" and sub == 1:
                em.dv_toks = []
                emit_attn0()
                col = {}
                for s_, v_ in _anotes:
                    col[s_] = max(col.get(s_, 0), v_)
                bw = list(col.items())
                for c in range(8):
                    em.dma(y_out[c], bigT[:, c * T:(c + 1) * T], "dO", waits=bw)
                em.emit("sync", lambda E: E.nop(), waits=[("dO", 16 * 8)])
                tap_done = True
                break
            if tap == "rbc":
                for kg in range(4):
                    em.emit("scalar", lambda E, kg=kg: E.copy(
                        out=rch(xnorm, 0)[:, kg * 512:(kg + 1) * 512], in_=pb[2 + kg].ap[:]),
                        waits=[pb[2 + kg].ready], inc="act")
                tmp = em.emit("scalar", lambda E: E.nop(), waits=[], inc="act")
                em.dma(y_out[0], rbc.ap[:], "dO", waits=[rbc.ready])
                em.dma(y_out[1], rch(xnorm, 0), "dO", waits=[("act", em.cnt["act"])])
                em.emit("sync", lambda E: E.nop(), waits=[("dO", 32)])
                tap_done = True
                break
            if tap == "xnorm":
                for c in range(NCH):
                    em.dma(y_out[c], rch(xnorm, c), "dO", waits=[xnorm.ready])
                em.emit("sync", lambda E: E.nop(), waits=[("dO", 16 * NCH)])
                tap_done = True
                break
            if tap in ("logits", "wtm") and sub in (0, 2):
                emit_router()
                if tap == "logits":
                    tw = [rtr.users[i] for i in range(NCH)]
                    em.dma(y_out[0][:, 0:128], rtr.ap[:], "dO", waits=tw)
                else:
                    em.dma(y_out[0][:, 0:128].rearrange("p t -> p t"), wtm.ap[:], "dO",
                           waits=[wtm.ready])
                em.emit("sync", lambda E: E.nop(), waits=[("dO", 16)])
                tap_done = True
                break
            if tap == "wbc0" and sub in (0, 2):
                emit_router()
                emit_wbc(0)
                em.dma(y_out[0], wbc.ap[:], "dO", waits=[wbc.ready])
                em.emit("sync", lambda E: E.nop(), waits=[("dO", 16)])
                tap_done = True
                break
            if sub in (0, 2):
                emit_moe()
            elif sub == 1:
                em.dv_toks = []
                emit_attn0()
            elif sub == 3 or sub == 5:
                emit_ffn1()
            elif sub == 4:
                em.dv_toks = []
                emit_attn1()
            RESID_TOKS[:] = [t for t in emit_glu.rtoks]
            emit_glu.rtoks = []
            col = {}
            for s_, v_ in RESID_TOKS:
                col[s_] = max(col.get(s_, 0), v_)
            RESID_TOKS[:] = list(col.items())
            for c in range(NCH):
                cc = {}
                for s_, v_ in _rchunk[c]:
                    cc[s_] = max(cc.get(s_, 0), v_)
                RESID_CTOKS[c] = list(cc.items()) if cc else list(RESID_TOKS)
                _rchunk[c] = []
            xnorm.produced(None)
            xnorm.users = []

        if tap_done:
            block = ctx.enter_context(nc.Block())
            for eng in ENGINES:
                ops = em.ops[eng]
                if not ops:
                    continue

                def body(E, ops=ops):
                    for op in ops:
                        op(E)
                getattr(block, eng)(body)
            return nc
        st_toks = []
        for c in range(NCH):
            if last:
                lb = wland[c % 2]
                tc_ = em.emit("gpsimd", lambda E, c=c, lb=lb: E.tensor_copy(
                    out=lb.ap[:, 0:T], in_=rch(resid, c)),
                    waits=resid_wait() + lb.barrier(), inc="pool")
                lb.produced(tc_)
                td = em.dma(y_out[c], lb.ap[:, 0:T], "dO", waits=[tc_])
                lb.used(td)
            else:
                td = em.dma(y_out[c], rch(resid, c), "dO", waits=resid_wait())
            st_toks.append(td)
        em.emit("sync", lambda E: E.nop(), waits=[("dO", 16 * NCH)])

        block = ctx.enter_context(nc.Block())
        for eng in ENGINES:
            ops = em.ops[eng]
            if not ops:
                continue

            def body(E, ops=ops):
                for op in ops:
                    op(E)
            getattr(block, eng)(body)

    return nc



GROUPS = [[0, 1, 2, 3, 4, 5]]

_cache = {}


def _get_prog(key, subs, first, last):
    if key not in _cache:
        _cache[key] = build_program(subs, first, last)
    return _cache[key]


def kernel(**inputs):
    h = prep_inputs(inputs)
    shared = {
        "wr": h["wr"], "wg": h["wg"], "wu": h["wu"], "wd": h["wd"],
        "l0_wqkv": h["l0_wqkv"], "l0_wo": h["l0_wo"],
        "l1_wqkv": h["l1_wqkv"], "l1_wout": h["l1_wout"],
        "bqk": h["bqk"], "bout": h["bout"], "tri": h["tri"],
        "l1_wg": h["l1_wg"], "l1_wu": h["l1_wu"], "l1_wd": h["l1_wd"],
        "ident": h["ident"],
    }

    def in_map_for(subs, first, core, cur):
        need = set(subs)
        m = {}
        m["x"] = h["x"][core] if first else cur[core]
        if need & {0, 2}:
            for k in ("wr", "wg", "wu", "wd"):
                m[k] = shared[k]
        if 1 in need:
            m["l0_wqkv"] = shared["l0_wqkv"]
            m["l0_wo"] = shared["l0_wo"]
        if 4 in need:
            for k in ("l1_wqkv", "l1_wout", "bqk", "bout", "tri"):
                m[k] = shared[k]
        if need & {3, 5}:
            for k in ("l1_wg", "l1_wu", "l1_wd"):
                m[k] = shared[k]
        m["ident"] = shared["ident"]
        return m

    cur = None
    for gi, subs in enumerate(GROUPS):
        first = gi == 0
        last = gi == len(GROUPS) - 1
        nc = _get_prog((tuple(subs), first, last), subs, first, last)
        in_maps = [in_map_for(subs, first, c, cur) for c in range(N_CORES)]
        res = run_bass_kernel_spmd(nc, in_maps, core_ids=list(range(N_CORES)))
        cur = [res.results[c]["y"] for c in range(N_CORES)]

    out = np.empty((N_CORES, T, D), np.float32)
    for c in range(N_CORES):
        out[c] = cur[c].reshape(D, T).T
    return out


# version 22
# speedup vs baseline: 1.0883x; 1.0334x over previous
import math
import numpy as np
import ml_dtypes
from contextlib import ExitStack

import concourse.bass as bass
import concourse.mybir as mybir
from concourse.bass_utils import run_bass_kernel_spmd

F32 = mybir.dt.float32
BF16 = mybir.dt.bfloat16
AF = mybir.ActivationFunctionType
ALU = mybir.AluOpType
AX = mybir.AxisListType
BF = ml_dtypes.bfloat16

N_CORES = 8
D = 2048
T = 2048
NCH = 16
NH = 16
NE = 8
F1 = 8192
EPS = 1e-5
SM_SCALE = 1.0 / math.sqrt(128.0)
NEG = -30000.0

ENGINES = ("sync", "tensor", "vector", "scalar", "gpsimd")
DEBUG = {}
SUBLAYERS = (0, 1, 2, 3, 4, 5)


def _panels(W):
    K, M = W.shape
    return np.ascontiguousarray(
        W.reshape(K // 128, 128, M // 128, 128).transpose(2, 1, 0, 3).reshape(M // 128, 128, K))


def prep_inputs(inp):
    h = {}
    x = np.asarray(inp["L0_attn_x"], np.float32)
    h["x"] = [np.ascontiguousarray(x[c].T).reshape(NCH, 128, T) for c in range(N_CORES)]
    h["l0_wqkv"] = _panels(np.asarray(inp["L0_attn_w_qkv"], np.float32))
    h["l0_wo"] = _panels(np.asarray(inp["L0_attn_w_o"], np.float32))
    h["wr"] = np.ascontiguousarray(
        np.asarray(inp["L0_ffn_Wr"], np.float32).reshape(NCH, 128, NE).transpose(1, 0, 2)).reshape(128, NCH * NE)
    for nm, key in (("wg", "L0_ffn_Wg"), ("wu", "L0_ffn_Wu"), ("wd", "L0_ffn_Wd")):
        h[nm] = np.ascontiguousarray(np.stack(
            [_panels(np.asarray(inp[key], np.float32)[e]) for e in range(NE)]))
    h["wd"] = np.ascontiguousarray(
        h["wd"].reshape(NE, 16, 128, 2, 1024).transpose(0, 3, 1, 2, 4))
    h["l1_wqkv"] = _panels(np.ascontiguousarray(np.asarray(inp["L1_attn_w_qkv"], np.float32).T))
    h["l1_wout"] = _panels(np.ascontiguousarray(np.asarray(inp["L1_attn_w_out"], np.float32).T))
    h["l1_wg"] = _panels(np.ascontiguousarray(np.asarray(inp["L1_ffn_w_gate"], np.float32).T))
    h["l1_wu"] = _panels(np.ascontiguousarray(np.asarray(inp["L1_ffn_w_up"], np.float32).T))
    l1wd = _panels(np.ascontiguousarray(np.asarray(inp["L1_ffn_w_down"], np.float32).T))
    h["l1_wd"] = np.ascontiguousarray(l1wd.reshape(16, 128, 8, 1024).transpose(2, 0, 1, 3))
    h["bqk"] = np.ascontiguousarray(np.asarray(inp["L1_attn_b_qkv"], np.float32).reshape(48, 128).T)
    h["bout"] = np.ascontiguousarray(np.asarray(inp["L1_attn_b_out"], np.float32).reshape(16, 128).T)
    h["ident"] = np.eye(128, dtype=np.float32).astype(BF)
    h["tri"] = np.tril(np.ones((128, 128), np.float32)).astype(BF)
    return h



class Em:
    def __init__(self):
        self.ops = {e: [] for e in ENGINES}
        self.cnt = {}
        self.hw = {}
        self.sems = {}
        self.dv_toks = []

    def _wait(self, eng, tok):
        if tok is None:
            return
        sem, val = tok
        if val <= 0 or val <= self.hw.get((eng, sem), 0):
            return
        self.hw[(eng, sem)] = val

        def cl(E, s=sem, v=val):
            E.wait_ge(self.sems[s], v)
        self.ops[eng].append(cl)

    def emit(self, eng, fn, waits=(), inc=None, n=1):
        for t in waits:
            self._wait(eng, t)
        if inc is None:
            self.ops[eng].append(fn)
            return None
        self.cnt[inc] = self.cnt.get(inc, 0) + n
        tgt = self.cnt[inc]

        def cl(E, fn=fn, s=inc, k=n):
            fn(E).then_inc(self.sems[s], k)
        self.ops[eng].append(cl)
        return (inc, tgt)

    def dma(self, out, in_, sem, waits=(), queue="sync"):
        return self.emit(queue, lambda E, o=out, i=in_: E.dma_start(out=o, in_=i),
                         waits=waits, inc=sem, n=16)


class Buf:
    __slots__ = ("ap", "ready", "users")

    def __init__(self, ap):
        self.ap = ap
        self.ready = None
        self.users = []

    def barrier(self):
        toks = {}
        for t in self.users + ([self.ready] if self.ready else []):
            if t and t[1] > toks.get(t[0], -1):
                toks[t[0]] = t[1]
        return [(s, v) for s, v in toks.items()]

    def produced(self, tok):
        self.ready = tok
        self.users = []

    def used(self, tok):
        if tok is not None:
            self.users.append(tok)



def build_program(sublayers, first, last, tap=None):
    nc = bass.Bass("TRN2", target_bir_lowering=False)
    em = Em()
    need = set(sublayers)

    x_in = nc.dram_tensor("x", [NCH, 128, T], F32 if first else BF16, kind="ExternalInput").ap()
    y_out = nc.dram_tensor("y", [NCH, 128, T], F32 if last else BF16, kind="ExternalOutput").ap()

    W = {}
    if need & {0, 2}:
        W["wr"] = nc.dram_tensor("wr", [128, NCH * NE], F32, kind="ExternalInput").ap()
        for nm in ("wg", "wu"):
            W[nm] = nc.dram_tensor(nm, [NE, 16, 128, D], F32, kind="ExternalInput").ap()
        W["wd"] = nc.dram_tensor("wd", [NE, 2, 16, 128, 1024], F32, kind="ExternalInput").ap()
    if 1 in need:
        W["l0_wqkv"] = nc.dram_tensor("l0_wqkv", [48, 128, D], F32, kind="ExternalInput").ap()
        W["l0_wo"] = nc.dram_tensor("l0_wo", [16, 128, D], F32, kind="ExternalInput").ap()
    if 4 in need:
        W["l1_wqkv"] = nc.dram_tensor("l1_wqkv", [48, 128, D], F32, kind="ExternalInput").ap()
        W["l1_wout"] = nc.dram_tensor("l1_wout", [16, 128, D], F32, kind="ExternalInput").ap()
        W["bqk"] = nc.dram_tensor("bqk", [128, 48], F32, kind="ExternalInput").ap()
        W["bout"] = nc.dram_tensor("bout", [128, 16], F32, kind="ExternalInput").ap()
        W["tri"] = nc.dram_tensor("tri", [128, 128], BF16, kind="ExternalInput").ap()
    if need & {3, 5}:
        W["l1_wg"] = nc.dram_tensor("l1_wg", [64, 128, D], F32, kind="ExternalInput").ap()
        W["l1_wu"] = nc.dram_tensor("l1_wu", [64, 128, D], F32, kind="ExternalInput").ap()
        W["l1_wd"] = nc.dram_tensor("l1_wd", [8, 16, 128, 1024], F32, kind="ExternalInput").ap()
    W["ident"] = nc.dram_tensor("ident", [128, 128], BF16, kind="ExternalInput").ap()
    if {0, 2} <= need:
        for nm in ("wg", "wu"):
            W[nm + "_c"] = nc.dram_tensor(nm + "_c", [NE, 16, 128, D], BF16, kind="Internal").ap()
        W["wd_c"] = nc.dram_tensor("wd_c", [NE, 2, 16, 128, 1024], BF16, kind="Internal").ap()
    if {3, 5} <= need:
        W["l1_wg_c"] = nc.dram_tensor("l1_wg_c", [64, 128, D], BF16, kind="Internal").ap()
        W["l1_wu_c"] = nc.dram_tensor("l1_wu_c", [64, 128, D], BF16, kind="Internal").ap()
        W["l1_wd_c"] = nc.dram_tensor("l1_wd_c", [8, 16, 128, 1024], BF16, kind="Internal").ap()
    if need & {1, 4}:
        W["vals"] = nc.dram_tensor("vals", [NCH, 128, T], BF16, kind="Internal").ap()

    ctx = ExitStack()
    with ctx:
        def S(name, shape, dt):
            return Buf(ctx.enter_context(nc.sbuf_tensor(name, shape, dt)).ap())

        resid = S("resid", [128, NCH * T], BF16)
        xnorm = S("xnorm", [128, NCH * T], BF16)
        bigT = ctx.enter_context(nc.sbuf_tensor("big", [128, 16384], BF16)).ap()
        wland = [S(f"wland{i}", [128, 2048], F32) for i in range(2)]
        wbf = [S(f"wbf{i}", [128, 2048], BF16) for i in range(2)]
        scrT = ctx.enter_context(nc.sbuf_tensor("scr", [128, 2048], BF16)).ap()
        auxT = ctx.enter_context(nc.sbuf_tensor("aux", [128, 4096], BF16)).ap()
        rbc = Buf(auxT[:, 0:T])
        wbc = Buf(auxT[:, T:2 * T]) if need & {0, 2} else None
        aux_f32 = auxT.bitcast(F32)
        rowA = S("rowA", [1, 512], F32)
        rowB = S("rowB", [1, 512], F32)
        rowT = S("rowT", [1, 512], F32)
        rowC = S("rowC", [1, 512], BF16)
        rowD = S("rowD", [1, 512], BF16)
        wrow = S("wrow", [1, 128], BF16)
        ones1 = S("ones1", [1, 128], BF16)
        ones128 = S("ones128", [128, 128], BF16)
        stat = S("stat", [128, 12], F32)
        nrm = S("nrm", [128, 12], F32)
        bcol = S("bcol", [128, 2], F32)
        rtr = S("rtr", [128, 128], F32) if need & {0, 2} else None
        rtw = S("rtw", [128, 16], F32) if need & {0, 2} else None
        wtm = S("wtm", [128, 128], BF16) if need & {0, 2} else None
        wrB = S("wrB", [128, NCH * NE], BF16) if need & {0, 2} else None
        identS = S("identS", [128, 128], BF16)
        triS = S("triS", [128, 128], BF16) if 4 in need else None
        bqkS = S("bqkS", [128, 48], F32) if 4 in need else None
        boutS = S("boutS", [128, 16], F32) if 4 in need else None
        pTs = ([S(f"pTs{i}", [128, 128], BF16) for i in range(2)]
               + [Buf(scrT[:, 1024:1152]), Buf(scrT[:, 1152:1280])]) if 4 in need else None

        pb = [Buf(ctx.enter_context(nc.psum_tensor(f"pb{i}", [128, 512], F32)).ap())
              for i in range(6)]
        pTa = Buf(ctx.enter_context(nc.psum_tensor("pTa", [128, 1024], BF16)).ap())
        pTb = Buf(ctx.enter_context(nc.psum_tensor("pTb", [128, 1024], BF16)).ap())
        pTaF = Buf(pTa.ap.bitcast(F32))
        pTbF = Buf(pTb.ap.bitcast(F32))

        for s in ("dL", "dV", "dVL", "dO", "dC", "pe", "act", "dve", "pool"):
            em.sems[s] = ctx.enter_context(nc.semaphore(s))

        def rch(buf, c):
            return buf.ap[:, c * T:(c + 1) * T]

        def rsg(buf, c, tg, w=512):
            return buf.ap[:, c * T + tg * w: c * T + (tg + 1) * w]

        ARENA = []
        _anotes = []
        RESID_CTOKS = [[] for _ in range(NCH)]
        _rchunk = [[] for _ in range(NCH)]

        def anote(tok):
            if tok is not None:
                _anotes.append(tok)
            return tok

        def arena_flush():
            col = {}
            for s_, v_ in _anotes:
                col[s_] = max(col.get(s_, 0), v_)
            ARENA[:] = list(col.items())
            _anotes.clear()

        tok0 = em.emit("vector", lambda E: E.memset(ones1.ap[:], 1.0), inc="dve")
        ones1.produced(tok0)
        tok0 = em.emit("vector", lambda E: E.memset(ones128.ap[:], 1.0), inc="dve")
        ones128.produced(tok0)
        identS.produced(em.dma(identS.ap[:], W["ident"], "dL"))
        if 4 in need:
            triS.produced(em.dma(triS.ap[:], W["tri"], "dL"))
            bqkS.produced(em.dma(bqkS.ap[:], W["bqk"], "dL"))
            boutS.produced(em.dma(boutS.ap[:], W["bout"], "dL"))
        if need & {0, 2}:
            wrS = S("wrS", [128, NCH * NE], F32)
            wrS.produced(em.dma(wrS.ap[:], W["wr"], "dL"))
            t = em.emit("gpsimd", lambda E: E.tensor_copy(out=wrB.ap[:], in_=wrS.ap[:]),
                        waits=[wrS.ready], inc="pool")
            wrB.produced(t)

        in_toks = []
        for c in range(NCH):
            if first:
                lb = wland[c % 2]
                tl = em.dma(lb.ap[:, 0:T], x_in[c], "dL", waits=lb.barrier())
                lb.produced(tl)
                tc_ = em.emit("gpsimd",
                              lambda E, c=c, lb=lb: E.tensor_copy(out=rch(resid, c), in_=lb.ap[:, 0:T]),
                              waits=[tl], inc="pool")
                lb.used(tc_)
                in_toks.append(tc_)
            else:
                in_toks.append(em.dma(rch(resid, c), x_in[c], "dL"))
        resid.produced(in_toks[-1])
        for t in in_toks[:-1]:
            resid.used(t)
        resid_ready = resid.barrier()
        resid.users = []
        resid.ready = None
        RESID_TOKS = list(resid_ready)
        for c in range(NCH):
            RESID_CTOKS[c] = [in_toks[c]]

        def resid_wait():
            return list(RESID_TOKS)

        _pl = {"i": 0}

        def load_panel(hbm_ap, cols, scale=None, cache_to=None, cache_from=None):
            bb = wbf[_pl["i"] % 2]
            if cache_from is not None:
                _pl["i"] += 1
                tl = em.dma(bb.ap[:, 0:cols], cache_from, "dL",
                            waits=bb.barrier() + [("dC", em.cnt.get("dC", 0))])
                bb.produced(tl)
                return bb
            lb = wland[_pl["i"] % 2]
            _pl["i"] += 1
            tl = em.dma(lb.ap[:, 0:cols], hbm_ap, "dL", waits=lb.barrier())
            lb.produced(tl)
            if scale is None:
                if _pl["i"] % 2 == 0:
                    fn = lambda E, lb=lb, bb=bb, n=cols: E.tensor_copy(out=bb.ap[:, 0:n], in_=lb.ap[:, 0:n])
                    tc_ = em.emit("gpsimd", fn, waits=[tl] + bb.barrier(), inc="pool")
                else:
                    fn = lambda E, lb=lb, bb=bb, n=cols: E.activation(
                        out=bb.ap[:, 0:n], in_=lb.ap[:, 0:n], func=AF.Identity, bias=0.0, scale=1.0)
                    tc_ = em.emit("scalar", fn, waits=[tl] + bb.barrier(), inc="act")
            else:
                fn = lambda E, lb=lb, bb=bb, n=cols, s=scale: E.activation(
                    out=bb.ap[:, 0:n], in_=lb.ap[:, 0:n], func=AF.Identity, bias=0.0, scale=s)
                tc_ = em.emit("scalar", fn, waits=[tl] + bb.barrier(), inc="act")
            lb.used(tc_)
            bb.produced(tc_)
            if cache_to is not None:
                tw = em.dma(cache_to, bb.ap[:, 0:cols], "dC", waits=[tc_])
                bb.used(tw)
            return bb

        def emit_ln():
            xq = [Buf(scrT[:, 0:512]), Buf(scrT[:, 512:1024])]
            for kg in range(4):
                b_sum, b_sq = pb[0], pb[1]
                for c in range(NCH):
                    t = em.emit("tensor",
                                lambda E, c=c, kg=kg, b=b_sum: E.matmul(
                                    b.ap[:], ones128.ap[:], rsg(resid, c, kg),
                                    start=(c == 0), stop=(c == NCH - 1)),
                                waits=(RESID_CTOKS[c] + [ones128.ready] + (b_sum.barrier() if c == 0 else [])),
                                inc="pe" if c == NCH - 1 else None)
                b_sum.produced(t)
                ta = em.emit("scalar", lambda E, b=b_sum: E.copy(out=rowA.ap[0:1, :], in_=b.ap[0:1, :]),
                             waits=[b_sum.ready] + rowA.barrier(), inc="act")
                b_sum.used(ta)
                rowA.produced(ta)
                for c in range(NCH):
                    xb = xq[c % 2]
                    ts_ = em.emit("scalar",
                                  lambda E, c=c, kg=kg, xb=xb: E.activation(
                                      out=xb.ap[:], in_=rsg(resid, c, kg), func=AF.Square),
                                  waits=RESID_CTOKS[c] + xb.barrier() + ARENA, inc="act")
                    xb.produced(ts_)
                    t = em.emit("tensor",
                                lambda E, xb=xb, b=b_sq, c=c: E.matmul(
                                    b.ap[:], ones128.ap[:], xb.ap[:],
                                    start=(c == 0), stop=(c == NCH - 1)),
                                waits=[ts_] + (b_sq.barrier() if c == 0 else []),
                                inc="pe")
                    xb.used(t)
                    anote(t)
                b_sq.produced(t)
                tb = em.emit("scalar", lambda E, b=b_sq: E.copy(out=rowB.ap[0:1, :], in_=b.ap[0:1, :]),
                             waits=[b_sq.ready] + rowB.barrier(), inc="act")
                b_sq.used(tb)
                rowB.produced(tb)
                t1 = em.emit("vector", lambda E: E.tensor_scalar_mul(rowA.ap[0:1, :], rowA.ap[0:1, :], 1.0 / D),
                             waits=[rowA.ready], inc="dve")
                rowA.produced(t1)
                t2 = em.emit("vector", lambda E: E.scalar_tensor_tensor(
                    out=rowT.ap[0:1, :], in0=rowA.ap[0:1, :], scalar=-1.0, in1=rowA.ap[0:1, :],
                    op0=ALU.mult, op1=ALU.mult), waits=[t1] + rowT.barrier(), inc="dve")
                rowT.produced(t2)
                rowA.used(t2)
                t3 = em.emit("vector", lambda E: E.scalar_tensor_tensor(
                    out=rowB.ap[0:1, :], in0=rowB.ap[0:1, :], scalar=1.0 / D, in1=rowT.ap[0:1, :],
                    op0=ALU.mult, op1=ALU.add), waits=[rowB.ready, t2], inc="dve")
                rowB.produced(t3)
                rowT.used(t3)
                t3b = em.emit("vector", lambda E: E.tensor_scalar_add(rowB.ap[0:1, :], rowB.ap[0:1, :], EPS),
                              waits=[t3], inc="dve")
                t4 = em.emit("scalar", lambda E: E.activation(
                    out=rowB.ap[0:1, :], in_=rowB.ap[0:1, :], func=AF.Sqrt, bias=0.0, scale=1.0),
                    waits=[t3b], inc="act")
                rowB.produced(t4)
                t5 = em.emit("vector", lambda E: E.reciprocal(rowT.ap[0:1, :], rowB.ap[0:1, :]),
                             waits=[t4] + rowT.barrier(), inc="dve")
                rowT.produced(t5)
                rowB.used(t5)
                t5b = em.emit("vector", lambda E: E.tensor_copy(out=rowC.ap[0:1, :], in_=rowT.ap[0:1, :]),
                              waits=[t5] + rowC.barrier(), inc="dve")
                rowC.produced(t5b)
                rowT.used(t5b)
                t6 = em.emit("vector", lambda E: E.tensor_mul(rowD.ap[0:1, :], rowA.ap[0:1, :], rowT.ap[0:1, :]),
                             waits=[t5, rowA.ready] + rowD.barrier(), inc="dve")
                rowD.produced(t6)
                rowA.used(t6)
                rowT.used(t6)
                tr = em.emit("tensor", lambda E, b=b_sum: E.matmul(
                    b.ap[:], ones1.ap[:], rowC.ap[0:1, :], start=True, stop=True),
                    waits=[t5b, ones1.ready] + b_sum.barrier(), inc="pe")
                b_sum.produced(tr)
                rowC.used(tr)
                te = em.emit("scalar", lambda E, b=b_sum, kg=kg: E.copy(
                    out=rbc.ap[:, kg * 512:(kg + 1) * 512], in_=b.ap[:]),
                    waits=([tr] + rbc.barrier() + ARENA) if kg == 0 else [tr], inc="act")
                b_sum.used(te)
                if kg == 3:
                    rbc.produced(te)
                else:
                    rbc.used(te)
                bmr = pb[2 + kg]
                tm = em.emit("tensor", lambda E, b=bmr: E.matmul(
                    b.ap[:], ones1.ap[:], rowD.ap[0:1, :], start=True, stop=True),
                    waits=[t6, ones1.ready] + bmr.barrier(), inc="pe")
                bmr.produced(tm)
                rowD.used(tm)
            last = None
            for c in range(NCH):
                tmu = em.emit("vector", lambda E, c=c: E.tensor_mul(rch(xnorm, c), rch(resid, c), rbc.ap[:]),
                              waits=RESID_CTOKS[c] + [rbc.ready] + (xnorm.barrier() if c == 0 else []),
                              inc="dve")
                rbc.used(tmu)
                anote(tmu)
                for kg in range(4):
                    last = em.emit("vector", lambda E, c=c, kg=kg: E.tensor_sub(
                        rsg(xnorm, c, kg), rsg(xnorm, c, kg), pb[2 + kg].ap[:]),
                        waits=[tmu, pb[2 + kg].ready], inc="dve")
            for kg in range(4):
                pb[2 + kg].used(last)
            xnorm.produced(last)
            return last

        tbs = [Buf(scrT[:, 1024:1536]), Buf(scrT[:, 1536:2048])]

        def emit_glu(g_aps, u_aps, d_aps, d_kc, gbuf_base, clip7, use_wbc, d_scale,
                     pre_d_hook=None):
            nm = len(g_aps)
            for mi in range(nm):
                gp = load_panel(**g_aps[mi])
                up = load_panel(**u_aps[mi])
                for tgh in range(2):
                    bg = [pb[0], pb[1]]
                    bu = [pb[2], pb[3]]
                    for kc in range(NCH):
                        for j in range(2):
                            tg = tgh * 2 + j
                            t = em.emit("tensor", lambda E, gp=gp, kc=kc, tg=tg, b=bg[j]: E.matmul(
                                b.ap[:], gp.ap[:, kc * 128:(kc + 1) * 128], rsg(xnorm, kc, tg),
                                start=(kc == 0), stop=(kc == NCH - 1)),
                                waits=([gp.ready, xnorm.ready] + (bg[j].barrier() if kc == 0 else [])),
                                inc="pe" if kc == NCH - 1 else None)
                            if kc == NCH - 1:
                                bg[j].produced(t)
                                gp.used(t)
                                xnorm.used(t)
                    for kc in range(NCH):
                        for j in range(2):
                            tg = tgh * 2 + j
                            t = em.emit("tensor", lambda E, up=up, kc=kc, tg=tg, b=bu[j]: E.matmul(
                                b.ap[:], up.ap[:, kc * 128:(kc + 1) * 128], rsg(xnorm, kc, tg),
                                start=(kc == 0), stop=(kc == NCH - 1)),
                                waits=([up.ready, xnorm.ready] + (bu[j].barrier() if kc == 0 else [])),
                                inc="pe" if kc == NCH - 1 else None)
                            if kc == NCH - 1:
                                bu[j].produced(t)
                                up.used(t)
                                xnorm.used(t)
                    for j in range(2):
                        tg = tgh * 2 + j
                        tpos = (gbuf_base + mi) * T + tg * 512
                        tb = tbs[j]
                        ts_ = em.emit("scalar", lambda E, b=bg[j], tb=tb: E.activation(
                            out=tb.ap[:], in_=b.ap[:], func=AF.Silu),
                            waits=[bg[j].ready] + tb.barrier() + ARENA, inc="act")
                        tb.produced(ts_)
                        bg[j].used(ts_)
                        if use_wbc is not None:
                            tu = em.emit("vector", lambda E, b=bu[j], o=tpos, tg=tg: E.tensor_mul(
                                bigT[:, o:o + 512], b.ap[:],
                                use_wbc.ap[:, tg * 512:(tg + 1) * 512]),
                                waits=[bu[j].ready, use_wbc.ready, emit_glu.prev_d] + ARENA, inc="dve")
                            use_wbc.used(tu)
                            anote(tu)
                            bu[j].used(tu)
                            td = em.emit("vector", lambda E, tb=tb, o=tpos: E.scalar_tensor_tensor(
                                out=bigT[:, o:o + 512], in0=tb.ap[:], scalar=7.0,
                                in1=bigT[:, o:o + 512], op0=ALU.min, op1=ALU.mult),
                                waits=[ts_, tu], inc="dve")
                        else:
                            td = em.emit("vector", lambda E, tb=tb, b=bu[j], o=tpos: E.tensor_mul(
                                bigT[:, o:o + 512], tb.ap[:], b.ap[:]),
                                waits=[ts_, bu[j].ready, emit_glu.prev_d] + ARENA, inc="dve")
                            bu[j].used(td)
                        tb.used(td)
                        emit_glu.gtoks.append(td)
            gtok = emit_glu.gtoks[-1]
            gdeps = list(emit_glu.gtoks)
            emit_glu.gtoks = []
            if pre_d_hook is not None:
                pre_d_hook()
            for dm2 in range(8):
                dp = load_panel(**d_aps[dm2])
                for sub in range(2):
                    dm = dm2 * 2 + sub
                    for tgh in range(2):
                        bd = [pb[2], pb[3]] if tgh == 0 else [pb[4], pb[5]]
                        for kc in range(d_kc):
                            for j in range(2):
                                tg = tgh * 2 + j
                                t = em.emit("tensor", lambda E, dp=dp, kc=kc, tg=tg, b=bd[j],
                                            gb=gbuf_base, sub=sub: E.matmul(
                                    b.ap[:], dp.ap[:, sub * 1024 + kc * 128: sub * 1024 + (kc + 1) * 128],
                                    bigT[:, (gb + kc) * T + tg * 512: (gb + kc) * T + tg * 512 + 512],
                                    start=(kc == 0), stop=(kc == d_kc - 1)),
                                    waits=([dp.ready] + gdeps + (bd[j].barrier() if kc == 0 else [])),
                                    inc="pe" if kc == d_kc - 1 else None)
                                if kc == d_kc - 1:
                                    bd[j].produced(t)
                                    dp.used(t)
                                    emit_glu.prev_d = t
                                    anote(t)
                        for j in range(2):
                            tg = tgh * 2 + j
                            te = em.emit("vector", lambda E, dm=dm, tg=tg, b=bd[j]: E.tensor_add(
                                rsg(resid, dm, tg), rsg(resid, dm, tg), b.ap[:]),
                                waits=[bd[j].ready], inc="dve")
                            bd[j].used(te)
                            emit_glu.rtoks.append(te)
                            _rchunk[dm].append(te)
        emit_glu.gtoks = []
        emit_glu.rtoks = []
        emit_glu.prev_d = None

        def emit_router():
            emit_router.mtoks = []
            ev_toks = []
            for tc in range(NCH):
                b = pb[0] if tc % 2 == 0 else pb[1]
                for kc in range(NCH):
                    t = em.emit("tensor", lambda E, tc=tc, kc=kc, b=b: E.matmul(
                        b.ap[:, 0:8], xnorm.ap[:, kc * T + tc * 128: kc * T + tc * 128 + 128],
                        wrB.ap[:, kc * 8:(kc + 1) * 8],
                        start=(kc == 0), stop=(kc == NCH - 1)),
                        waits=([xnorm.ready, wrB.ready] + (b.barrier() if kc == 0 else [])),
                        inc="pe" if kc == NCH - 1 else None)
                b.produced(t)
                xnorm.used(t)
                te = em.emit("scalar", lambda E, tc=tc, b=b: E.copy(
                    out=rtr.ap[:, tc * 8:(tc + 1) * 8], in_=b.ap[:, 0:8]),
                    waits=[b.ready], inc="act")
                b.used(te)
                rtr.used(te)
                ev_toks.append(te)
            for tc in range(NCH):
                lg = rtr.ap[:, tc * 8:(tc + 1) * 8]
                t = em.emit("vector", lambda E, lg=lg: E.reduce_max(stat.ap[:, 0:1], lg, axis=AX.X),
                            waits=[ev_toks[tc]] + stat.barrier(), inc="dve")
                stat.produced(t)
                t1 = em.emit("vector", lambda E, lg=lg: E.tensor_scalar(
                    out=rtw.ap[:, 0:8], in0=lg, scalar1=stat.ap[:, 0:1], scalar2=1.0,
                    op0=ALU.is_equal, op1=ALU.mult), waits=[t] + rtw.barrier(), inc="dve")
                t2 = em.emit("vector", lambda E, lg=lg: E.scalar_tensor_tensor(
                    out=rtw.ap[:, 8:16], in0=rtw.ap[:, 0:8], scalar=NEG, in1=lg,
                    op0=ALU.mult, op1=ALU.add), waits=[t1], inc="dve")
                t3 = em.emit("vector", lambda E: E.reduce_max(stat.ap[:, 1:2], rtw.ap[:, 8:16], axis=AX.X),
                             waits=[t2], inc="dve")
                t4 = em.emit("vector", lambda E: E.tensor_sub(stat.ap[:, 2:3], stat.ap[:, 0:1], stat.ap[:, 1:2]),
                             waits=[t3], inc="dve")
                t5 = em.emit("scalar", lambda E: E.activation(out=stat.ap[:, 3:4], in_=stat.ap[:, 2:3],
                                                             func=AF.Sigmoid), waits=[t4], inc="act")
                t6 = em.emit("vector", lambda E: E.tensor_scalar(
                    out=stat.ap[:, 4:5], in0=stat.ap[:, 3:4], scalar1=-1.0, scalar2=1.0,
                    op0=ALU.mult, op1=ALU.add), waits=[t5], inc="dve")
                t7 = em.emit("vector", lambda E: E.tensor_scalar(
                    out=rtw.ap[:, 8:16], in0=rtw.ap[:, 8:16], scalar1=stat.ap[:, 1:2], scalar2=1.0,
                    op0=ALU.is_equal, op1=ALU.mult), waits=[t6], inc="dve")
                last = None
                for e in range(NE):
                    ta = em.emit("vector", lambda E, e=e: E.tensor_scalar(
                        out=rtw.ap[:, 8 + e:9 + e], in0=rtw.ap[:, 8 + e:9 + e],
                        scalar1=stat.ap[:, 4:5], scalar2=1.0, op0=ALU.mult, op1=ALU.mult),
                        waits=[t7], inc="dve")
                    last = em.emit("vector", lambda E, e=e, tc=tc: E.scalar_tensor_tensor(
                        out=wtm.ap[:, tc * 8 + e: tc * 8 + e + 1], in0=rtw.ap[:, e:e + 1],
                        scalar=stat.ap[:, 3:4], in1=rtw.ap[:, 8 + e:9 + e],
                        op0=ALU.mult, op1=ALU.add), waits=[ta], inc="dve")
                rtw.produced(last)
                stat.used(last)
                emit_router.mtoks.append(last)
            wtm.produced(last)

        def emit_wbc(e):
            last = None
            for tc in range(NCH):
                pt = pTa if tc % 2 == 0 else pTb
                tt = em.emit("tensor", lambda E, e=e, tc=tc, pt=pt: E.transpose(
                    pt.ap[0:1, 0:128], wtm.ap[:, tc * 8 + e: tc * 8 + e + 1], identS.ap[:]),
                    waits=[emit_router.mtoks[tc], identS.ready] + ARENA + pt.barrier(), inc="pe")
                pt.produced(tt)
                tc_ = em.emit("scalar", lambda E, pt=pt: E.copy(out=wrow.ap[0:1, :], in_=pt.ap[0:1, 0:128]),
                              waits=[tt] + wrow.barrier(), inc="act")
                pt.used(tc_)
                wrow.produced(tc_)
                b = pb[0] if tc % 2 == 0 else pb[1]
                tm = em.emit("tensor", lambda E, b=b: E.matmul(
                    b.ap[:, 0:128], ones1.ap[:], wrow.ap[0:1, :], start=True, stop=True),
                    waits=[tc_, ones1.ready] + b.barrier(), inc="pe")
                b.produced(tm)
                wrow.used(tm)
                last = em.emit("scalar", lambda E, tc=tc, b=b: E.copy(
                    out=wbc.ap[:, tc * 128:(tc + 1) * 128], in_=b.ap[:, 0:128]),
                    waits=([tm] + wbc.barrier() + ARENA) if tc == 0 else [tm], inc="act")
                b.used(last)
            wbc.produced(last)

        def emit_moe():
            call = emit_moe.call
            emit_moe.call += 1
            cached = {0, 2} <= need
            emit_router()
            emit_wbc(0)
            for e in range(NE):
                for hh in range(2):
                    if cached and call > 0:
                        g_aps = [dict(hbm_ap=None, cols=D, cache_from=W["wg_c"][e, hh * 8 + mi])
                                 for mi in range(8)]
                        u_aps = [dict(hbm_ap=None, cols=D, cache_from=W["wu_c"][e, hh * 8 + mi])
                                 for mi in range(8)]
                        d_aps = [dict(hbm_ap=None, cols=2048,
                                      cache_from=W["wd_c"][e, hh, dm2 * 2:(dm2 + 1) * 2]
                                      .transpose([1, 0, 2]))
                                 for dm2 in range(8)]
                    else:
                        g_aps = [dict(hbm_ap=W["wg"][e, hh * 8 + mi], cols=D,
                                      cache_to=W["wg_c"][e, hh * 8 + mi] if cached else None)
                                 for mi in range(8)]
                        u_aps = [dict(hbm_ap=W["wu"][e, hh * 8 + mi], cols=D,
                                      cache_to=W["wu_c"][e, hh * 8 + mi] if cached else None)
                                 for mi in range(8)]
                        d_aps = [dict(hbm_ap=W["wd"][e, hh, dm2 * 2:(dm2 + 1) * 2]
                                      .transpose([1, 0, 2]), cols=2048, scale=0.5,
                                      cache_to=W["wd_c"][e, hh, dm2 * 2:(dm2 + 1) * 2]
                                      .transpose([1, 0, 2]) if cached else None)
                                 for dm2 in range(8)]
                    hook = (lambda e=e: emit_wbc(e + 1)) if (hh == 1 and e + 1 < NE) else None
                    emit_glu(g_aps, u_aps, d_aps, 8, 0, True, wbc, 0.5, pre_d_hook=hook)
        emit_moe.call = 0

        def emit_ffn1():
            call = emit_ffn1.call
            emit_ffn1.call += 1
            cached = {3, 5} <= need
            for grp in range(8):
                if cached and call > 0:
                    g_aps = [dict(hbm_ap=None, cols=D, cache_from=W["l1_wg_c"][grp * 8 + mi])
                             for mi in range(8)]
                    u_aps = [dict(hbm_ap=None, cols=D, cache_from=W["l1_wu_c"][grp * 8 + mi])
                             for mi in range(8)]
                    d_aps = [dict(hbm_ap=None, cols=2048,
                                  cache_from=W["l1_wd_c"][grp, dm2 * 2:(dm2 + 1) * 2]
                                  .transpose([1, 0, 2]))
                             for dm2 in range(8)]
                else:
                    g_aps = [dict(hbm_ap=W["l1_wg"][grp * 8 + mi], cols=D,
                                  cache_to=W["l1_wg_c"][grp * 8 + mi] if cached else None)
                             for mi in range(8)]
                    u_aps = [dict(hbm_ap=W["l1_wu"][grp * 8 + mi], cols=D,
                                  cache_to=W["l1_wu_c"][grp * 8 + mi] if cached else None)
                             for mi in range(8)]
                    d_aps = [dict(hbm_ap=W["l1_wd"][grp, dm2 * 2:(dm2 + 1) * 2]
                                  .transpose([1, 0, 2]), cols=2048, scale=0.5,
                                  cache_to=W["l1_wd_c"][grp, dm2 * 2:(dm2 + 1) * 2]
                                  .transpose([1, 0, 2]) if cached else None)
                             for dm2 in range(8)]
                emit_glu(g_aps, u_aps, d_aps, 8, 0, False, None, 0.5)
        emit_ffn1.call = 0

        bigB = [Buf(bigT[:, i * T:(i + 1) * T]) for i in range(8)]

        def proj_head(panel, dest, bias_col, pbanks):
            toks = []
            for tgh in range(2):
                for kc in range(NCH):
                    for j in range(2):
                        tg = tgh * 2 + j
                        b = pbanks[tgh * 2 + j] if len(pbanks) == 4 else pbanks[j]
                        t = em.emit("tensor", lambda E, panel=panel, kc=kc, tg=tg, b=b: E.matmul(
                            b.ap[:], panel.ap[:, kc * 128:(kc + 1) * 128], rsg(xnorm, kc, tg),
                            start=(kc == 0), stop=(kc == NCH - 1)),
                            waits=([panel.ready, xnorm.ready] + (b.barrier() if kc == 0 else [])),
                            inc="pe" if kc == NCH - 1 else None)
                        if kc == NCH - 1:
                            b.produced(t)
                            panel.used(t)
                            xnorm.used(t)
                for j in range(2):
                    tg = tgh * 2 + j
                    b = pbanks[tgh * 2 + j] if len(pbanks) == 4 else pbanks[j]
                    if bias_col is None:
                        te = em.emit("scalar", lambda E, b=b, tg=tg: E.copy(
                            out=dest.ap[:, tg * 512:(tg + 1) * 512], in_=b.ap[:]),
                            waits=[b.ready] + ARENA + (dest.barrier() if tg == 0 else []), inc="act")
                    else:
                        te = em.emit("scalar", lambda E, b=b, tg=tg: E.activation(
                            out=dest.ap[:, tg * 512:(tg + 1) * 512], in_=b.ap[:], func=AF.Identity,
                            bias=bias_col, scale=1.0),
                            waits=[b.ready, bqkS.ready] + ARENA + (dest.barrier() if tg == 0 else []), inc="act")
                    b.used(te)
                    anote(te)
                    toks.append(te)
            dest.produced(toks[-1])
            for t in toks[:-1]:
                dest.used(t)

        def emit_cbound(hh, srcs, probs_scr, cbanks):
            sq = bigB[3]
            mtoks = []
            for oi, srcb in enumerate(srcs):
                tq = em.emit("scalar", lambda E, s=srcb: E.activation(
                    out=sq.ap[:, 0:T], in_=s.ap[:, 0:T], func=AF.Square),
                    waits=[srcb.ready] + sq.barrier() + ARENA, inc="act")
                sq.produced(tq)
                srcb.used(tq)
                for kg in range(4):
                    b = cbanks[kg % 2]
                    t = em.emit("tensor", lambda E, kg=kg, b=b: E.matmul(
                        b.ap[:], ones128.ap[:], sq.ap[:, kg * 512:(kg + 1) * 512],
                        start=True, stop=True),
                        waits=[tq, ones128.ready] + b.barrier(), inc="pe")
                    b.produced(t)
                    sq.used(t)
                    ta = em.emit("vector", lambda E, b=b, c=oi * 4 + kg, p=probs_scr: E.tensor_scalar(
                        out=p.ap[:, 0:512], in0=b.ap[:], scalar1=1.0, scalar2=0.0,
                        op0=ALU.mult, op1=ALU.max, accum_out=nrm.ap[:, c:c + 1]),
                        waits=[t] + probs_scr.barrier(), inc="dve")
                    b.used(ta)
                    probs_scr.used(ta)
                    mtoks.append(ta)
            t = em.emit("vector", lambda E: E.tensor_max(nrm.ap[:, 0:4], nrm.ap[:, 0:4], nrm.ap[:, 4:8]),
                        waits=mtoks, inc="dve")
            t = em.emit("vector", lambda E: E.tensor_max(nrm.ap[:, 0:2], nrm.ap[:, 0:2], nrm.ap[:, 2:4]),
                        waits=[t], inc="dve")
            t = em.emit("vector", lambda E: E.tensor_mul(nrm.ap[:, 8:9], nrm.ap[:, 0:1], nrm.ap[:, 1:2]),
                        waits=[t], inc="dve")
            t = em.emit("scalar", lambda E: E.activation(out=nrm.ap[:, 9:10], in_=nrm.ap[:, 8:9],
                                                         func=AF.Sqrt), waits=[t], inc="act")
            t = em.emit("vector", lambda E, hh=hh: E.tensor_scalar_mul(
                bcol.ap[:, hh:hh + 1], nrm.ap[:, 9:10], -SM_SCALE), waits=[t], inc="dve")
            return t

        def emit_attn0():
            q_h, k_h, v_h = bigB[0], bigB[1], bigB[2]
            prbs0 = [bigB[5], Buf(auxT[:, 0:T])]
            rrep = Buf(scrT[:, 0:128])
            btoks = [None, None]

            def proj_qk0(h):
                qp = load_panel(W["l0_wqkv"][h], D)
                proj_head(qp, q_h, None, [pb[0], pb[1]])
                kp = load_panel(W["l0_wqkv"][16 + h], D)
                proj_head(kp, k_h, None, [pb[0], pb[1]])

            def proj_v0(h):
                vp = load_panel(W["l0_wqkv"][32 + h], D)
                proj_head(vp, v_h, None, [pb[0], pb[1]])

            proj_qk0(0)
            btoks[0] = emit_cbound(0, (q_h, k_h), prbs0[0], [pb[2], pb[3]])
            proj_v0(0)
            for h in range(NH):

                def do_sexp(qc):
                    probs = prbs0[qc % 2]
                    sp = 6 * (qc % 2)
                    sbanks = [pb[0], pb[1]] if qc % 2 == 0 else [pTaF, pTbF]
                    extoks = []
                    for kg in range(4):
                        b = sbanks[kg % 2]
                        t = em.emit("tensor", lambda E, qc=qc, kg=kg, b=b: E.matmul(
                            b.ap[:], q_h.ap[:, qc * 128:(qc + 1) * 128],
                            k_h.ap[:, kg * 512:(kg + 1) * 512], start=True, stop=True),
                            waits=[q_h.ready, k_h.ready] + b.barrier(), inc="pe")
                        b.produced(t)
                        q_h.used(t)
                        k_h.used(t)
                        anote(t)
                        tex = em.emit("scalar", lambda E, kg=kg, b=b, probs=probs, sp=sp, hh=h % 2:
                                      E.activation(
                            out=probs.ap[:, kg * 512:(kg + 1) * 512], in_=b.ap[:], func=AF.Exp,
                            bias=bcol.ap[:, hh:hh + 1], scale=SM_SCALE,
                            accum_out=stat.ap[:, sp + kg:sp + kg + 1]),
                            waits=[t, btoks[h % 2]] + ARENA + (probs.barrier() if kg == 0 else []),
                            inc="act")
                        b.used(tex)
                        anote(tex)
                        extoks.append(tex)
                    probs.produced(extoks[-1])
                    for t in extoks[:-1]:
                        probs.used(t)

                do_sexp(0)
                for qc in range(NCH):
                    probs = prbs0[qc % 2]
                    sp = 6 * (qc % 2)
                    tr1 = em.emit("vector", lambda E, sp=sp: E.tensor_scalar(
                        out=nrm.ap[:, 8:12], in0=stat.ap[:, sp:sp + 4], scalar1=1.0, scalar2=0.0,
                        op0=ALU.mult, op1=ALU.add, accum_out=stat.ap[:, sp + 4:sp + 5]),
                        waits=probs.barrier() + [probs.ready], inc="dve")
                    tr2 = em.emit("vector", lambda E, sp=sp: E.reciprocal(
                        stat.ap[:, sp + 5:sp + 6], stat.ap[:, sp + 4:sp + 5]),
                        waits=[tr1], inc="dve")
                    t6 = em.emit("vector", lambda E, r=rrep, sp=sp: E.tensor_scalar_mul(
                        r.ap, ones128.ap[:], stat.ap[:, sp + 5:sp + 6]),
                        waits=[tr2] + rrep.barrier(), inc="dve")
                    rrep.produced(t6)
                    if qc + 1 < NCH:
                        do_sexp(qc + 1)
                    for kg in range(4):
                        b = pb[2 + kg]
                        t = em.emit("tensor", lambda E, kg=kg, qc=qc, b=b, r=rrep, probs=probs: E.matmul(
                            b.ap[:], r.ap, probs.ap[:, kg * 512:(kg + 1) * 512],
                            start=(qc == 0), stop=(qc == NCH - 1)),
                            waits=([t6, probs.ready] + (b.barrier() if qc == 0 else [])),
                            inc="pe")
                        if qc == NCH - 1:
                            b.produced(t)
                        rrep.used(t)
                        probs.used(t)
                        anote(t)
                if h + 1 < NH:
                    proj_qk0(h + 1)
                vtoks = []
                for kg in range(4):
                    b = pb[2 + kg]
                    tv = em.emit("vector", lambda E, kg=kg, b=b: E.tensor_mul(
                        v_h.ap[:, kg * 512:(kg + 1) * 512], v_h.ap[:, kg * 512:(kg + 1) * 512], b.ap[:]),
                        waits=[b.ready, v_h.ready], inc="dve")
                    b.used(tv)
                    v_h.used(tv)
                    anote(tv)
                    vtoks.append(tv)
                td = em.dma(W["vals"][h], v_h.ap[:, 0:T], "dV", waits=vtoks, queue="gpsimd")
                v_h.used(td)
                anote(td)
                em.dv_toks.append(td)
                if h + 1 < NH:
                    proj_v0(h + 1)
                    btoks[(h + 1) % 2] = emit_cbound((h + 1) % 2, (q_h, k_h), prbs0[0],
                                                     [pb[2], pb[3]])
            out_proj(W["l0_wo"], None)

        def out_proj(panels_ap, bias_cols):
            xbar = xnorm.barrier()
            pre = [load_panel(panels_ap[0], D), load_panel(panels_ap[1], D)]
            vload = []
            for c in range(NCH):
                t = em.dma(rch(xnorm, c), W["vals"][c], "dVL",
                           waits=xbar + [em.dv_toks[c]], queue="gpsimd")
                vload.append(t)
            xnorm.produced(vload[-1])
            for t in vload[:-1]:
                xnorm.used(t)
            for dm in range(16):
                op = pre[dm] if dm < 2 else load_panel(panels_ap[dm], D)
                for tgh in range(2):
                    bd = [pb[2], pb[3]] if tgh == 0 else [pb[4], pb[5]]
                    for kc in range(NCH):
                        for j in range(2):
                            tg = tgh * 2 + j
                            t = em.emit("tensor", lambda E, op=op, kc=kc, tg=tg, b=bd[j]: E.matmul(
                                b.ap[:], op.ap[:, kc * 128:(kc + 1) * 128], rsg(xnorm, kc, tg),
                                start=(kc == 0), stop=(kc == NCH - 1)),
                                waits=([op.ready, vload[kc]] + (bd[j].barrier() if kc == 0 else [])),
                                inc="pe" if kc == NCH - 1 else None)
                            if kc == NCH - 1:
                                bd[j].produced(t)
                                op.used(t)
                                xnorm.used(t)
                    for j in range(2):
                        tg = tgh * 2 + j
                        if bias_cols is None:
                            te = em.emit("vector", lambda E, dm=dm, tg=tg, b=bd[j]: E.tensor_add(
                                rsg(resid, dm, tg), rsg(resid, dm, tg), b.ap[:]),
                                waits=[bd[j].ready], inc="dve")
                        else:
                            te = em.emit("vector", lambda E, dm=dm, tg=tg, b=bd[j]: E.scalar_tensor_tensor(
                                out=rsg(resid, dm, tg), in0=b.ap[:], scalar=bias_cols[:, dm:dm + 1],
                                in1=rsg(resid, dm, tg), op0=ALU.add, op1=ALU.add),
                                waits=[bd[j].ready, boutS.ready], inc="dve")
                        bd[j].used(te)
                        emit_glu.rtoks.append(te)
                        _rchunk[dm].append(te)

        def emit_attn1():
            q_h, k_h, vtok = bigB[0], bigB[1], bigB[2]
            prbs = [bigB[5], bigB[7]]
            vals = bigB[6]
            btoks = [None, None]

            def proj_qk1(h):
                qp = load_panel(W["l1_wqkv"][h], D)
                proj_head(qp, q_h, bqkS.ap[:, h:h + 1], [pb[0], pb[1], pb[2], pb[5]])
                kp = load_panel(W["l1_wqkv"][16 + h], D)
                proj_head(kp, k_h, bqkS.ap[:, 16 + h:17 + h], [pb[0], pb[1], pb[2], pb[5]])

            def proj_v1(h):
                vp = load_panel(W["l1_wqkv"][32 + h], D)
                vtoks = []
                for tc in range(NCH):
                    b = pb[0] if tc % 2 == 0 else pb[1]
                    for kc in range(NCH):
                        t = em.emit("tensor", lambda E, vp=vp, tc=tc, kc=kc, b=b: E.matmul(
                            b.ap[:, 0:128], xnorm.ap[:, kc * T + tc * 128: kc * T + tc * 128 + 128],
                            vp.ap[:, kc * 128:(kc + 1) * 128],
                            start=(kc == 0), stop=(kc == NCH - 1)),
                            waits=([vp.ready, xnorm.ready] + (b.barrier() if kc == 0 else [])),
                            inc="pe" if kc == NCH - 1 else None)
                    b.produced(t)
                    vp.used(t)
                    xnorm.used(t)
                    te = em.emit("scalar", lambda E, tc=tc, b=b: E.copy(
                        out=vtok.ap[:, tc * 128:(tc + 1) * 128], in_=b.ap[:, 0:128]),
                        waits=[b.ready] + ARENA + (vtok.barrier() if tc == 0 else []), inc="act")
                    b.used(te)
                    anote(te)
                    vtoks.append(te)
                vtok.produced(vtoks[-1])
                for t in vtoks[:-1]:
                    vtok.used(t)

            proj_qk1(0)
            btoks[0] = emit_cbound(0, (q_h, k_h), prbs[0], [pb[3], pb[4]])
            proj_v1(0)
            for h in range(NH):

                def do_sexp1(qc):
                    probs = prbs[qc % 2]
                    sp = 6 * (qc % 2)
                    L = (qc + 1) * 128
                    nkg = (L + 511) // 512
                    sbanks = [pb[0], pb[1]] if qc % 2 == 0 else [pb[2], pb[5]]
                    extoks = []
                    for kg in range(nkg):
                        w = min(512, L - kg * 512)
                        b = sbanks[kg % 2]
                        t = em.emit("tensor", lambda E, qc=qc, kg=kg, w=w, b=b: E.matmul(
                            b.ap[:, 0:w], q_h.ap[:, qc * 128:(qc + 1) * 128],
                            k_h.ap[:, kg * 512: kg * 512 + w], start=True, stop=True),
                            waits=[q_h.ready, k_h.ready] + b.barrier(), inc="pe")
                        b.produced(t)
                        q_h.used(t)
                        k_h.used(t)
                        anote(t)
                        if kg < nkg - 1:
                            tex = em.emit("scalar", lambda E, kg=kg, b=b, probs=probs, sp=sp,
                                          hh=h % 2: E.activation(
                                out=probs.ap[:, kg * 512:(kg + 1) * 512], in_=b.ap[:], func=AF.Exp,
                                bias=bcol.ap[:, hh:hh + 1], scale=SM_SCALE,
                                accum_out=stat.ap[:, sp + kg:sp + kg + 1]),
                                waits=[t, btoks[h % 2]] + ARENA + (probs.barrier() if kg == 0 else []),
                                inc="act")
                        else:
                            tex = em.emit("scalar", lambda E, kg=kg, w=w, b=b, probs=probs,
                                          hh=h % 2: E.activation(
                                out=probs.ap[:, kg * 512: kg * 512 + w], in_=b.ap[:, 0:w], func=AF.Exp,
                                bias=bcol.ap[:, hh:hh + 1], scale=SM_SCALE),
                                waits=[t, btoks[h % 2]] + ARENA + (probs.barrier() if kg == 0 else []),
                                inc="act")
                        b.used(tex)
                        anote(tex)
                        extoks.append(tex)
                    tm = em.emit("vector", lambda E, qc=qc, probs=probs: E.tensor_mul(
                        probs.ap[:, qc * 128:(qc + 1) * 128], probs.ap[:, qc * 128:(qc + 1) * 128],
                        triS.ap[:]), waits=[extoks[-1], triS.ready], inc="dve")
                    tps = em.emit("vector", lambda E, qc=qc, probs=probs, sp=sp, L=L, nkg=nkg:
                                  E.tensor_scalar(
                        out=bigB[3].ap[:, 0:(L - (nkg - 1) * 512)],
                        in0=probs.ap[:, (nkg - 1) * 512: L], scalar1=1.0, scalar2=0.0,
                        op0=ALU.mult, op1=ALU.add, accum_out=stat.ap[:, sp + 3:sp + 4]),
                        waits=[tm] + bigB[3].barrier(), inc="dve")
                    bigB[3].used(tps)
                    tr = tps
                    for kg in range(nkg - 1):
                        tr = em.emit("vector", lambda E, sp=sp, kg=kg: E.tensor_add(
                            stat.ap[:, sp + 3:sp + 4], stat.ap[:, sp + 3:sp + 4],
                            stat.ap[:, sp + kg:sp + kg + 1]),
                            waits=[tr, extoks[kg]], inc="dve")
                    tr2 = em.emit("vector", lambda E, sp=sp: E.reciprocal(
                        stat.ap[:, sp + 5:sp + 6], stat.ap[:, sp + 3:sp + 4]),
                        waits=[tr], inc="dve")
                    trep = em.emit("vector", lambda E, sp=sp, par=qc % 2: E.tensor_scalar_mul(
                        scrT[:, 1280 + par * 128: 1408 + par * 128], ones128.ap[:],
                        stat.ap[:, sp + 5:sp + 6]),
                        waits=[tr2, ones128.ready] + ARENA, inc="dve")
                    probs.produced(tm)
                    return tm, trep

                tns = [None] * NCH
                tns[0] = do_sexp1(0)
                for qc in range(NCH):
                    probs = prbs[qc % 2]
                    tn, trep = tns[qc]
                    if qc + 1 < NCH:
                        tns[qc + 1] = do_sexp1(qc + 1)
                    bav = pb[3] if qc % 2 == 0 else pb[4]
                    cptoks = [None] * (qc + 1)

                    def emit_T(kc, probs=probs, tn=tn):
                        pt = pTa if kc % 2 == 0 else pTb
                        tt = em.emit("tensor", lambda E, kc=kc, pt=pt: E.transpose(
                            pt.ap[:, 0:128], probs.ap[:, kc * 128:(kc + 1) * 128], identS.ap[:]),
                            waits=[tn, identS.ready] + ARENA + pt.barrier(), inc="pe")
                        pt.produced(tt)
                        probs.used(tt)
                        ps = pTs[kc % 4]
                        tc_ = em.emit("scalar", lambda E, pt=pt, ps=ps: E.copy(
                            out=ps.ap[:], in_=pt.ap[:, 0:128]),
                            waits=[tt] + ARENA + ps.barrier(), inc="act")
                        pt.used(tc_)
                        ps.produced(tc_)
                        cptoks[kc] = tc_

                    emit_T(0)
                    if qc >= 1:
                        emit_T(1)
                    for kc in range(qc + 1):
                        if kc + 2 <= qc:
                            emit_T(kc + 2)
                        ps = pTs[kc % 4]
                        t = em.emit("tensor", lambda E, kc=kc, qc=qc, b=bav, ps=ps: E.matmul(
                            b.ap[:, 0:128], vtok.ap[:, kc * 128:(kc + 1) * 128], ps.ap[:],
                            start=(kc == 0), stop=(kc == qc)),
                            waits=([cptoks[kc], vtok.ready] + (bav.barrier() if kc == 0 else [])),
                            inc="pe")
                        ps.used(t)
                        vtok.used(t)
                        anote(t)
                    bav.produced(t)
                    ptr = pTb if qc % 2 == 0 else pTa
                    ttr = em.emit("tensor", lambda E, ptr=ptr, par=qc % 2: E.transpose(
                        ptr.ap[:, 0:128], scrT[:, 1280 + par * 128: 1408 + par * 128],
                        identS.ap[:]),
                        waits=[trep, identS.ready] + ptr.barrier(), inc="pe")
                    ptr.produced(ttr)
                    tcr = em.emit("scalar", lambda E, ptr=ptr, par=qc % 2: E.copy(
                        out=scrT[:, 1536 + par * 128: 1664 + par * 128], in_=ptr.ap[:, 0:128]),
                        waits=[ttr] + ARENA, inc="act")
                    ptr.used(tcr)
                    te1 = em.emit("vector", lambda E, qc=qc, b=bav, par=qc % 2: E.tensor_mul(
                        vals.ap[:, qc * 128:(qc + 1) * 128], b.ap[:, 0:128],
                        scrT[:, 1536 + par * 128: 1664 + par * 128]),
                        waits=[bav.ready, tcr] + (vals.barrier() if qc == 0 else []), inc="dve")
                    te = em.emit("vector", lambda E, qc=qc, h=h: E.tensor_scalar_add(
                        vals.ap[:, qc * 128:(qc + 1) * 128], vals.ap[:, qc * 128:(qc + 1) * 128],
                        bqkS.ap[:, 32 + h:33 + h]),
                        waits=[te1, bqkS.ready], inc="dve")
                    bav.used(te1)
                    vals.used(te)
                    anote(te)
                if h + 1 < NH:
                    proj_qk1(h + 1)
                td = em.dma(W["vals"][h], vals.ap[:, 0:T], "dV", waits=vals.barrier(), queue="gpsimd")
                vals.produced(None)
                vals.used(td)
                anote(td)
                em.dv_toks.append(td)
                if h + 1 < NH:
                    proj_v1(h + 1)
                    btoks[(h + 1) % 2] = emit_cbound((h + 1) % 2, (q_h, k_h), prbs[0],
                                                     [pb[3], pb[4]])
            out_proj(W["l1_wout"], boutS.ap)

        tap_done = False
        for si, sub in enumerate(sublayers):
            arena_flush()
            emit_ln()
            if tap == "big" and sub == 1:
                em.dv_toks = []
                emit_attn0()
                col = {}
                for s_, v_ in _anotes:
                    col[s_] = max(col.get(s_, 0), v_)
                bw = list(col.items())
                for c in range(8):
                    em.dma(y_out[c], bigT[:, c * T:(c + 1) * T], "dO", waits=bw)
                em.emit("sync", lambda E: E.nop(), waits=[("dO", 16 * 8)])
                tap_done = True
                break
            if tap == "rbc":
                for kg in range(4):
                    em.emit("scalar", lambda E, kg=kg: E.copy(
                        out=rch(xnorm, 0)[:, kg * 512:(kg + 1) * 512], in_=pb[2 + kg].ap[:]),
                        waits=[pb[2 + kg].ready], inc="act")
                tmp = em.emit("scalar", lambda E: E.nop(), waits=[], inc="act")
                em.dma(y_out[0], rbc.ap[:], "dO", waits=[rbc.ready])
                em.dma(y_out[1], rch(xnorm, 0), "dO", waits=[("act", em.cnt["act"])])
                em.emit("sync", lambda E: E.nop(), waits=[("dO", 32)])
                tap_done = True
                break
            if tap == "xnorm":
                for c in range(NCH):
                    em.dma(y_out[c], rch(xnorm, c), "dO", waits=[xnorm.ready])
                em.emit("sync", lambda E: E.nop(), waits=[("dO", 16 * NCH)])
                tap_done = True
                break
            if tap in ("logits", "wtm") and sub in (0, 2):
                emit_router()
                if tap == "logits":
                    tw = [rtr.users[i] for i in range(NCH)]
                    em.dma(y_out[0][:, 0:128], rtr.ap[:], "dO", waits=tw)
                else:
                    em.dma(y_out[0][:, 0:128].rearrange("p t -> p t"), wtm.ap[:], "dO",
                           waits=[wtm.ready])
                em.emit("sync", lambda E: E.nop(), waits=[("dO", 16)])
                tap_done = True
                break
            if tap == "wbc0" and sub in (0, 2):
                emit_router()
                emit_wbc(0)
                em.dma(y_out[0], wbc.ap[:], "dO", waits=[wbc.ready])
                em.emit("sync", lambda E: E.nop(), waits=[("dO", 16)])
                tap_done = True
                break
            if sub in (0, 2):
                emit_moe()
            elif sub == 1:
                em.dv_toks = []
                emit_attn0()
            elif sub == 3 or sub == 5:
                emit_ffn1()
            elif sub == 4:
                em.dv_toks = []
                emit_attn1()
            RESID_TOKS[:] = [t for t in emit_glu.rtoks]
            emit_glu.rtoks = []
            col = {}
            for s_, v_ in RESID_TOKS:
                col[s_] = max(col.get(s_, 0), v_)
            RESID_TOKS[:] = list(col.items())
            for c in range(NCH):
                cc = {}
                for s_, v_ in _rchunk[c]:
                    cc[s_] = max(cc.get(s_, 0), v_)
                RESID_CTOKS[c] = list(cc.items()) if cc else list(RESID_TOKS)
                _rchunk[c] = []
            xnorm.produced(None)
            xnorm.users = []

        if tap_done:
            block = ctx.enter_context(nc.Block())
            for eng in ENGINES:
                ops = em.ops[eng]
                if not ops:
                    continue

                def body(E, ops=ops):
                    for op in ops:
                        op(E)
                getattr(block, eng)(body)
            return nc
        st_toks = []
        for c in range(NCH):
            if last:
                lb = wland[c % 2]
                tc_ = em.emit("gpsimd", lambda E, c=c, lb=lb: E.tensor_copy(
                    out=lb.ap[:, 0:T], in_=rch(resid, c)),
                    waits=RESID_CTOKS[c] + lb.barrier(), inc="pool")
                lb.produced(tc_)
                td = em.dma(y_out[c], lb.ap[:, 0:T], "dO", waits=[tc_])
                lb.used(td)
            else:
                td = em.dma(y_out[c], rch(resid, c), "dO", waits=RESID_CTOKS[c])
            st_toks.append(td)
        em.emit("sync", lambda E: E.nop(), waits=[("dO", 16 * NCH)])

        block = ctx.enter_context(nc.Block())
        for eng in ENGINES:
            ops = em.ops[eng]
            if not ops:
                continue

            def body(E, ops=ops):
                for op in ops:
                    op(E)
            getattr(block, eng)(body)

    return nc



GROUPS = [[0, 1, 2, 3, 4, 5]]

_cache = {}


def _get_prog(key, subs, first, last):
    if key not in _cache:
        _cache[key] = build_program(subs, first, last)
    return _cache[key]


def kernel(**inputs):
    h = prep_inputs(inputs)
    shared = {
        "wr": h["wr"], "wg": h["wg"], "wu": h["wu"], "wd": h["wd"],
        "l0_wqkv": h["l0_wqkv"], "l0_wo": h["l0_wo"],
        "l1_wqkv": h["l1_wqkv"], "l1_wout": h["l1_wout"],
        "bqk": h["bqk"], "bout": h["bout"], "tri": h["tri"],
        "l1_wg": h["l1_wg"], "l1_wu": h["l1_wu"], "l1_wd": h["l1_wd"],
        "ident": h["ident"],
    }

    def in_map_for(subs, first, core, cur):
        need = set(subs)
        m = {}
        m["x"] = h["x"][core] if first else cur[core]
        if need & {0, 2}:
            for k in ("wr", "wg", "wu", "wd"):
                m[k] = shared[k]
        if 1 in need:
            m["l0_wqkv"] = shared["l0_wqkv"]
            m["l0_wo"] = shared["l0_wo"]
        if 4 in need:
            for k in ("l1_wqkv", "l1_wout", "bqk", "bout", "tri"):
                m[k] = shared[k]
        if need & {3, 5}:
            for k in ("l1_wg", "l1_wu", "l1_wd"):
                m[k] = shared[k]
        m["ident"] = shared["ident"]
        return m

    cur = None
    for gi, subs in enumerate(GROUPS):
        first = gi == 0
        last = gi == len(GROUPS) - 1
        nc = _get_prog((tuple(subs), first, last), subs, first, last)
        in_maps = [in_map_for(subs, first, c, cur) for c in range(N_CORES)]
        res = run_bass_kernel_spmd(nc, in_maps, core_ids=list(range(N_CORES)))
        cur = [res.results[c]["y"] for c in range(N_CORES)]

    out = np.empty((N_CORES, T, D), np.float32)
    for c in range(N_CORES):
        out[c] = cur[c].reshape(D, T).T
    return out
